# version 12
# baseline (speedup 1.0000x reference)
import math
import numpy as np
import ml_dtypes
import concourse.bass as bass
import concourse.mybir as mybir
from concourse.bass_utils import run_bass_kernel_spmd

F32 = mybir.dt.float32
BF16 = mybir.dt.bfloat16
I32 = mybir.dt.int32
AF = mybir.ActivationFunctionType
ALU = mybir.AluOpType
AX = mybir.AxisListType

D = 4096
NF = 32
HQ = 32
HKV = 8
DH = 128
I_FF = 11008
NIB = 86
RSQ = 1.0 / math.sqrt(128.0)
EPS = 1e-5
DEBUG_OUTS = False

ENGINES = ("pe", "dve", "act", "pool", "sp")
NDMASEM = 8
DMA_QUEUES = ("qsp", "qact", "qpool")
SEMS = ("pe", "dve", "act", "pool") + tuple(
    f"{q}{r}" for q in DMA_QUEUES for r in range(NDMASEM))
DMA_SEM = {"sp": "qsp", "act": "qact", "pool": "qpool"}


def _flat(toks):
    out = []
    for t in toks:
        if t is None:
            continue
        if isinstance(t, list):
            out.extend(x for x in t if x is not None)
        else:
            out.append(t)
    return out


class _B:

    def __init__(self, nc):
        self.nc = nc
        self.ops = {e: [] for e in ENGINES}
        self.cnt = {s: 0 for s in SEMS}
        self.cache = {e: {s: 0 for s in SEMS} for e in ENGINES}
        self.pend_bar = {e: None for e in ENGINES}
        self.sem = {}
        self.n_inst = 0
        self.dma_rot = {q: 0 for q in DMA_QUEUES}

    def _waits(self, eng, waits):
        out = []
        bar = self.pend_bar[eng]
        if bar is not None:
            self.pend_bar[eng] = None
            for s, v in bar:
                if s == eng:
                    continue
                out.append((s, v))
        out.extend(_flat(list(waits)))
        ded = []
        for s, v in out:
            if v > self.cache[eng][s]:
                self.cache[eng][s] = v
                ded.append((s, v))
        return ded

    def op(self, eng, fn, waits=(), inc=True):
        ded = self._waits(eng, waits)
        tok = None
        incs = []
        if inc:
            self.cnt[eng] += 1
            tok = (eng, self.cnt[eng])
            incs = [(eng, 1)]
        self.ops[eng].append((ded, fn, incs))
        self.n_inst += 1 + len(ded)
        return tok

    def dma(self, eng, fn, waits=()):
        q = DMA_SEM[eng]
        r = self.dma_rot[q] % NDMASEM
        self.dma_rot[q] += 1
        s = f"{q}{r}"
        ded = self._waits(eng, list(waits) + ([(s, self.cnt[s])] if self.cnt[s] else []))
        self.cnt[s] += 16
        tok = (s, self.cnt[s])
        self.ops[eng].append((ded, fn, [(s, 16)]))
        self.n_inst += 1 + len(ded)
        return tok

    def wait(self, eng, waits):
        ded = self._waits(eng, waits)
        if ded:
            self.ops[eng].append((ded, None, []))
            self.n_inst += len(ded)

    def bar(self):
        snap = [(s, self.cnt[s]) for s in SEMS if self.cnt[s] > 0]
        for e in ENGINES:
            self.pend_bar[e] = snap

    def replay(self, block):
        def mk(eng):
            def run(e):
                for ded, fn, incs in self.ops[eng]:
                    for s, v in ded:
                        e.wait_ge(self.sem[s], v)
                    if fn is not None:
                        inst = fn(e)
                        for s, a in incs:
                            inst.then_inc(self.sem[s], a)
            return run

        block.tensor(mk("pe"))
        block.vector(mk("dve"))
        block.scalar(mk("act"))
        block.gpsimd(mk("pool"))
        block.sync(mk("sp"))


def _q_perm(nheads):
    p = []
    for h in range(nheads):
        base = h * DH
        p.extend(base + j for j in range(0, DH, 2))
        p.extend(base + j for j in range(1, DH, 2))
    return np.array(p, dtype=np.int64)


def _host_consts(S):
    inv = 1.0 / (10000.0 ** (np.arange(0, DH, 2, dtype=np.float64) / DH))
    t = np.arange(S, dtype=np.float64)
    ang = np.outer(inv, t)
    cos = np.cos(ang).astype(np.float32)
    sin = np.sin(ang).astype(np.float32)
    cp = np.concatenate([cos, cos], axis=0)
    sp = np.concatenate([-sin, sin], axis=0)
    swap = np.zeros((128, 128), dtype=np.float32)
    for i in range(128):
        swap[i, (i + 64) % 128] = 1.0
    masks = np.zeros((4, 128, 512), dtype=np.float32)
    ql = np.arange(512)
    kl = np.arange(128)
    for off in range(4):
        masks[off] = (ql[None, :] >= off * 128 + kl[:, None]).astype(np.float32)
    bf = ml_dtypes.bfloat16
    return {
        "cp_c": cp.astype(bf), "sp_c": sp.astype(bf),
        "swap_c": swap.astype(bf),
        "mask_c": masks.astype(bf),
        "onec_c": np.ones((128, 1), dtype=np.float32).astype(bf),
        "oner_c": np.ones((1, 128), dtype=np.float32).astype(bf),
        "ident_c": np.eye(128, dtype=np.float32).astype(bf),
        "ident32_c": np.eye(128, dtype=np.float32),
    }


def build_program(S):
    assert S in (512, 1024, 2048)
    NTC = S // 512
    NQB = NTC
    NT128 = S // 128
    NQ = HQ * NQB
    import os
    NHALF = 2 if S >= 1024 else 1
    if os.environ.get("KNHALF"):
        NHALF = int(os.environ["KNHALF"])
        assert S // NHALF >= 512
    HALF_BAR = bool(os.environ.get("KHALFBAR"))
    TH = S // NHALF
    NTC_H = TH // 512

    nc = bass.Bass("TRN2", target_bir_lowering=False)

    xin = nc.dram_tensor("xin", [S, D], F32, kind="ExternalInput").ap()
    wq = nc.dram_tensor("wq", [D, D], F32, kind="ExternalInput").ap()
    wk = nc.dram_tensor("wk", [D, HKV * DH], F32, kind="ExternalInput").ap()
    wv = nc.dram_tensor("wv", [D, HKV * DH], F32, kind="ExternalInput").ap()
    wo = nc.dram_tensor("wo", [D, D], F32, kind="ExternalInput").ap()
    gq = nc.dram_tensor("gq", [I_FF, D], I32, kind="ExternalInput").ap()
    gs = nc.dram_tensor("gs", [I_FF, NF], F32, kind="ExternalInput").ap()
    gz = nc.dram_tensor("gz", [I_FF, NF], F32, kind="ExternalInput").ap()
    uq = nc.dram_tensor("uq", [I_FF, D], I32, kind="ExternalInput").ap()
    us = nc.dram_tensor("us", [I_FF, NF], F32, kind="ExternalInput").ap()
    uz = nc.dram_tensor("uz", [I_FF, NF], F32, kind="ExternalInput").ap()
    dq = nc.dram_tensor("dq", [D, I_FF], I32, kind="ExternalInput").ap()
    ds = nc.dram_tensor("ds", [D, NIB], F32, kind="ExternalInput").ap()
    dz = nc.dram_tensor("dz", [D, NIB], F32, kind="ExternalInput").ap()
    cp_c = nc.dram_tensor("cp_c", [128, S], BF16, kind="ExternalInput").ap()
    sp_c = nc.dram_tensor("sp_c", [128, S], BF16, kind="ExternalInput").ap()
    swap_c = nc.dram_tensor("swap_c", [128, 128], BF16, kind="ExternalInput").ap()
    mask_c = nc.dram_tensor("mask_c", [4, 128, 512], BF16, kind="ExternalInput").ap()
    onec_c = nc.dram_tensor("onec_c", [128, 1], BF16, kind="ExternalInput").ap()
    oner_c = nc.dram_tensor("oner_c", [1, 128], BF16, kind="ExternalInput").ap()
    ident_c = nc.dram_tensor("ident_c", [128, 128], BF16, kind="ExternalInput").ap()
    ident32_c = nc.dram_tensor("ident32_c", [128, 128], F32, kind="ExternalInput").ap()
    out = nc.dram_tensor("out", [S, D], F32, kind="ExternalOutput").ap()

    xbf_s = nc.dram_tensor("xbf_s", [S, D], BF16).ap()
    qs_s = nc.dram_tensor("qs_s", [D, S], BF16).ap()
    ks_s = nc.dram_tensor("ks_s", [HKV * DH, S], BF16).ap()
    as_s = nc.dram_tensor("as_s", [S, D], BF16).ap()
    o1_s = nc.dram_tensor("o1_s", [S, D], F32).ap()
    h_s = nc.dram_tensor("h_s", [S, D], F32).ap()
    hbf_s = nc.dram_tensor("hbf_s", [S, D], BF16).ap()
    wgdq_s = nc.dram_tensor("wgdq_s", [I_FF, D], BF16).ap()
    wudq_s = nc.dram_tensor("wudq_s", [I_FF, D], BF16).ap()
    hg_s = nc.dram_tensor("hg_s", [I_FF, S], BF16).ap()
    o2_s = nc.dram_tensor("o2_s", [S, D], F32).ap()

    b = _B(nc)
    ctx_stack = []

    def alloc(name, shape, dtype, psum=False):
        cm = nc.psum_tensor(name, shape, dtype) if psum else nc.sbuf_tensor(name, shape, dtype)
        h = cm.__enter__()
        ctx_stack.append((name, cm))
        return h

    def free(*names):
        names = set(names)
        while names:
            n, cm = ctx_stack[-1]
            assert n in names, f"non-LIFO free: top={n}, want={names}"
            cm.__exit__(None, None, None)
            ctx_stack.pop()
            names.discard(n)

    swap_sb = alloc("swap_sb", [128, 128], BF16)
    onec_sb = alloc("onec_sb", [128, 1], BF16)
    oner_sb = alloc("oner_sb", [1, 128], BF16)
    ident_sb = alloc("ident_sb", [128, 128], BF16)
    negc_sb = alloc("negc_sb", [128, NQ], F32)
    eps_sb = alloc("eps_sb", [128, 1], F32)
    v_sb = alloc("v_sb", [128, NT128 * HKV * 130], BF16)

    t_swap = b.dma("sp", lambda e: e.dma_start(swap_sb[:], swap_c[:]))
    t_onec = b.dma("sp", lambda e: e.dma_start(onec_sb[:], onec_c[:]))
    t_oner = b.dma("sp", lambda e: e.dma_start(oner_sb[:], oner_c[:]))
    t_ident = b.dma("sp", lambda e: e.dma_start(ident_sb[:], ident_c[:]))
    t_vones = b.op("dve", lambda e: e.memset(v_sb[:], 1.0))
    t_eps = b.op("dve", lambda e: e.memset(eps_sb[:], EPS))

    x_fm = alloc("x_fm", [128, NF * S], BF16)
    xrow_sb = [alloc("xrow0", [128, D], BF16)]
    ps_tr1 = [alloc(f"ps_tr1_{i}", [128, 128], BF16, psum=True) for i in range(3)]
    t_h2h = []
    rch = S // 4
    for j in range(4):
        t_h2h.append(b.dma("pool", lambda e, j=j: e.dma_start(
            xbf_s[j * rch:(j + 1) * rch, :], xin[j * rch:(j + 1) * rch, :])))
    t_xfm = {}
    tr01_ctr = [0]
    L0 = {}

    def tk0(key):
        v = L0.get(key)
        return [] if v is None else (v if isinstance(v, list) else [v])

    for qc in range(NT128):
        rb = 0
        t_xr = b.dma("sp", lambda e, qc=qc, rb=rb: e.dma_start(
            xrow_sb[rb][:], xbf_s[qc * 128:(qc + 1) * 128, :]),
            waits=[t_h2h[qc // 4]] + tk0(("xrowfree", rb)))
        last_tr = None
        for g in range(NF):
            pt = tr01_ctr[0] % 3
            tr01_ctr[0] += 1
            t_tr = b.op("pe", lambda e, rb=rb, g=g, pt=pt: e.transpose(
                ps_tr1[pt][:], xrow_sb[rb][:, g * 128:(g + 1) * 128], ident_sb[:]),
                waits=[t_xr, t_ident] + tk0(("tr1", pt)))
            last_tr = t_tr
            col = g * S + qc * 128
            if g % 2 == 0:
                t_ev = b.op("act", lambda e, col=col, pt=pt: e.copy(
                    x_fm[:, col:col + 128], ps_tr1[pt][:]), waits=[t_tr])
            else:
                t_ev = b.op("dve", lambda e, col=col, pt=pt: e.tensor_copy(
                    x_fm[:, col:col + 128], ps_tr1[pt][:]), waits=[t_tr])
            L0[("tr1", pt)] = t_ev
            t_xfm[(qc, g)] = t_ev
        L0[("xrowfree", rb)] = last_tr

    cp_sb = alloc("cp_sb", [128, S], BF16)
    sp_sb = alloc("sp_sb", [128, S], BF16)
    t_cp = b.dma("sp", lambda e: e.dma_start(cp_sb[:], cp_c[:]))
    t_sp = b.dma("sp", lambda e: e.dma_start(sp_sb[:], sp_c[:]))

    cb_row = alloc("cb_row", [1, NQ], F32)
    kn_row = alloc("kn_row", [1, HKV * NTC], F32)
    slab_sb = [alloc(f"slab{i}", [128, NF * 128], BF16) for i in range(2)]
    qraw_sb = [alloc(f"qraw{i}", [128, 512], BF16) for i in range(2)]
    m1_sb = [alloc(f"m1_{i}", [128, 512], BF16) for i in range(2)]
    m2_sb = [alloc(f"m2_{i}", [128, 512], BF16) for i in range(2)]
    qfm_sb = [alloc(f"qfm{i}", [128, 512], BF16) for i in range(2)]
    qsq_sb = [alloc("qsq0", [128, 512], BF16)]
    vfm_sb = [alloc(f"vfm{i}", [128, 512], BF16) for i in range(2)]
    kmax2 = alloc("kmax2", [1, HKV], F32)
    cbs_sb = alloc("cbs_sb", [1, NQ], F32)
    negc_bf = alloc("negc_bf", [1, NQ], BF16)
    ps_p = [alloc(f"ps_p{i}", [128, 512], F32, psum=True) for i in range(2)]
    ps_sw = [alloc(f"ps_sw{i}", [128, 512], F32, psum=True) for i in range(2)]
    ps_n = alloc("ps_n", [1, 512], F32, psum=True)

    last = {}

    def tk(key):
        v = last.get(key)
        return [] if v is None else (v if isinstance(v, list) else [v])

    n_ob = HQ + HKV + HKV

    def slab_src(ob):
        if ob < HQ:
            return wq, ob, D
        if ob < HQ + HKV:
            return wk, ob - HQ, HKV * DH
        return wv, ob - HQ - HKV, HKV * DH

    def emit_slab_load(ob, buf):
        w, col, rw = slab_src(ob)
        toks = []
        for j in range(4):
            def fn(e, w=w, col=col, j=j, buf=buf, rw=rw):
                src = bass.AP(w.tensor, j * 8 * 128 * rw + col * 128,
                              [[rw, 128], [128 * rw, 8], [1, 128]])
                dst = slab_sb[buf][:, j * 1024:(j + 1) * 1024].rearrange(
                    "p (c d) -> p c d", d=128)
                return e.dma_start(dst, src)
            toks.append(b.dma("pool", fn, waits=tk(("slabfree", buf))))
        return toks

    slab_tok = [None, None]
    slab_tok[0] = emit_slab_load(0, 0)
    npp = 0

    for ob in range(n_ob):
        buf = ob % 2
        if ob + 1 < n_ob:
            slab_tok[(ob + 1) % 2] = emit_slab_load(ob + 1, (ob + 1) % 2)
        is_q = ob < HQ
        is_v = ob >= HQ + HKV
        g_kv = None if is_q else (ob - HQ if not is_v else ob - HQ - HKV)
        for tc in range(NTC):
            pp = npp % 2
            npp += 1
            mm_tok = None
            for fc in range(NF):
                mm_tok = b.op("pe", lambda e, fc=fc, tc=tc, pp=pp, buf=buf: e.matmul(
                    ps_p[pp][:],
                    slab_sb[buf][:, fc * 128:(fc + 1) * 128],
                    x_fm[:, fc * S + tc * 512: fc * S + tc * 512 + 512],
                    start=(fc == 0), stop=(fc == NF - 1)),
                    waits=slab_tok[buf] +
                          [t_xfm[(qq, fc)] for qq in range(4 * tc, 4 * tc + 4)] +
                          tk(("ps_p", pp)),
                    inc=(fc == NF - 1))
            if tc == NTC - 1:
                last[("slabfree", buf)] = mm_tok
            rb = (ob * NTC + tc) % 2
            if is_v:
                t_ev = b.op("act", lambda e, pp=pp, rb=rb: e.copy(
                    vfm_sb[rb][:], ps_p[pp][:]),
                    waits=[mm_tok] + tk(("vfm", rb)))
                last[("ps_p", pp)] = t_ev
                t_ve = None
                for j in range(4):
                    pt = tr01_ctr[0] % 3
                    tr01_ctr[0] += 1
                    t_tr = b.op("pe", lambda e, rb=rb, j=j, pt=pt: e.transpose(
                        ps_tr1[pt][:], vfm_sb[rb][:, j * 128:(j + 1) * 128], ident_sb[:]),
                        waits=[t_ev, t_ident] + tk0(("tr1", pt)))
                    col = (tc * 4 + j) * HKV * 130 + g_kv * 130
                    t_ve = b.op("dve", lambda e, pt=pt, col=col: e.tensor_copy(
                        v_sb[:, col:col + 128], ps_tr1[pt][:]),
                        waits=[t_tr, t_vones])
                    L0[("tr1", pt)] = t_ve
                last[("vfm", rb)] = t_ve
                continue
            t_ev = b.op("act", lambda e, pp=pp, rb=rb: e.copy(
                qraw_sb[rb][:], ps_p[pp][:]),
                waits=[mm_tok] + tk(("qraw", rb)))
            last[("ps_p", pp)] = t_ev
            psw = (ob * NTC + tc) % 2
            t_sw = b.op("pe", lambda e, rb=rb, psw=psw: e.matmul(
                ps_sw[psw][:], swap_sb[:], qraw_sb[rb][:], start=True, stop=True),
                waits=[t_ev, t_swap] + tk(("ps_sw", psw)))
            t_m1 = b.op("pool", lambda e, rb=rb, tc=tc: e.tensor_mul(
                m1_sb[rb][:], qraw_sb[rb][:], cp_sb[:, tc * 512:tc * 512 + 512]),
                waits=[t_ev, t_cp] + tk(("m1", rb)))
            last[("qraw", rb)] = [t_sw, t_m1]
            t_m2 = b.op("dve", lambda e, rb=rb, psw=psw, tc=tc: e.tensor_mul(
                m2_sb[rb][:], ps_sw[psw][:], sp_sb[:, tc * 512:tc * 512 + 512]),
                waits=[t_sw, t_sp] + tk(("m2", rb)))
            last[("ps_sw", psw)] = t_m2
            t_fm = b.op("dve", lambda e, rb=rb: e.tensor_add(
                qfm_sb[rb][:], m1_sb[rb][:], m2_sb[rb][:]),
                waits=[t_m1, t_m2] + tk(("qfm", rb)))
            last[("m1", rb)] = t_fm
            last[("m2", rb)] = t_fm
            if is_q:
                t_st = b.dma("sp", lambda e, ob=ob, tc=tc, rb=rb: e.dma_start(
                    qs_s[ob * 128:(ob + 1) * 128, tc * 512:tc * 512 + 512],
                    qfm_sb[rb][:]), waits=[t_fm])
            else:
                t_st = b.dma("sp", lambda e, g_kv=g_kv, tc=tc, rb=rb: e.dma_start(
                    ks_s[g_kv * 128:(g_kv + 1) * 128, tc * 512:tc * 512 + 512],
                    qfm_sb[rb][:]), waits=[t_fm])
            t_sq = b.op("act", lambda e, rb=rb: e.square(qsq_sb[0][:], qfm_sb[rb][:]),
                        waits=[t_fm] + tk(("qsq",)))
            last[("qfm", rb)] = [t_st, t_sq]
            t_nm = b.op("pe", lambda e: e.matmul(
                ps_n[:], onec_sb[:], qsq_sb[0][:], start=True, stop=True),
                waits=[t_sq, t_onec] + tk(("ps_n",)))
            last[("qsq",)] = t_nm
            if is_q:
                idx = ob * NQB + tc
                t_rd = b.op("dve", lambda e, idx=idx: e.reduce_max(
                    cb_row[0:1, idx:idx + 1], ps_n[0:1, :], axis=AX.X),
                    waits=[t_nm])
            else:
                idx = g_kv * NTC + tc
                t_rd = b.op("dve", lambda e, idx=idx: e.reduce_max(
                    kn_row[0:1, idx:idx + 1], ps_n[0:1, :], axis=AX.X),
                    waits=[t_nm])
            last[("ps_n",)] = t_rd

    kn_3d = kn_row[0:1, :].rearrange("p (g t) -> p g t", g=HKV)
    km_3d = kmax2[0:1, :].rearrange("p (g o) -> p g o", o=1)
    t1 = b.op("dve", lambda e: e.tensor_reduce(
        km_3d, kn_3d, axis=AX.X, op=ALU.max),
        waits=[last[("ps_n",)]])
    cb_3d = cb_row[0:1, :].rearrange("p (g q) -> p g q", g=HKV)
    cb_b, km_b = bass.broadcast_tensor_aps(cb_3d, km_3d)
    t2 = b.op("dve", lambda e: e.tensor_tensor(cb_b, cb_b, km_b, op=ALU.mult),
              waits=[t1])
    t3 = b.op("act", lambda e: e.activation(
        cbs_sb[0:1, :], cb_row[0:1, :], AF.Sqrt, scale=1.0 / 128.0), waits=[t2])
    t4 = b.op("dve", lambda e: e.tensor_scalar(
        negc_bf[0:1, :], cbs_sb[0:1, :], -1.0, None, op0=ALU.mult), waits=[t3])
    t5 = b.op("pe", lambda e: e.matmul(
        ps_sw[0][:, 0:NQ], oner_sb[:], negc_bf[0:1, :], start=True, stop=True),
        waits=[t4, t_oner] + tk(("ps_sw", 0)))
    t_negc = b.op("dve", lambda e: e.tensor_copy(negc_sb[:, 0:NQ], ps_sw[0][:, 0:NQ]),
                  waits=[t5])

    free("ps_n", "ps_sw1", "ps_sw0", "ps_p1", "ps_p0")
    free("negc_bf", "cbs_sb", "kmax2", "vfm1", "vfm0", "qsq0", "qfm1", "qfm0",
         "m2_1", "m2_0", "m1_1", "m1_0", "qraw1", "qraw0", "slab1", "slab0",
         "kn_row", "cb_row", "sp_sb", "cp_sb")
    free("ps_tr1_2", "ps_tr1_1", "ps_tr1_0")
    free("xrow0")
    free("x_fm")
    b.bar()

    mask_sb = alloc("mask_sb", [128, 4 * 512], BF16)
    kfm_sb = alloc("kfm_sb", [128, HKV * S], BF16)
    qh_sb = [alloc(f"qh{i}", [128, S], BF16) for i in range(2)]
    pkm_sb = [alloc(f"pkm{i}", [128, NT128 * 512], BF16) for i in range(2)]
    aacc_sb = alloc("aacc_sb", [128, NT128 * 512], BF16)
    r_sb = alloc("r_sb", [128, 1], F32)
    ps_s = [alloc(f"ps_s{i}", [128, 512], F32, psum=True) for i in range(4)]
    ps_av = [alloc(f"ps_av{i}", [128, 132], F32, psum=True) for i in range(2)]

    _mask_src = bass.AP(mask_c.tensor, 0, [[512, 128], [128 * 512, 4], [1, 512]])
    _mask_dst = mask_sb[:].rearrange("p (o q) -> p o q", q=512)
    t_mask = b.dma("sp", lambda e: e.dma_start(_mask_dst, _mask_src))
    t_k = {}
    for g in range(HKV):
        t_k[g] = b.dma("sp", lambda e, g=g: e.dma_start(
            kfm_sb[:, g * S:(g + 1) * S], ks_s[g * 128:(g + 1) * 128, :]))

    L2 = {}

    def tk2(key):
        v = L2.get(key)
        return [] if v is None else (v if isinstance(v, list) else [v])

    def q_load(h, buf):
        return b.dma("sp", lambda e, h=h, buf=buf: e.dma_start(
            qh_sb[buf][:], qs_s[h * 128:(h + 1) * 128, :]),
            waits=tk2(("qhfree", buf)))

    L2[("qld", 0)] = q_load(0, 0)
    prod_pkm = {}
    avq = []
    av_ctr = [0]
    sc_ctr = [0]

    def pop_av(n):
        for _ in range(n):
            if avq:
                avq.pop(0)[1]()

    def drain_av_through(seq):
        while avq and avq[0][0] <= seq:
            avq.pop(0)[1]()

    def make_av_group(h, qb, pb, g, hh, kb_hi, q4, state):
        qsub = qb * 4 + q4

        def clo():
            ab = av_ctr[0] % 2
            av_ctr[0] += 1
            t_av = None
            for kb in range(qsub + 1):
                t_av = b.op("pe", lambda e, pb=pb, kb=kb, q4=q4, g=g, qsub=qsub, ab=ab: e.matmul(
                    ps_av[ab][:, 0:129],
                    pkm_sb[pb][:, kb * 512 + q4 * 128: kb * 512 + q4 * 128 + 128],
                    v_sb[:, kb * HKV * 130 + g * 130: kb * HKV * 130 + g * 130 + 129],
                    start=(kb == 0), stop=(kb == qsub)),
                    waits=[prod_pkm[(pb, kb)]] + tk2(("ps_av", ab)),
                    inc=(kb == qsub))
            state["last"] = t_av
            t_r = b.op("dve", lambda e, ab=ab: e.reciprocal(
                r_sb[:], ps_av[ab][:, 128:129]), waits=[t_av] + tk2(("r",)))
            t_an = b.op("dve", lambda e, ab=ab, qsub=qsub, hh=hh: e.tensor_scalar(
                aacc_sb[:, qsub * 512 + hh * 128: qsub * 512 + hh * 128 + 128],
                ps_av[ab][:, 0:128], r_sb[:], None, op0=ALU.mult),
                waits=[t_r] + tk2(("aaccfree", qsub)))
            L2[("r",)] = t_an
            L2[("ps_av", ab)] = t_an
            L2[("aaccw", qsub, hh)] = t_an
            if q4 == 3:
                for kb in range(kb_hi):
                    L2[("pkmfree", pb, kb)] = state["last"]
        return clo

    def make_store(g):
        def clo():
            for qsub in range(NT128):
                w = [L2[("aaccw", qsub, j)] for j in range(4)]
                t_sta = b.dma("sp", lambda e, qsub=qsub, g=g: e.dma_start(
                    as_s[qsub * 128:(qsub + 1) * 128, g * 512:(g + 1) * 512],
                    aacc_sb[:, qsub * 512:(qsub + 1) * 512]), waits=w)
                L2[("aaccfree", qsub)] = t_sta
        return clo

    for h in range(HQ):
        g = h // 4
        hh = h % 4
        hb = h % 2
        if h + 1 < HQ:
            L2[("qld", (h + 1) % 2)] = q_load(h + 1, (h + 1) % 2)
        last_sc_mm = None
        for qb in range(NQB):
            seq = h * NQB + qb
            pb = seq % 2
            kb_hi = 4 * qb + 4
            drain_av_through(seq - 2)
            for kb in range(kb_hi):
                sb_ = sc_ctr[0] % 4
                sc_ctr[0] += 1
                t_mm = b.op("pe", lambda e, g=g, kb=kb, qb=qb, hb=hb, sb_=sb_: e.matmul(
                    ps_s[sb_][:],
                    kfm_sb[:, g * S + kb * 128: g * S + kb * 128 + 128],
                    qh_sb[hb][:, qb * 512: qb * 512 + 512],
                    start=True, stop=True),
                    waits=[t_k[g]] + tk2(("qld", hb)) + tk2(("ps_s", sb_)))
                last_sc_mm = t_mm
                t_ex = b.op("act", lambda e, sb_=sb_, pb=pb, kb=kb, h=h, qb=qb: e.activation(
                    pkm_sb[pb][:, kb * 512: kb * 512 + 512], ps_s[sb_][:],
                    AF.Exp, bias=negc_sb[:, h * NQB + qb: h * NQB + qb + 1],
                    scale=RSQ),
                    waits=[t_mm, t_negc] + tk2(("pkmfree", pb, kb)))
                L2[("ps_s", sb_)] = t_ex
                if kb >= 4 * qb:
                    off = kb - 4 * qb
                    t_mk = b.op("dve", lambda e, pb=pb, kb=kb, off=off: e.tensor_mul(
                        pkm_sb[pb][:, kb * 512: kb * 512 + 512],
                        pkm_sb[pb][:, kb * 512: kb * 512 + 512],
                        mask_sb[:, off * 512:(off + 1) * 512]),
                        waits=[t_ex, t_mask])
                    prod_pkm[(pb, kb)] = t_mk
                else:
                    prod_pkm[(pb, kb)] = t_ex
                if kb % 2 == 1:
                    pop_av(1)
            state = {"last": None}
            for q4 in range(4):
                avq.append((seq, make_av_group(h, qb, pb, g, hh, kb_hi, q4, state)))
            if hh == 3 and qb == NQB - 1:
                avq.append((seq, make_store(g)))
        L2[("qhfree", hb)] = last_sc_mm
    pop_av(len(avq))

    free("ps_av1", "ps_av0", "ps_s3", "ps_s2", "ps_s1", "ps_s0")
    free("r_sb", "aacc_sb", "pkm1", "pkm0", "qh1", "qh0", "kfm_sb", "mask_sb")
    free("v_sb")
    b.bar()

    afm_sb = alloc("afm_sb", [128, NF * S], BF16)
    arow_sb = [alloc("arow0", [128, D], BF16)]
    wo_sb = [alloc(f"wo{i}", [128, NF * 512], BF16) for i in range(2)]
    o1t_sb = [alloc(f"o1t{i}", [128, 512], F32) for i in range(2)]
    ps_o = [alloc(f"ps_o{i}", [128, 512], F32, psum=True) for i in range(2)]
    ps_tr3 = [alloc(f"ps_tr3_{i}", [128, 128], BF16, psum=True) for i in range(3)]

    L3 = {}

    def tk3(key):
        v = L3.get(key)
        return [] if v is None else (v if isinstance(v, list) else [v])

    t_afm = {}
    tr3_ctr = [0]
    for qc in range(NT128):
        rb = 0
        t_ar = b.dma("sp", lambda e, qc=qc, rb=rb: e.dma_start(
            arow_sb[rb][:], as_s[qc * 128:(qc + 1) * 128, :]),
            waits=tk3(("arowfree", rb)))
        last_tr = None
        for cc in range(NF):
            pt = tr3_ctr[0] % 3
            tr3_ctr[0] += 1
            t_tr = b.op("pe", lambda e, rb=rb, cc=cc, pt=pt: e.transpose(
                ps_tr3[pt][:], arow_sb[rb][:, cc * 128:(cc + 1) * 128], ident_sb[:]),
                waits=[t_ar, t_ident] + tk3(("tr3", pt)))
            last_tr = t_tr
            col = cc * S + qc * 128
            if cc % 2 == 0:
                t_ev = b.op("act", lambda e, col=col, pt=pt: e.copy(
                    afm_sb[:, col:col + 128], ps_tr3[pt][:]), waits=[t_tr])
            else:
                t_ev = b.op("dve", lambda e, col=col, pt=pt: e.tensor_copy(
                    afm_sb[:, col:col + 128], ps_tr3[pt][:]), waits=[t_tr])
            L3[("tr3", pt)] = t_ev
            t_afm[(qc, cc)] = t_ev
        L3[("arowfree", rb)] = last_tr

    def wo_load(fc, buf):
        toks = []
        for j in range(4):
            def fn(e, fc=fc, j=j, buf=buf):
                src = bass.AP(wo.tensor, j * 8 * 128 * D + fc * 512,
                              [[D, 128], [128 * D, 8], [1, 512]])
                dst = wo_sb[buf][:, j * 8 * 512:(j + 1) * 8 * 512].rearrange(
                    "p (c f) -> p c f", f=512)
                return e.dma_start(dst, src)
            toks.append(b.dma("pool", fn, waits=tk3(("wofree", buf))))
        return toks

    wo_tok = [None, None]
    wo_tok[0] = wo_load(0, 0)
    for fc in range(8):
        buf = fc % 2
        if fc + 1 < 8:
            wo_tok[(fc + 1) % 2] = wo_load(fc + 1, (fc + 1) % 2)
        for qc in range(NT128):
            po = qc % 2
            t_mm = None
            for cc in range(NF):
                t_mm = b.op("pe", lambda e, cc=cc, qc=qc, po=po, buf=buf: e.matmul(
                    ps_o[po][:],
                    afm_sb[:, cc * S + qc * 128: cc * S + qc * 128 + 128],
                    wo_sb[buf][:, cc * 512:(cc + 1) * 512],
                    start=(cc == 0), stop=(cc == NF - 1)),
                    waits=wo_tok[buf] + [t_afm[(qc, cc)]] + tk3(("ps_o", po)),
                    inc=(cc == NF - 1))
            if qc == NT128 - 1:
                L3[("wofree", buf)] = t_mm
            t_ev = b.op("act", lambda e, po=po: e.copy(
                o1t_sb[po][:], ps_o[po][:]),
                waits=[t_mm] + tk3(("o1t", po)))
            L3[("ps_o", po)] = t_ev
            t_st = b.dma("sp", lambda e, qc=qc, fc=fc, po=po: e.dma_start(
                o1_s[qc * 128:(qc + 1) * 128, fc * 512:(fc + 1) * 512],
                o1t_sb[po][:]), waits=[t_ev])
            L3[("o1t", po)] = t_st

    free("ps_tr3_2", "ps_tr3_1", "ps_tr3_0", "ps_o1", "ps_o0")
    free("o1t1", "o1t0", "wo1", "wo0", "arow0", "afm_sb")
    b.bar()

    hfm_sb = alloc("hfm_sb", [128, NF * S], BF16)
    o1r_sb = [alloc(f"o1r{i}", [128, D], F32) for i in range(2)]
    xr_sb = [alloc(f"xr{i}", [128, D], F32) for i in range(2)]
    hbf_t = alloc("hbft0", [128, D], BF16)
    bn_sb = [alloc(f"bn_sb{i}", [128, 48], F32) for i in range(2)]
    mv_sb = [alloc(f"mv_sb{i}", [128, 2], F32) for i in range(2)]
    sd_sb = [alloc(f"sd_sb{i}", [128, 1], F32) for i in range(2)]
    rs_sb = [alloc(f"rs_sb{i}", [128, 1], F32) for i in range(2)]
    ps_h4 = [alloc(f"ps_h4_{i}", [128, 128], BF16, psum=True) for i in range(3)]

    L4 = {}

    def tk4(key):
        v = L4.get(key)
        return [] if v is None else (v if isinstance(v, list) else [v])

    for qc in range(NT128):
        rb = qc % 2
        t_lo = b.dma("sp", lambda e, qc=qc, rb=rb: e.dma_start(
            o1r_sb[rb][:], o1_s[qc * 128:(qc + 1) * 128, :]),
            waits=tk4(("o1r", rb)))
        t_lx = b.dma("pool", lambda e, qc=qc, rb=rb: e.dma_start(
            xr_sb[rb][:], xin[qc * 128:(qc + 1) * 128, :]),
            waits=tk4(("xr", rb)))
        ad_toks = []
        t_bn = None
        for j in range(8):
            t_ad = b.op("dve", lambda e, rb=rb, j=j: e.tensor_add(
                o1r_sb[rb][:, j * 512:(j + 1) * 512],
                o1r_sb[rb][:, j * 512:(j + 1) * 512],
                xr_sb[rb][:, j * 512:(j + 1) * 512]),
                waits=[t_lo, t_lx])
            ad_toks.append(t_ad)
            t_bn = b.op("dve", lambda e, rb=rb, j=j: e.bn_stats(
                bn_sb[rb][:, j * 6:(j + 1) * 6],
                o1r_sb[rb][:, j * 512:(j + 1) * 512]),
                waits=[t_ad] + (tk4(("bn", rb)) if j == 0 else []))
        L4[("xr", rb)] = ad_toks
        t_ag = b.op("dve", lambda e, rb=rb: e.bn_aggr(mv_sb[rb][:], bn_sb[rb][:]),
                    waits=[t_bn])
        L4[("bn", rb)] = t_ag
        t_sd = b.op("act", lambda e, rb=rb: e.activation(
            sd_sb[rb][:], mv_sb[rb][:, 1:2], AF.Sqrt, bias=eps_sb[:]),
            waits=[t_ag, t_eps] + tk4(("sd", rb)))
        t_rs = b.op("dve", lambda e, rb=rb: e.reciprocal(rs_sb[rb][:], sd_sb[rb][:]),
                    waits=[t_sd] + tk4(("rs", rb)))
        L4[("sd", rb)] = t_rs
        t_h = b.op("dve", lambda e, rb=rb: e.tensor_scalar(
            o1r_sb[rb][:], o1r_sb[rb][:], mv_sb[rb][:, 0:1], rs_sb[rb][:],
            op0=ALU.subtract, op1=ALU.mult),
            waits=[t_rs])
        L4[("rs", rb)] = t_h
        t_cb = b.op("act", lambda e, rb=rb: e.copy(hbf_t[:], o1r_sb[rb][:]),
                    waits=[t_h] + tk4(("hbft",)))
        t_s1 = b.dma("sp", lambda e, qc=qc, rb=rb: e.dma_start(
            h_s[qc * 128:(qc + 1) * 128, :], o1r_sb[rb][:]), waits=[t_h])
        L4[("o1r", rb)] = [t_cb, t_s1]
        t_tr_last = None
        for g in range(NF):
            pt = (qc * NF + g) % 3
            t_tr = b.op("pe", lambda e, g=g, pt=pt: e.transpose(
                ps_h4[pt][:], hbf_t[:, g * 128:(g + 1) * 128], ident_sb[:]),
                waits=[t_cb, t_ident] + tk4(("h4ps", pt)))
            t_tr_last = t_tr
            col = g * S + qc * 128
            if g % 2 == 0:
                t_ev = b.op("act", lambda e, col=col, pt=pt: e.copy(
                    hfm_sb[:, col:col + 128], ps_h4[pt][:]), waits=[t_tr])
            else:
                t_ev = b.op("dve", lambda e, col=col, pt=pt: e.tensor_copy(
                    hfm_sb[:, col:col + 128], ps_h4[pt][:]), waits=[t_tr])
            L4[("h4ps", pt)] = t_ev
        L4[("hbft",)] = t_tr_last

    free("ps_h4_2", "ps_h4_1", "ps_h4_0")
    free("rs_sb1", "rs_sb0", "sd_sb1", "sd_sb0", "mv_sb1", "mv_sb0",
         "bn_sb1", "bn_sb0", "hbft0", "xr1", "xr0", "o1r1", "o1r0")
    b.bar()

    grawq_sb = [alloc(f"grawq{i}", [128, D], BF16) for i in range(3)]
    sc5_sb = [[alloc(f"sc5_{m}_{i}", [128, NF], F32) for i in range(2)]
              for m in range(2)]
    zc5_sb = [[alloc(f"zc5_{m}_{i}", [128, NF], F32) for i in range(2)]
              for m in range(2)]
    zs5_sb = [[alloc(f"zs5_{m}_{i}", [128, NF], F32) for i in range(2)]
              for m in range(2)]
    wn5_sb = alloc("wn5_sb", [128, NF * 128], BF16)
    wslab5_sb = [alloc(f"wslab5_{m}", [128, NF * 128], BF16) for m in range(2)]
    gtile5_sb = alloc("gtile5_sb", [128, S], BF16)
    hgt5_sb = [alloc(f"hgt5_{i}", [128, S], BF16) for i in range(2)]
    ps_g5 = [alloc(f"ps_g5_{i}", [128, 512], F32, psum=True) for i in range(2)]
    ps_u5 = [alloc(f"ps_u5_{i}", [128, 512], F32, psum=True) for i in range(2)]
    ps_t5 = [alloc(f"ps_t5_{i}", [128, 128], BF16, psum=True) for i in range(4)]

    L5 = {}

    def tk5(key):
        v = L5.get(key)
        return [] if v is None else (v if isinstance(v, list) else [v])

    MATS5 = [("g", gq, gs, gz), ("u", uq, us, uz)]
    raw5_ctr = [0]
    tr5_ctr = [0]

    def emit_gu_scales(ib, m):
        ip = ib % 2
        _, qw, sw, zw = MATS5[m]
        t_s = b.dma("sp", lambda e, sw=sw, ib=ib, m=m, ip=ip: e.dma_start(
            sc5_sb[m][ip][:], sw[ib * 128:(ib + 1) * 128, :]),
            waits=tk5(("scuse5", m, ip)))
        t_z = b.dma("sp", lambda e, zw=zw, ib=ib, m=m, ip=ip: e.dma_start(
            zc5_sb[m][ip][:], zw[ib * 128:(ib + 1) * 128, :]),
            waits=tk5(("scuse5", m, ip)))
        t_zs = b.op("dve", lambda e, m=m, ip=ip: e.tensor_mul(
            zs5_sb[m][ip][:], zc5_sb[m][ip][:], sc5_sb[m][ip][:]),
            waits=[t_s, t_z] + tk5(("scuse5", m, ip)))
        L5[("scld5", m, ip)] = [t_s, t_zs]

    def emit_gu_raw(ib, m):
        slot = raw5_ctr[0] % 3
        raw5_ctr[0] += 1
        _, qw, _, _ = MATS5[m]
        tok = b.dma("pool", lambda e, qw=qw, ib=ib, slot=slot: e.dma_start(
            grawq_sb[slot][:], qw[ib * 128:(ib + 1) * 128, :]),
            waits=tk5(("rawfree5", slot)))
        return slot, tok

    def production5(ib, m, rawslot, rawtok):
        ip = ib % 2
        sc_last = None
        for g in range(NF):
            pslot = tr5_ctr[0] % 4
            tr5_ctr[0] += 1
            t_sc = b.op("dve", lambda e, m=m, ip=ip, g=g, rawslot=rawslot: e.tensor_scalar(
                wn5_sb[:, g * 128:(g + 1) * 128],
                grawq_sb[rawslot][:, g * 128:(g + 1) * 128],
                sc5_sb[m][ip][:, g:g + 1], zs5_sb[m][ip][:, g:g + 1],
                op0=ALU.mult, op1=ALU.subtract),
                waits=[rawtok] + tk5(("scld5", m, ip)) + tk5(("wn5", g)))
            sc_last = t_sc
            L5[("scuse5", m, ip)] = t_sc
            t_tr = b.op("pe", lambda e, g=g, pslot=pslot: e.transpose(
                ps_t5[pslot][:],
                wn5_sb[:, g * 128:(g + 1) * 128], ident_sb[:]),
                waits=[t_sc, t_ident] + tk5(("t5ps", pslot)))
            L5[("wn5", g)] = t_tr
            if g % 2 == 0:
                t_ev = b.op("act", lambda e, m=m, g=g, pslot=pslot: e.copy(
                    wslab5_sb[m][:, g * 128:(g + 1) * 128], ps_t5[pslot][:]),
                    waits=[t_tr] + tk5(("mmlast5", m)))
            else:
                t_ev = b.op("dve", lambda e, m=m, g=g, pslot=pslot: e.tensor_copy(
                    wslab5_sb[m][:, g * 128:(g + 1) * 128], ps_t5[pslot][:]),
                    waits=[t_tr] + tk5(("mmlast5", m)))
            L5[("t5ps", pslot)] = t_ev
            L5[("slabrdy5", m, g)] = t_ev
            yield
        L5[("rawfree5", rawslot)] = sc_last

    def emit_gu_matmuls(ib, m, prod):
        psb = ps_g5 if m == 0 else ps_u5
        hb5 = (ib % 2)
        step = 0
        for tc in range(NTC):
            pp = tc % 2
            t_mm = None
            for g in range(NF):
                t_mm = b.op("pe", lambda e, m=m, g=g, tc=tc, pp=pp, psb=psb: e.matmul(
                    psb[pp][:],
                    wslab5_sb[m][:, g * 128:(g + 1) * 128],
                    hfm_sb[:, g * S + tc * 512: g * S + tc * 512 + 512],
                    start=(g == 0), stop=(g == NF - 1)),
                    waits=tk5(("slabrdy5", m, g)) +
                          (tk5(("psfree5", m, pp)) if g == 0 else []),
                    inc=(g == NF - 1))
                step += 1
                if prod is not None and step % 4 == 0:
                    next(prod, None)
            L5[("mmlast5", m)] = t_mm
            if m == 0:
                t_si = b.op("act", lambda e, tc=tc, pp=pp: e.activation(
                    gtile5_sb[:, tc * 512:(tc + 1) * 512], ps_g5[pp][:], AF.Silu),
                    waits=[t_mm] + tk5(("gtfree", tc)))
                L5[("psfree5", 0, pp)] = t_si
                L5[("gtw", tc)] = t_si
            else:
                t_mul = b.op("dve", lambda e, tc=tc, pp=pp, hb5=hb5: e.tensor_mul(
                    hgt5_sb[hb5][:, tc * 512:(tc + 1) * 512],
                    ps_u5[pp][:], gtile5_sb[:, tc * 512:(tc + 1) * 512]),
                    waits=[t_mm] + tk5(("gtw", tc)) + tk5(("hgt5", hb5)))
                L5[("psfree5", 1, pp)] = t_mul
                L5[("gtfree", tc)] = t_mul
                L5[("mulw", tc)] = t_mul
        if m == 1:
            t_hg = b.dma("sp", lambda e, ib=ib, hb5=hb5: e.dma_start(
                hg_s[ib * 128:(ib + 1) * 128, :], hgt5_sb[hb5][:]),
                waits=[L5[("mulw", tc2)] for tc2 in range(NTC)])
            L5[("hgt5", hb5)] = t_hg

    order5 = [(ib, m) for ib in range(NIB) for m in range(2)]
    emit_gu_scales(0, 0)
    s0, t0 = emit_gu_raw(0, 0)
    for _ in production5(0, 0, s0, t0):
        pass
    for idx, (ib, m) in enumerate(order5):
        if idx + 1 < len(order5):
            ib2, m2 = order5[idx + 1]
            emit_gu_scales(ib2, m2)
            s2, t2 = emit_gu_raw(ib2, m2)
            nxt = production5(ib2, m2, s2, t2)
        else:
            nxt = None
        emit_gu_matmuls(ib, m, nxt)
        if nxt is not None:
            for _ in nxt:
                pass

    free("ps_t5_3", "ps_t5_2", "ps_t5_1", "ps_t5_0", "ps_u5_1", "ps_u5_0",
         "ps_g5_1", "ps_g5_0")
    free("hgt5_1", "hgt5_0", "gtile5_sb", "wslab5_1", "wslab5_0", "wn5_sb")
    free(*[f"zs5_{m}_{i}" for m in (1, 0) for i in (1, 0)])
    free(*[f"zc5_{m}_{i}" for m in (1, 0) for i in (1, 0)])
    free(*[f"sc5_{m}_{i}" for m in (1, 0) for i in (1, 0)])
    free("grawq2", "grawq1", "grawq0")
    free("hfm_sb")
    b.bar()

    IC_CH = [22, 22, 21, 21]
    IC_OFF = [0, 22, 44, 65]
    MAXC = 22
    NHG = 8

    ident32_sb = alloc("ident32_sb", [128, 128], F32)
    rawd_sb = [[alloc(f"rawd{p}_{j}", [128, MAXC * 128], BF16) for j in range(4)]
               for p in range(2)]
    dsc6_sb = [[alloc(f"dsc6_{p}_{j}", [128, NIB], F32) for j in range(4)]
               for p in range(2)]
    dzc6_sb = [[alloc(f"dzc6_{p}_{j}", [128, NIB], F32) for j in range(4)]
               for p in range(2)]
    dzs6_sb = [[alloc(f"dzs6_{p}_{j}", [128, NIB], F32) for j in range(4)]
               for p in range(2)]
    wnat6_sb = alloc("wnat6_sb", [128, 8 * 128], BF16)
    wdts_sb = [alloc(f"wdts{p}", [128, MAXC * 512], BF16) for p in range(2)]
    NHGT = 8
    hgt6_sb = [alloc(f"hgt6_{p}", [128, 512], BF16) for p in range(NHGT)]
    o2acc_sb = [alloc(f"o2acc_{j}", [128, NTC * 512], F32) for j in range(4)]
    o2a_sb = [alloc(f"o2a_{p}", [128, 2048], F32) for p in range(2)]
    ps_acc = [alloc(f"ps_acc{j}", [128, 512], F32, psum=True) for j in range(4)]
    ps_wtr = [alloc(f"ps_wtr{i}", [128, 128], BF16, psum=True) for i in range(4)]

    t_id32 = b.dma("sp", lambda e: e.dma_start(ident32_sb[:], ident32_c[:]))

    L6 = {}

    def tk6(key):
        v = L6.get(key)
        return [] if v is None else (v if isinstance(v, list) else [v])

    wtr_ctr = [0]
    hgl_ctr = [0]

    def emit_hg_scales(hg):
        hp = hg % 2
        for j in range(4):
            r0 = (hg * 4 + j) * 128
            t_s = b.dma("act", lambda e, r0=r0, hp=hp, j=j: e.dma_start(
                dsc6_sb[hp][j][:], ds[r0:r0 + 128, :]),
                waits=tk6(("scuse", hp, j)))
            t_z = b.dma("act", lambda e, r0=r0, hp=hp, j=j: e.dma_start(
                dzc6_sb[hp][j][:], dz[r0:r0 + 128, :]),
                waits=tk6(("scuse", hp, j)))
            t_zs = b.op("dve", lambda e, hp=hp, j=j: e.tensor_mul(
                dzs6_sb[hp][j][:], dzc6_sb[hp][j][:], dsc6_sb[hp][j][:]),
                waits=[t_s, t_z] + tk6(("scuse", hp, j)))
            L6[("scld", hp, j)] = [t_s, t_zs]

    def production_steps(hg, c):
        hp = hg % 2
        p = c % 2
        cw = IC_CH[c]
        i0 = IC_OFF[c]
        raw_toks = []
        for j in range(4):
            r0 = (hg * 4 + j) * 128

            def fn(e, r0=r0, p=p, j=j, i0=i0, cw=cw):
                return e.dma_start(
                    rawd_sb[p][j][:, 0:cw * 128],
                    dq[r0:r0 + 128, i0 * 128:(i0 + cw) * 128])
            raw_toks.append(b.dma("pool", fn, waits=tk6(("rawfree", p, j))))
        sc_last = {j: None for j in range(4)}
        pend_ev = []

        def flush_evacs():
            while pend_ev:
                j2, il2, pslot, t_tr = pend_ev.pop(0)
                col = il2 * 512 + j2 * 128
                if (il2 * 4 + j2) % 2 == 0:
                    t_ev = b.op("act", lambda e, p=p, col=col, pslot=pslot: e.copy(
                        wdts_sb[p][:, col:col + 128], ps_wtr[pslot][:]),
                        waits=[t_tr] + tk6(("mmlast", p)))
                else:
                    t_ev = b.op("dve", lambda e, p=p, col=col, pslot=pslot: e.tensor_copy(
                        wdts_sb[p][:, col:col + 128], ps_wtr[pslot][:]),
                        waits=[t_tr] + tk6(("mmlast", p)))
                L6[("wtrps", pslot)] = t_ev
                L6[("slabrdy", p, il2, j2)] = t_ev

        for il in range(cw):
            i = i0 + il
            flush_evacs()
            for j in range(4):
                wslot = wtr_ctr[0] % 8
                pslot = wtr_ctr[0] % 4
                wtr_ctr[0] += 1
                t_sc = b.op("dve", lambda e, p=p, j=j, il=il, i=i, wslot=wslot, hp=hp: e.tensor_scalar(
                    wnat6_sb[:, wslot * 128:(wslot + 1) * 128],
                    rawd_sb[p][j][:, il * 128:(il + 1) * 128],
                    dsc6_sb[hp][j][:, i:i + 1], dzs6_sb[hp][j][:, i:i + 1],
                    op0=ALU.mult, op1=ALU.subtract),
                    waits=[raw_toks[j]] + tk6(("scld", hp, j)) + tk6(("wnslot", wslot)))
                sc_last[j] = t_sc
                L6[("scuse", hp, j)] = t_sc
                t_tr = b.op("pe", lambda e, wslot=wslot, pslot=pslot: e.transpose(
                    ps_wtr[pslot][:],
                    wnat6_sb[:, wslot * 128:(wslot + 1) * 128], ident_sb[:]),
                    waits=[t_sc, t_ident] + tk6(("wtrps", pslot)))
                L6[("wnslot", wslot)] = t_tr
                pend_ev.append((j, il, pslot, t_tr))
            yield
        flush_evacs()
        for j in range(4):
            L6[("rawfree", p, j)] = sc_last[j]

    def emit_chunk_matmuls(hg, c, prod):
        p = c % 2
        cw = IC_CH[c]
        i0 = IC_OFF[c]
        seq = [(tc, il) for tc in range(NTC) for il in range(cw)]
        AHEAD = 6
        hl_toks = {}

        def issue_load(k):
            tc, il = seq[k]
            i = i0 + il
            hb = hgl_ctr[0] % NHGT
            hgl_ctr[0] += 1
            hl_toks[k] = (hb, b.dma("sp", lambda e, i=i, tc=tc, hb=hb: e.dma_start(
                hgt6_sb[hb][:],
                hg_s[i * 128:(i + 1) * 128, tc * 512:(tc + 1) * 512]),
                waits=tk6(("hgfree", hb))))

        for k in range(min(AHEAD, len(seq))):
            issue_load(k)
        mm_last = None
        mm_toks = {}
        for k, (tc, il) in enumerate(seq):
            if k + AHEAD < len(seq):
                issue_load(k + AHEAD)
            hb, t_hl = hl_toks.pop(k)
            for j in range(4):
                do_inc = (j == 3) or (il == cw - 1)
                t_mm = b.op("pe", lambda e, p=p, il=il, j=j, hb=hb, cw=cw: e.matmul(
                    ps_acc[j][:],
                    wdts_sb[p][:, il * 512 + j * 128: il * 512 + j * 128 + 128],
                    hgt6_sb[hb][:],
                    start=(il == 0), stop=(il == cw - 1)),
                    waits=[t_hl] + tk6(("slabrdy", p, il, j)) +
                          (tk6(("psfree", j)) if il == 0 else []),
                    inc=do_inc)
                if il == cw - 1:
                    mm_toks[j] = t_mm
                if j == 3:
                    L6[("hgfree", hb)] = t_mm
                    mm_last = t_mm
            if prod is not None:
                next(prod, None)
            if il == cw - 1:
                for j in range(4):
                    if c == 0:
                        t_fl = b.op("dve", lambda e, j=j, tc=tc: e.tensor_copy(
                            o2acc_sb[j][:, tc * 512:(tc + 1) * 512], ps_acc[j][:]),
                            waits=[mm_toks[j]] + tk6(("o2accfree", j, tc)))
                    else:
                        t_fl = b.op("dve", lambda e, j=j, tc=tc: e.tensor_add(
                            o2acc_sb[j][:, tc * 512:(tc + 1) * 512],
                            o2acc_sb[j][:, tc * 512:(tc + 1) * 512], ps_acc[j][:]),
                            waits=[mm_toks[j]])
                    L6[("psfree", j)] = t_fl
                    if c == 3:
                        L6[("accdone", j, tc)] = t_fl
        L6[("mmlast", p)] = mm_last

    def emit_hg_output(hg):
        otr_last = {}
        for tc in range(NTC):
            op_ = (hg * NTC + tc) % 2
            ev_toks = []
            for j in range(4):
                for ts_ in range(4):
                    po = (j + ts_) % 4
                    src_off = tc * 512 + ts_ * 128
                    t_tr = b.op("pe", lambda e, j=j, po=po, src_off=src_off: e.transpose(
                        ps_acc[po][:, 0:128],
                        o2acc_sb[j][:, src_off:src_off + 128], ident32_sb[:]),
                        waits=tk6(("accdone", j, tc)) + [t_id32] +
                              tk6(("otrps", po)) + tk6(("psfree", po)))
                    dcol = ts_ * 512 + j * 128
                    if (j * 4 + ts_) % 2 == 0:
                        t_ev = b.op("act", lambda e, op_=op_, dcol=dcol, po=po: e.copy(
                            o2a_sb[op_][:, dcol:dcol + 128],
                            ps_acc[po][:, 0:128]),
                            waits=[t_tr] + tk6(("o2afree", op_)))
                    else:
                        t_ev = b.op("dve", lambda e, op_=op_, dcol=dcol, po=po: e.tensor_copy(
                            o2a_sb[op_][:, dcol:dcol + 128],
                            ps_acc[po][:, 0:128]),
                            waits=[t_tr] + tk6(("o2afree", op_)))
                    L6[("otrps", po)] = t_ev
                    otr_last[po] = t_ev
                    ev_toks.append(t_ev)
                    if ts_ == 3:
                        L6[("o2accfree", j, tc)] = t_tr
            dst = bass.AP(o2_s.tensor, (tc * 512) * D + hg * 512,
                          [[D, 128], [128 * D, 4], [1, 512]])
            src = o2a_sb[op_][:].rearrange("p (s h) -> p s h", h=512)
            t_st = b.dma("act", lambda e, dst=dst, src=src: e.dma_start(dst, src),
                         waits=ev_toks)
            L6[("o2afree", op_)] = t_st
        for po in range(4):
            L6[("psfree", po)] = otr_last[po]

    emit_hg_scales(0)
    for _ in production_steps(0, 0):
        pass
    for hg in range(NHG):
        for c in range(4):
            if c < 3:
                nxt = production_steps(hg, c + 1)
            elif hg + 1 < NHG:
                emit_hg_scales(hg + 1)
                nxt = production_steps(hg + 1, 0)
            else:
                nxt = None
            emit_chunk_matmuls(hg, c, nxt)
            if nxt is not None:
                for _ in nxt:
                    pass
        emit_hg_output(hg)

    free("ps_wtr3", "ps_wtr2", "ps_wtr1", "ps_wtr0",
         "ps_acc3", "ps_acc2", "ps_acc1", "ps_acc0")
    free("o2a_1", "o2a_0", "o2acc_3", "o2acc_2", "o2acc_1", "o2acc_0",
         *[f"hgt6_{i}" for i in range(NHGT - 1, -1, -1)],
         "wdts1", "wdts0", "wnat6_sb")
    free(*[f"dzs6_{p}_{j}" for p in (1, 0) for j in (3, 2, 1, 0)][::1])
    free(*[f"dzc6_{p}_{j}" for p in (1, 0) for j in (3, 2, 1, 0)])
    free(*[f"dsc6_{p}_{j}" for p in (1, 0) for j in (3, 2, 1, 0)])
    free(*[f"rawd{p}_{j}" for p in (1, 0) for j in (3, 2, 1, 0)])
    free("ident32_sb")
    b.bar()

    o2r_sb = [alloc(f"o2r{i}", [128, D], F32) for i in range(2)]
    hr_sb = [alloc(f"hr{i}", [128, D], F32) for i in range(2)]
    bn7_sb = [alloc(f"bn7_sb{i}", [128, 48], F32) for i in range(2)]
    mv7_sb = [alloc(f"mv7_sb{i}", [128, 2], F32) for i in range(2)]
    sd7_sb = [alloc(f"sd7_sb{i}", [128, 1], F32) for i in range(2)]
    rs7_sb = [alloc(f"rs7_sb{i}", [128, 1], F32) for i in range(2)]

    L7 = {}

    def tk7(key):
        v = L7.get(key)
        return [] if v is None else (v if isinstance(v, list) else [v])

    for qc in range(NT128):
        rb = qc % 2
        t_lo = b.dma("sp", lambda e, qc=qc, rb=rb: e.dma_start(
            o2r_sb[rb][:], o2_s[qc * 128:(qc + 1) * 128, :]),
            waits=tk7(("o2r", rb)))
        t_lh = b.dma("pool", lambda e, qc=qc, rb=rb: e.dma_start(
            hr_sb[rb][:], h_s[qc * 128:(qc + 1) * 128, :]),
            waits=tk7(("hr", rb)))
        ad_toks = []
        t_bn = None
        for j in range(8):
            t_ad = b.op("dve", lambda e, rb=rb, j=j: e.tensor_add(
                o2r_sb[rb][:, j * 512:(j + 1) * 512],
                o2r_sb[rb][:, j * 512:(j + 1) * 512],
                hr_sb[rb][:, j * 512:(j + 1) * 512]),
                waits=[t_lo, t_lh])
            ad_toks.append(t_ad)
            t_bn = b.op("dve", lambda e, rb=rb, j=j: e.bn_stats(
                bn7_sb[rb][:, j * 6:(j + 1) * 6],
                o2r_sb[rb][:, j * 512:(j + 1) * 512]),
                waits=[t_ad] + (tk7(("bn7", rb)) if j == 0 else []))
        L7[("hr", rb)] = ad_toks
        t_ag = b.op("dve", lambda e, rb=rb: e.bn_aggr(mv7_sb[rb][:], bn7_sb[rb][:]),
                    waits=[t_bn])
        L7[("bn7", rb)] = t_ag
        t_sd = b.op("act", lambda e, rb=rb: e.activation(
            sd7_sb[rb][:], mv7_sb[rb][:, 1:2], AF.Sqrt, bias=eps_sb[:]),
            waits=[t_ag, t_eps] + tk7(("sd7", rb)))
        t_rs = b.op("dve", lambda e, rb=rb: e.reciprocal(rs7_sb[rb][:], sd7_sb[rb][:]),
                    waits=[t_sd] + tk7(("rs7", rb)))
        L7[("sd7", rb)] = t_rs
        t_no = b.op("dve", lambda e, rb=rb: e.tensor_scalar(
            o2r_sb[rb][:], o2r_sb[rb][:], mv7_sb[rb][:, 0:1], rs7_sb[rb][:],
            op0=ALU.subtract, op1=ALU.mult), waits=[t_rs])
        L7[("rs7", rb)] = t_no
        t_st = b.dma("sp", lambda e, qc=qc, rb=rb: e.dma_start(
            out[qc * 128:(qc + 1) * 128, :], o2r_sb[rb][:]), waits=[t_no])
        L7[("o2r", rb)] = t_st

    if DEBUG_OUTS:
        b.wait("pool", [(s, b.cnt[s]) for s in SEMS if b.cnt[s] > 0])
        dbg = {
            "dbg_qs": (qs_s, [D, S], BF16),
            "dbg_ks": (ks_s, [HKV * DH, S], BF16),
            "dbg_as": (as_s, [S, D], BF16),
            "dbg_o1": (o1_s, [S, D], F32),
            "dbg_h": (h_s, [S, D], F32),
            "dbg_hg": (hg_s, [I_FF, S], BF16),
            "dbg_hbf": (hbf_s, [S, D], BF16),
            "dbg_wg": (wgdq_s, [I_FF, D], BF16),
            "dbg_wu": (wudq_s, [I_FF, D], BF16),
            "dbg_o2": (o2_s, [S, D], F32),
        }
        for name, (srcap, shp, dt) in dbg.items():
            dout = nc.dram_tensor(name, shp, dt, kind="ExternalOutput").ap()
            b.dma("pool", lambda e, dout=dout, srcap=srcap: e.dma_start(
                dout[:], srcap[:]))

    b.wait("sp", [(s, b.cnt[s]) for s in SEMS if b.cnt[s] > 0])

    import contextlib
    with contextlib.ExitStack() as es:
        for s in SEMS:
            b.sem[s] = es.enter_context(nc.semaphore(s))
        block = es.enter_context(nc.Block())
        b.replay(block)
    for n, cm in reversed(ctx_stack):
        cm.__exit__(None, None, None)
    ctx_stack.clear()
    return nc


def make_inputs(inputs, S):
    x = np.ascontiguousarray(np.asarray(inputs["L0_attn_x"], dtype=np.float32))
    qperm = _q_perm(HQ)
    kperm = _q_perm(HKV)
    base = {
        "wq": np.ascontiguousarray(np.asarray(inputs["L0_attn_wq"], dtype=np.float32)[:, qperm]),
        "wk": np.ascontiguousarray(np.asarray(inputs["L0_attn_wk"], dtype=np.float32)[:, kperm]),
        "wv": np.ascontiguousarray(np.asarray(inputs["L0_attn_wv"], dtype=np.float32)),
        "wo": np.ascontiguousarray(np.asarray(inputs["L0_attn_wo"], dtype=np.float32)),
        "gq": np.ascontiguousarray(np.asarray(inputs["L0_ffn_gate_qweight"], dtype=np.int32)),
        "gs": np.ascontiguousarray(np.asarray(inputs["L0_ffn_gate_scales"], dtype=np.float32)),
        "gz": np.ascontiguousarray(np.asarray(inputs["L0_ffn_gate_zeros"], dtype=np.float32)),
        "uq": np.ascontiguousarray(np.asarray(inputs["L0_ffn_up_qweight"], dtype=np.int32)),
        "us": np.ascontiguousarray(np.asarray(inputs["L0_ffn_up_scales"], dtype=np.float32)),
        "uz": np.ascontiguousarray(np.asarray(inputs["L0_ffn_up_zeros"], dtype=np.float32)),
        "dq": np.ascontiguousarray(np.asarray(inputs["L0_ffn_down_qweight"], dtype=np.int32)),
        "ds": np.ascontiguousarray(np.asarray(inputs["L0_ffn_down_scales"], dtype=np.float32)),
        "dz": np.ascontiguousarray(np.asarray(inputs["L0_ffn_down_zeros"], dtype=np.float32)),
    }
    base.update(_host_consts(S))
    return x, base


def kernel(**inputs):
    x, base = make_inputs(inputs, 2048)
    B, S, _ = x.shape
    assert B == 8 and S == 2048
    nc = build_program(S)
    in_maps = []
    for c in range(8):
        m = dict(base)
        m["xin"] = np.ascontiguousarray(x[c])
        in_maps.append(m)
    res = run_bass_kernel_spmd(nc, in_maps, core_ids=list(range(8)))
    return np.stack([r["out"] for r in res.results], axis=0)


# version 14
# speedup vs baseline: 1.1359x; 1.1359x over previous
import math
import numpy as np
import ml_dtypes
import concourse.bass as bass
import concourse.mybir as mybir
from concourse.bass_utils import run_bass_kernel_spmd

F32 = mybir.dt.float32
BF16 = mybir.dt.bfloat16
I32 = mybir.dt.int32
AF = mybir.ActivationFunctionType
ALU = mybir.AluOpType
AX = mybir.AxisListType

D = 4096
NF = 32
HQ = 32
HKV = 8
DH = 128
I_FF = 11008
NIB = 86
RSQ = 1.0 / math.sqrt(128.0)
EPS = 1e-5
DEBUG_OUTS = False

ENGINES = ("pe", "dve", "act", "pool", "sp")
NDMASEM = 8
DMA_QUEUES = ("qsp", "qact", "qpool")
SEMS = ("pe", "dve", "act", "pool") + tuple(
    f"{q}{r}" for q in DMA_QUEUES for r in range(NDMASEM))
DMA_SEM = {"sp": "qsp", "act": "qact", "pool": "qpool"}


def _flat(toks):
    out = []
    for t in toks:
        if t is None:
            continue
        if isinstance(t, list):
            out.extend(x for x in t if x is not None)
        else:
            out.append(t)
    return out


class _B:

    def __init__(self, nc):
        self.nc = nc
        self.ops = {e: [] for e in ENGINES}
        self.cnt = {s: 0 for s in SEMS}
        self.cache = {e: {s: 0 for s in SEMS} for e in ENGINES}
        self.pend_bar = {e: None for e in ENGINES}
        self.sem = {}
        self.n_inst = 0
        self.dma_rot = {q: 0 for q in DMA_QUEUES}

    def _waits(self, eng, waits):
        out = []
        bar = self.pend_bar[eng]
        if bar is not None:
            self.pend_bar[eng] = None
            for s, v in bar:
                if s == eng:
                    continue
                out.append((s, v))
        out.extend(_flat(list(waits)))
        ded = []
        for s, v in out:
            if v > self.cache[eng][s]:
                self.cache[eng][s] = v
                ded.append((s, v))
        return ded

    def op(self, eng, fn, waits=(), inc=True):
        ded = self._waits(eng, waits)
        tok = None
        incs = []
        if inc:
            self.cnt[eng] += 1
            tok = (eng, self.cnt[eng])
            incs = [(eng, 1)]
        self.ops[eng].append((ded, fn, incs))
        self.n_inst += 1 + len(ded)
        return tok

    def dma(self, eng, fn, waits=()):
        q = DMA_SEM[eng]
        r = self.dma_rot[q] % NDMASEM
        self.dma_rot[q] += 1
        s = f"{q}{r}"
        ded = self._waits(eng, list(waits) + ([(s, self.cnt[s])] if self.cnt[s] else []))
        self.cnt[s] += 16
        tok = (s, self.cnt[s])
        self.ops[eng].append((ded, fn, [(s, 16)]))
        self.n_inst += 1 + len(ded)
        return tok

    def wait(self, eng, waits):
        ded = self._waits(eng, waits)
        if ded:
            self.ops[eng].append((ded, None, []))
            self.n_inst += len(ded)

    def bar(self):
        snap = [(s, self.cnt[s]) for s in SEMS if self.cnt[s] > 0]
        for e in ENGINES:
            self.pend_bar[e] = snap

    def replay(self, block):
        def mk(eng):
            def run(e):
                for ded, fn, incs in self.ops[eng]:
                    for s, v in ded:
                        e.wait_ge(self.sem[s], v)
                    if fn is not None:
                        inst = fn(e)
                        for s, a in incs:
                            inst.then_inc(self.sem[s], a)
            return run

        block.tensor(mk("pe"))
        block.vector(mk("dve"))
        block.scalar(mk("act"))
        block.gpsimd(mk("pool"))
        block.sync(mk("sp"))


def _q_perm(nheads):
    p = []
    for h in range(nheads):
        base = h * DH
        p.extend(base + j for j in range(0, DH, 2))
        p.extend(base + j for j in range(1, DH, 2))
    return np.array(p, dtype=np.int64)


def _host_consts(S):
    inv = 1.0 / (10000.0 ** (np.arange(0, DH, 2, dtype=np.float64) / DH))
    t = np.arange(S, dtype=np.float64)
    ang = np.outer(inv, t)
    cos = np.cos(ang).astype(np.float32)
    sin = np.sin(ang).astype(np.float32)
    cp = np.concatenate([cos, cos], axis=0)
    sp = np.concatenate([-sin, sin], axis=0)
    swap = np.zeros((128, 128), dtype=np.float32)
    for i in range(128):
        swap[i, (i + 64) % 128] = 1.0
    masks = np.zeros((4, 128, 512), dtype=np.float32)
    ql = np.arange(512)
    kl = np.arange(128)
    for off in range(4):
        masks[off] = (ql[None, :] >= off * 128 + kl[:, None]).astype(np.float32)
    bf = ml_dtypes.bfloat16
    return {
        "cp_c": cp.astype(bf), "sp_c": sp.astype(bf),
        "swap_c": swap.astype(bf),
        "mask_c": masks.astype(bf),
        "onec_c": np.ones((128, 1), dtype=np.float32).astype(bf),
        "oner_c": np.ones((1, 128), dtype=np.float32).astype(bf),
        "ident_c": np.eye(128, dtype=np.float32).astype(bf),
        "ident32_c": np.eye(128, dtype=np.float32),
    }


def build_program(S):
    assert S in (512, 1024, 2048)
    NTC = S // 512
    NQB = NTC
    NT128 = S // 128
    NQ = HQ * NQB
    import os
    NHALF = 2 if S >= 1024 else 1
    if os.environ.get("KNHALF"):
        NHALF = int(os.environ["KNHALF"])
        assert S // NHALF >= 512
    HALF_BAR = bool(os.environ.get("KHALFBAR"))
    TH = S // NHALF
    NTC_H = TH // 512

    nc = bass.Bass("TRN2", target_bir_lowering=False)

    xin = nc.dram_tensor("xin", [S, D], F32, kind="ExternalInput").ap()
    wq = nc.dram_tensor("wq", [D, D], F32, kind="ExternalInput").ap()
    wk = nc.dram_tensor("wk", [D, HKV * DH], F32, kind="ExternalInput").ap()
    wv = nc.dram_tensor("wv", [D, HKV * DH], F32, kind="ExternalInput").ap()
    wo = nc.dram_tensor("wo", [D, D], F32, kind="ExternalInput").ap()
    gq = nc.dram_tensor("gq", [I_FF, D], I32, kind="ExternalInput").ap()
    gs = nc.dram_tensor("gs", [I_FF, NF], F32, kind="ExternalInput").ap()
    gz = nc.dram_tensor("gz", [I_FF, NF], F32, kind="ExternalInput").ap()
    uq = nc.dram_tensor("uq", [I_FF, D], I32, kind="ExternalInput").ap()
    us = nc.dram_tensor("us", [I_FF, NF], F32, kind="ExternalInput").ap()
    uz = nc.dram_tensor("uz", [I_FF, NF], F32, kind="ExternalInput").ap()
    dq = nc.dram_tensor("dq", [D, I_FF], I32, kind="ExternalInput").ap()
    ds = nc.dram_tensor("ds", [D, NIB], F32, kind="ExternalInput").ap()
    dz = nc.dram_tensor("dz", [D, NIB], F32, kind="ExternalInput").ap()
    cp_c = nc.dram_tensor("cp_c", [128, S], BF16, kind="ExternalInput").ap()
    sp_c = nc.dram_tensor("sp_c", [128, S], BF16, kind="ExternalInput").ap()
    swap_c = nc.dram_tensor("swap_c", [128, 128], BF16, kind="ExternalInput").ap()
    mask_c = nc.dram_tensor("mask_c", [4, 128, 512], BF16, kind="ExternalInput").ap()
    onec_c = nc.dram_tensor("onec_c", [128, 1], BF16, kind="ExternalInput").ap()
    oner_c = nc.dram_tensor("oner_c", [1, 128], BF16, kind="ExternalInput").ap()
    ident_c = nc.dram_tensor("ident_c", [128, 128], BF16, kind="ExternalInput").ap()
    ident32_c = nc.dram_tensor("ident32_c", [128, 128], F32, kind="ExternalInput").ap()
    out = nc.dram_tensor("out", [S, D], F32, kind="ExternalOutput").ap()

    xbf_s = nc.dram_tensor("xbf_s", [S, D], BF16).ap()
    qs_s = nc.dram_tensor("qs_s", [D, S], BF16).ap()
    ks_s = nc.dram_tensor("ks_s", [HKV * DH, S], BF16).ap()
    as_s = nc.dram_tensor("as_s", [S, D], BF16).ap()
    o1_s = nc.dram_tensor("o1_s", [S, D], F32).ap()
    h_s = nc.dram_tensor("h_s", [S, D], F32).ap()
    hbf_s = nc.dram_tensor("hbf_s", [S, D], BF16).ap()
    wgdq_s = nc.dram_tensor("wgdq_s", [I_FF, D], BF16).ap()
    wudq_s = nc.dram_tensor("wudq_s", [I_FF, D], BF16).ap()
    hg_s = nc.dram_tensor("hg_s", [I_FF, S], BF16).ap()
    o2_s = nc.dram_tensor("o2_s", [S, D], F32).ap()

    b = _B(nc)
    ctx_stack = []

    def alloc(name, shape, dtype, psum=False):
        cm = nc.psum_tensor(name, shape, dtype) if psum else nc.sbuf_tensor(name, shape, dtype)
        h = cm.__enter__()
        ctx_stack.append((name, cm))
        return h

    def free(*names):
        names = set(names)
        while names:
            n, cm = ctx_stack[-1]
            assert n in names, f"non-LIFO free: top={n}, want={names}"
            cm.__exit__(None, None, None)
            ctx_stack.pop()
            names.discard(n)

    swap_sb = alloc("swap_sb", [128, 128], BF16)
    onec_sb = alloc("onec_sb", [128, 1], BF16)
    oner_sb = alloc("oner_sb", [1, 128], BF16)
    ident_sb = alloc("ident_sb", [128, 128], BF16)
    negc_sb = alloc("negc_sb", [128, NQ], F32)
    eps_sb = alloc("eps_sb", [128, 1], F32)
    v_sb = alloc("v_sb", [128, NT128 * HKV * 130], BF16)

    t_swap = b.dma("sp", lambda e: e.dma_start(swap_sb[:], swap_c[:]))
    t_onec = b.dma("sp", lambda e: e.dma_start(onec_sb[:], onec_c[:]))
    t_oner = b.dma("sp", lambda e: e.dma_start(oner_sb[:], oner_c[:]))
    t_ident = b.dma("sp", lambda e: e.dma_start(ident_sb[:], ident_c[:]))
    t_vones = b.op("dve", lambda e: e.memset(v_sb[:], 1.0))
    t_eps = b.op("dve", lambda e: e.memset(eps_sb[:], EPS))

    x_fm = alloc("x_fm", [128, NF * S], BF16)
    xrow_sb = [alloc("xrow0", [128, D], BF16)]
    ps_tr1 = [alloc(f"ps_tr1_{i}", [128, 128], BF16, psum=True) for i in range(3)]
    t_h2h = []
    rch = S // 4
    for j in range(4):
        t_h2h.append(b.dma("pool", lambda e, j=j: e.dma_start(
            xbf_s[j * rch:(j + 1) * rch, :], xin[j * rch:(j + 1) * rch, :])))
    t_xfm = {}
    tr01_ctr = [0]
    L0 = {}

    def tk0(key):
        v = L0.get(key)
        return [] if v is None else (v if isinstance(v, list) else [v])

    for qc in range(NT128):
        rb = 0
        t_xr = b.dma("sp", lambda e, qc=qc, rb=rb: e.dma_start(
            xrow_sb[rb][:], xbf_s[qc * 128:(qc + 1) * 128, :]),
            waits=[t_h2h[qc // 4]] + tk0(("xrowfree", rb)))
        last_tr = None
        for g in range(NF):
            pt = tr01_ctr[0] % 3
            tr01_ctr[0] += 1
            t_tr = b.op("pe", lambda e, rb=rb, g=g, pt=pt: e.transpose(
                ps_tr1[pt][:], xrow_sb[rb][:, g * 128:(g + 1) * 128], ident_sb[:]),
                waits=[t_xr, t_ident] + tk0(("tr1", pt)))
            last_tr = t_tr
            col = g * S + qc * 128
            if g % 2 == 0:
                t_ev = b.op("act", lambda e, col=col, pt=pt: e.copy(
                    x_fm[:, col:col + 128], ps_tr1[pt][:]), waits=[t_tr])
            else:
                t_ev = b.op("dve", lambda e, col=col, pt=pt: e.tensor_copy(
                    x_fm[:, col:col + 128], ps_tr1[pt][:]), waits=[t_tr])
            L0[("tr1", pt)] = t_ev
            t_xfm[(qc, g)] = t_ev
        L0[("xrowfree", rb)] = last_tr

    cp_sb = alloc("cp_sb", [128, S], BF16)
    sp_sb = alloc("sp_sb", [128, S], BF16)
    t_cp = b.dma("sp", lambda e: e.dma_start(cp_sb[:], cp_c[:]))
    t_sp = b.dma("sp", lambda e: e.dma_start(sp_sb[:], sp_c[:]))

    cb_row = alloc("cb_row", [1, NQ], F32)
    kn_row = alloc("kn_row", [1, HKV * NTC], F32)
    slab_sb = [alloc(f"slab{i}", [128, NF * 128], BF16) for i in range(2)]
    qraw_sb = [alloc(f"qraw{i}", [128, 512], BF16) for i in range(2)]
    m1_sb = [alloc(f"m1_{i}", [128, 512], BF16) for i in range(2)]
    m2_sb = [alloc(f"m2_{i}", [128, 512], BF16) for i in range(2)]
    qfm_sb = [alloc(f"qfm{i}", [128, 512], BF16) for i in range(2)]
    qsq_sb = [alloc("qsq0", [128, 512], BF16)]
    vfm_sb = [alloc(f"vfm{i}", [128, 512], BF16) for i in range(2)]
    kmax2 = alloc("kmax2", [1, HKV], F32)
    cbs_sb = alloc("cbs_sb", [1, NQ], F32)
    negc_bf = alloc("negc_bf", [1, NQ], BF16)
    ps_p = [alloc(f"ps_p{i}", [128, 512], F32, psum=True) for i in range(2)]
    ps_sw = [alloc(f"ps_sw{i}", [128, 512], F32, psum=True) for i in range(2)]
    ps_n = alloc("ps_n", [1, 512], F32, psum=True)

    last = {}

    def tk(key):
        v = last.get(key)
        return [] if v is None else (v if isinstance(v, list) else [v])

    n_ob = HQ + HKV + HKV

    def slab_src(ob):
        if ob < HQ:
            return wq, ob, D
        if ob < HQ + HKV:
            return wk, ob - HQ, HKV * DH
        return wv, ob - HQ - HKV, HKV * DH

    def emit_slab_load(ob, buf):
        w, col, rw = slab_src(ob)
        toks = []
        for j in range(4):
            def fn(e, w=w, col=col, j=j, buf=buf, rw=rw):
                src = bass.AP(w.tensor, j * 8 * 128 * rw + col * 128,
                              [[rw, 128], [128 * rw, 8], [1, 128]])
                dst = slab_sb[buf][:, j * 1024:(j + 1) * 1024].rearrange(
                    "p (c d) -> p c d", d=128)
                return e.dma_start(dst, src)
            toks.append(b.dma("pool", fn, waits=tk(("slabfree", buf))))
        return toks

    slab_tok = [None, None]
    slab_tok[0] = emit_slab_load(0, 0)
    npp = 0

    pend1 = []

    def make_rope_tail(ob, tc, rb, is_q, g_kv, t_ev):
        def clo():
            psw = (ob * NTC + tc) % 2
            t_sw = b.op("pe", lambda e, rb=rb, psw=psw: e.matmul(
                ps_sw[psw][:], swap_sb[:], qraw_sb[rb][:], start=True, stop=True),
                waits=[t_ev, t_swap] + tk(("ps_sw", psw)))
            t_m1 = b.op("pool", lambda e, rb=rb, tc=tc: e.tensor_mul(
                m1_sb[rb][:], qraw_sb[rb][:], cp_sb[:, tc * 512:tc * 512 + 512]),
                waits=[t_ev, t_cp] + tk(("m1", rb)))
            last[("qraw", rb)] = [t_sw, t_m1]
            t_m2 = b.op("dve", lambda e, rb=rb, psw=psw, tc=tc: e.tensor_mul(
                m2_sb[rb][:], ps_sw[psw][:], sp_sb[:, tc * 512:tc * 512 + 512]),
                waits=[t_sw, t_sp] + tk(("m2", rb)))
            last[("ps_sw", psw)] = t_m2
            t_fm = b.op("dve", lambda e, rb=rb: e.tensor_add(
                qfm_sb[rb][:], m1_sb[rb][:], m2_sb[rb][:]),
                waits=[t_m1, t_m2] + tk(("qfm", rb)))
            last[("m1", rb)] = t_fm
            last[("m2", rb)] = t_fm
            if is_q:
                t_st = b.dma("sp", lambda e, ob=ob, tc=tc, rb=rb: e.dma_start(
                    qs_s[ob * 128:(ob + 1) * 128, tc * 512:tc * 512 + 512],
                    qfm_sb[rb][:]), waits=[t_fm])
            else:
                t_st = b.dma("sp", lambda e, g_kv=g_kv, tc=tc, rb=rb: e.dma_start(
                    ks_s[g_kv * 128:(g_kv + 1) * 128, tc * 512:tc * 512 + 512],
                    qfm_sb[rb][:]), waits=[t_fm])
            t_sq = b.op("act", lambda e, rb=rb: e.square(qsq_sb[0][:], qfm_sb[rb][:]),
                        waits=[t_fm] + tk(("qsq",)))
            last[("qfm", rb)] = [t_st, t_sq]
            t_nm = b.op("pe", lambda e: e.matmul(
                ps_n[:], onec_sb[:], qsq_sb[0][:], start=True, stop=True),
                waits=[t_sq, t_onec] + tk(("ps_n",)))
            last[("qsq",)] = t_nm
            if is_q:
                idx = ob * NQB + tc
                t_rd = b.op("dve", lambda e, idx=idx: e.reduce_max(
                    cb_row[0:1, idx:idx + 1], ps_n[0:1, :], axis=AX.X),
                    waits=[t_nm])
            else:
                idx = g_kv * NTC + tc
                t_rd = b.op("dve", lambda e, idx=idx: e.reduce_max(
                    kn_row[0:1, idx:idx + 1], ps_n[0:1, :], axis=AX.X),
                    waits=[t_nm])
            last[("ps_n",)] = t_rd
        return clo

    def make_v_tail(tc, rb, g_kv, t_ev):
        def clo():
            t_ve = None
            for j in range(4):
                pt = tr01_ctr[0] % 3
                tr01_ctr[0] += 1
                t_tr = b.op("pe", lambda e, rb=rb, j=j, pt=pt: e.transpose(
                    ps_tr1[pt][:], vfm_sb[rb][:, j * 128:(j + 1) * 128], ident_sb[:]),
                    waits=[t_ev, t_ident] + tk0(("tr1", pt)))
                col = (tc * 4 + j) * HKV * 130 + g_kv * 130
                t_ve = b.op("dve", lambda e, pt=pt, col=col: e.tensor_copy(
                    v_sb[:, col:col + 128], ps_tr1[pt][:]),
                    waits=[t_tr, t_vones])
                L0[("tr1", pt)] = t_ve
            last[("vfm", rb)] = t_ve
        return clo

    for ob in range(n_ob):
        buf = ob % 2
        if ob + 1 < n_ob:
            slab_tok[(ob + 1) % 2] = emit_slab_load(ob + 1, (ob + 1) % 2)
        is_q = ob < HQ
        is_v = ob >= HQ + HKV
        g_kv = None if is_q else (ob - HQ if not is_v else ob - HQ - HKV)
        for tc in range(NTC):
            pp = npp % 2
            npp += 1
            mm_tok = None
            for fc in range(NF):
                mm_tok = b.op("pe", lambda e, fc=fc, tc=tc, pp=pp, buf=buf: e.matmul(
                    ps_p[pp][:],
                    slab_sb[buf][:, fc * 128:(fc + 1) * 128],
                    x_fm[:, fc * S + tc * 512: fc * S + tc * 512 + 512],
                    start=(fc == 0), stop=(fc == NF - 1)),
                    waits=slab_tok[buf] +
                          [t_xfm[(qq, fc)] for qq in range(4 * tc, 4 * tc + 4)] +
                          tk(("ps_p", pp)),
                    inc=(fc == NF - 1))
            if tc == NTC - 1:
                last[("slabfree", buf)] = mm_tok
            rb = (ob * NTC + tc) % 2
            if is_v:
                t_ev = b.op("act", lambda e, pp=pp, rb=rb: e.copy(
                    vfm_sb[rb][:], ps_p[pp][:]),
                    waits=[mm_tok] + tk(("vfm", rb)))
                last[("ps_p", pp)] = t_ev
                pend1.append(make_v_tail(tc, rb, g_kv, t_ev))
            else:
                t_ev = b.op("act", lambda e, pp=pp, rb=rb: e.copy(
                    qraw_sb[rb][:], ps_p[pp][:]),
                    waits=[mm_tok] + tk(("qraw", rb)))
                last[("ps_p", pp)] = t_ev
                pend1.append(make_rope_tail(ob, tc, rb, is_q, g_kv, t_ev))
            while len(pend1) > 1:
                pend1.pop(0)()
    while pend1:
        pend1.pop(0)()

    kn_3d = kn_row[0:1, :].rearrange("p (g t) -> p g t", g=HKV)
    km_3d = kmax2[0:1, :].rearrange("p (g o) -> p g o", o=1)
    t1 = b.op("dve", lambda e: e.tensor_reduce(
        km_3d, kn_3d, axis=AX.X, op=ALU.max),
        waits=[last[("ps_n",)]])
    cb_3d = cb_row[0:1, :].rearrange("p (g q) -> p g q", g=HKV)
    cb_b, km_b = bass.broadcast_tensor_aps(cb_3d, km_3d)
    t2 = b.op("dve", lambda e: e.tensor_tensor(cb_b, cb_b, km_b, op=ALU.mult),
              waits=[t1])
    t3 = b.op("act", lambda e: e.activation(
        cbs_sb[0:1, :], cb_row[0:1, :], AF.Sqrt, scale=1.0 / 128.0), waits=[t2])
    t4 = b.op("dve", lambda e: e.tensor_scalar(
        negc_bf[0:1, :], cbs_sb[0:1, :], -1.0, None, op0=ALU.mult), waits=[t3])
    t5 = b.op("pe", lambda e: e.matmul(
        ps_sw[0][:, 0:NQ], oner_sb[:], negc_bf[0:1, :], start=True, stop=True),
        waits=[t4, t_oner] + tk(("ps_sw", 0)))
    t_negc = b.op("dve", lambda e: e.tensor_copy(negc_sb[:, 0:NQ], ps_sw[0][:, 0:NQ]),
                  waits=[t5])

    free("ps_n", "ps_sw1", "ps_sw0", "ps_p1", "ps_p0")
    free("negc_bf", "cbs_sb", "kmax2", "vfm1", "vfm0", "qsq0", "qfm1", "qfm0",
         "m2_1", "m2_0", "m1_1", "m1_0", "qraw1", "qraw0", "slab1", "slab0",
         "kn_row", "cb_row", "sp_sb", "cp_sb")
    free("ps_tr1_2", "ps_tr1_1", "ps_tr1_0")
    free("xrow0")
    free("x_fm")
    b.bar()

    mask_sb = alloc("mask_sb", [128, 4 * 512], BF16)
    kfm_sb = alloc("kfm_sb", [128, HKV * S], BF16)
    qh_sb = [alloc(f"qh{i}", [128, S], BF16) for i in range(2)]
    pkm_sb = [alloc(f"pkm{i}", [128, NT128 * 512], BF16) for i in range(2)]
    aacc_sb = alloc("aacc_sb", [128, NT128 * 512], BF16)
    r_sb = alloc("r_sb", [128, 1], F32)
    ps_s = [alloc(f"ps_s{i}", [128, 512], F32, psum=True) for i in range(4)]
    ps_av = [alloc(f"ps_av{i}", [128, 132], F32, psum=True) for i in range(2)]

    _mask_src = bass.AP(mask_c.tensor, 0, [[512, 128], [128 * 512, 4], [1, 512]])
    _mask_dst = mask_sb[:].rearrange("p (o q) -> p o q", q=512)
    t_mask = b.dma("sp", lambda e: e.dma_start(_mask_dst, _mask_src))
    t_k = {}
    for g in range(HKV):
        t_k[g] = b.dma("sp", lambda e, g=g: e.dma_start(
            kfm_sb[:, g * S:(g + 1) * S], ks_s[g * 128:(g + 1) * 128, :]))

    L2 = {}

    def tk2(key):
        v = L2.get(key)
        return [] if v is None else (v if isinstance(v, list) else [v])

    def q_load(h, buf):
        return b.dma("sp", lambda e, h=h, buf=buf: e.dma_start(
            qh_sb[buf][:], qs_s[h * 128:(h + 1) * 128, :]),
            waits=tk2(("qhfree", buf)))

    L2[("qld", 0)] = q_load(0, 0)
    prod_pkm = {}
    avq = []
    av_ctr = [0]
    sc_ctr = [0]

    def pop_av(n):
        for _ in range(n):
            if avq:
                avq.pop(0)[1]()

    def drain_av_through(seq):
        while avq and avq[0][0] <= seq:
            avq.pop(0)[1]()

    def make_av_group(h, qb, pb, g, hh, kb_hi, q4, state):
        qsub = qb * 4 + q4

        def clo():
            ab = av_ctr[0] % 2
            av_ctr[0] += 1
            t_av = None
            for kb in range(qsub + 1):
                t_av = b.op("pe", lambda e, pb=pb, kb=kb, q4=q4, g=g, qsub=qsub, ab=ab: e.matmul(
                    ps_av[ab][:, 0:129],
                    pkm_sb[pb][:, kb * 512 + q4 * 128: kb * 512 + q4 * 128 + 128],
                    v_sb[:, kb * HKV * 130 + g * 130: kb * HKV * 130 + g * 130 + 129],
                    start=(kb == 0), stop=(kb == qsub)),
                    waits=[prod_pkm[(pb, kb)]] + tk2(("ps_av", ab)),
                    inc=(kb == qsub))
            state["last"] = t_av
            t_r = b.op("dve", lambda e, ab=ab: e.reciprocal(
                r_sb[:], ps_av[ab][:, 128:129]), waits=[t_av] + tk2(("r",)))
            t_an = b.op("dve", lambda e, ab=ab, qsub=qsub, hh=hh: e.tensor_scalar(
                aacc_sb[:, qsub * 512 + hh * 128: qsub * 512 + hh * 128 + 128],
                ps_av[ab][:, 0:128], r_sb[:], None, op0=ALU.mult),
                waits=[t_r] + tk2(("aaccfree", qsub)))
            L2[("r",)] = t_an
            L2[("ps_av", ab)] = t_an
            L2[("aaccw", qsub, hh)] = t_an
            if q4 == 3:
                for kb in range(kb_hi):
                    L2[("pkmfree", pb, kb)] = state["last"]
        return clo

    def make_store(g):
        def clo():
            for qsub in range(NT128):
                w = [L2[("aaccw", qsub, j)] for j in range(4)]
                t_sta = b.dma("sp", lambda e, qsub=qsub, g=g: e.dma_start(
                    as_s[qsub * 128:(qsub + 1) * 128, g * 512:(g + 1) * 512],
                    aacc_sb[:, qsub * 512:(qsub + 1) * 512]), waits=w)
                L2[("aaccfree", qsub)] = t_sta
        return clo

    for h in range(HQ):
        g = h // 4
        hh = h % 4
        hb = h % 2
        if h + 1 < HQ:
            L2[("qld", (h + 1) % 2)] = q_load(h + 1, (h + 1) % 2)
        last_sc_mm = None
        for qb in range(NQB):
            seq = h * NQB + qb
            pb = seq % 2
            kb_hi = 4 * qb + 4
            drain_av_through(seq - 2)
            for kb in range(kb_hi):
                sb_ = sc_ctr[0] % 4
                sc_ctr[0] += 1
                t_mm = b.op("pe", lambda e, g=g, kb=kb, qb=qb, hb=hb, sb_=sb_: e.matmul(
                    ps_s[sb_][:],
                    kfm_sb[:, g * S + kb * 128: g * S + kb * 128 + 128],
                    qh_sb[hb][:, qb * 512: qb * 512 + 512],
                    start=True, stop=True),
                    waits=[t_k[g]] + tk2(("qld", hb)) + tk2(("ps_s", sb_)))
                last_sc_mm = t_mm
                t_ex = b.op("act", lambda e, sb_=sb_, pb=pb, kb=kb, h=h, qb=qb: e.activation(
                    pkm_sb[pb][:, kb * 512: kb * 512 + 512], ps_s[sb_][:],
                    AF.Exp, bias=negc_sb[:, h * NQB + qb: h * NQB + qb + 1],
                    scale=RSQ),
                    waits=[t_mm, t_negc] + tk2(("pkmfree", pb, kb)))
                L2[("ps_s", sb_)] = t_ex
                if kb >= 4 * qb:
                    off = kb - 4 * qb
                    t_mk = b.op("dve", lambda e, pb=pb, kb=kb, off=off: e.tensor_mul(
                        pkm_sb[pb][:, kb * 512: kb * 512 + 512],
                        pkm_sb[pb][:, kb * 512: kb * 512 + 512],
                        mask_sb[:, off * 512:(off + 1) * 512]),
                        waits=[t_ex, t_mask])
                    prod_pkm[(pb, kb)] = t_mk
                else:
                    prod_pkm[(pb, kb)] = t_ex
                if kb % 2 == 1:
                    pop_av(1)
            state = {"last": None}
            for q4 in range(4):
                avq.append((seq, make_av_group(h, qb, pb, g, hh, kb_hi, q4, state)))
            if hh == 3 and qb == NQB - 1:
                avq.append((seq, make_store(g)))
        L2[("qhfree", hb)] = last_sc_mm
    pop_av(len(avq))

    free("ps_av1", "ps_av0", "ps_s3", "ps_s2", "ps_s1", "ps_s0")
    free("r_sb", "aacc_sb", "pkm1", "pkm0", "qh1", "qh0", "kfm_sb", "mask_sb")
    free("v_sb")
    b.bar()

    afm_sb = alloc("afm_sb", [128, NF * S], BF16)
    arow_sb = [alloc("arow0", [128, D], BF16)]
    wo_sb = [alloc(f"wo{i}", [128, NF * 512], BF16) for i in range(2)]
    o1t_sb = [alloc(f"o1t{i}", [128, 512], F32) for i in range(2)]
    ps_o = [alloc(f"ps_o{i}", [128, 512], F32, psum=True) for i in range(2)]
    ps_tr3 = [alloc(f"ps_tr3_{i}", [128, 128], BF16, psum=True) for i in range(3)]

    L3 = {}

    def tk3(key):
        v = L3.get(key)
        return [] if v is None else (v if isinstance(v, list) else [v])

    t_afm = {}
    tr3_ctr = [0]
    for qc in range(NT128):
        rb = 0
        t_ar = b.dma("sp", lambda e, qc=qc, rb=rb: e.dma_start(
            arow_sb[rb][:], as_s[qc * 128:(qc + 1) * 128, :]),
            waits=tk3(("arowfree", rb)))
        last_tr = None
        for cc in range(NF):
            pt = tr3_ctr[0] % 3
            tr3_ctr[0] += 1
            t_tr = b.op("pe", lambda e, rb=rb, cc=cc, pt=pt: e.transpose(
                ps_tr3[pt][:], arow_sb[rb][:, cc * 128:(cc + 1) * 128], ident_sb[:]),
                waits=[t_ar, t_ident] + tk3(("tr3", pt)))
            last_tr = t_tr
            col = cc * S + qc * 128
            if cc % 2 == 0:
                t_ev = b.op("act", lambda e, col=col, pt=pt: e.copy(
                    afm_sb[:, col:col + 128], ps_tr3[pt][:]), waits=[t_tr])
            else:
                t_ev = b.op("dve", lambda e, col=col, pt=pt: e.tensor_copy(
                    afm_sb[:, col:col + 128], ps_tr3[pt][:]), waits=[t_tr])
            L3[("tr3", pt)] = t_ev
            t_afm[(qc, cc)] = t_ev
        L3[("arowfree", rb)] = last_tr

    def wo_load(fc, buf):
        toks = []
        for j in range(4):
            def fn(e, fc=fc, j=j, buf=buf):
                src = bass.AP(wo.tensor, j * 8 * 128 * D + fc * 512,
                              [[D, 128], [128 * D, 8], [1, 512]])
                dst = wo_sb[buf][:, j * 8 * 512:(j + 1) * 8 * 512].rearrange(
                    "p (c f) -> p c f", f=512)
                return e.dma_start(dst, src)
            toks.append(b.dma("pool", fn, waits=tk3(("wofree", buf))))
        return toks

    wo_tok = [None, None]
    wo_tok[0] = wo_load(0, 0)
    for fc in range(8):
        buf = fc % 2
        if fc + 1 < 8:
            wo_tok[(fc + 1) % 2] = wo_load(fc + 1, (fc + 1) % 2)
        for qc in range(NT128):
            po = qc % 2
            t_mm = None
            for cc in range(NF):
                t_mm = b.op("pe", lambda e, cc=cc, qc=qc, po=po, buf=buf: e.matmul(
                    ps_o[po][:],
                    afm_sb[:, cc * S + qc * 128: cc * S + qc * 128 + 128],
                    wo_sb[buf][:, cc * 512:(cc + 1) * 512],
                    start=(cc == 0), stop=(cc == NF - 1)),
                    waits=wo_tok[buf] + [t_afm[(qc, cc)]] + tk3(("ps_o", po)),
                    inc=(cc == NF - 1))
            if qc == NT128 - 1:
                L3[("wofree", buf)] = t_mm
            t_ev = b.op("act", lambda e, po=po: e.copy(
                o1t_sb[po][:], ps_o[po][:]),
                waits=[t_mm] + tk3(("o1t", po)))
            L3[("ps_o", po)] = t_ev
            t_st = b.dma("sp", lambda e, qc=qc, fc=fc, po=po: e.dma_start(
                o1_s[qc * 128:(qc + 1) * 128, fc * 512:(fc + 1) * 512],
                o1t_sb[po][:]), waits=[t_ev])
            L3[("o1t", po)] = t_st

    free("ps_tr3_2", "ps_tr3_1", "ps_tr3_0", "ps_o1", "ps_o0")
    free("o1t1", "o1t0", "wo1", "wo0", "arow0", "afm_sb")
    b.bar()

    hfm_sb = alloc("hfm_sb", [128, NF * S], BF16)
    o1r_sb = [alloc(f"o1r{i}", [128, D], F32) for i in range(2)]
    xr_sb = [alloc(f"xr{i}", [128, D], F32) for i in range(2)]
    hbf_t = alloc("hbft0", [128, D], BF16)
    bn_sb = [alloc(f"bn_sb{i}", [128, 48], F32) for i in range(2)]
    mv_sb = [alloc(f"mv_sb{i}", [128, 2], F32) for i in range(2)]
    sd_sb = [alloc(f"sd_sb{i}", [128, 1], F32) for i in range(2)]
    rs_sb = [alloc(f"rs_sb{i}", [128, 1], F32) for i in range(2)]
    ps_h4 = [alloc(f"ps_h4_{i}", [128, 128], BF16, psum=True) for i in range(3)]

    L4 = {}

    def tk4(key):
        v = L4.get(key)
        return [] if v is None else (v if isinstance(v, list) else [v])

    for qc in range(NT128):
        rb = qc % 2
        t_lo = b.dma("sp", lambda e, qc=qc, rb=rb: e.dma_start(
            o1r_sb[rb][:], o1_s[qc * 128:(qc + 1) * 128, :]),
            waits=tk4(("o1r", rb)))
        t_lx = b.dma("pool", lambda e, qc=qc, rb=rb: e.dma_start(
            xr_sb[rb][:], xin[qc * 128:(qc + 1) * 128, :]),
            waits=tk4(("xr", rb)))
        ad_toks = []
        t_bn = None
        for j in range(8):
            t_ad = b.op("dve", lambda e, rb=rb, j=j: e.tensor_add(
                o1r_sb[rb][:, j * 512:(j + 1) * 512],
                o1r_sb[rb][:, j * 512:(j + 1) * 512],
                xr_sb[rb][:, j * 512:(j + 1) * 512]),
                waits=[t_lo, t_lx])
            ad_toks.append(t_ad)
            t_bn = b.op("dve", lambda e, rb=rb, j=j: e.bn_stats(
                bn_sb[rb][:, j * 6:(j + 1) * 6],
                o1r_sb[rb][:, j * 512:(j + 1) * 512]),
                waits=[t_ad] + (tk4(("bn", rb)) if j == 0 else []))
        L4[("xr", rb)] = ad_toks
        t_ag = b.op("dve", lambda e, rb=rb: e.bn_aggr(mv_sb[rb][:], bn_sb[rb][:]),
                    waits=[t_bn])
        L4[("bn", rb)] = t_ag
        t_sd = b.op("act", lambda e, rb=rb: e.activation(
            sd_sb[rb][:], mv_sb[rb][:, 1:2], AF.Sqrt, bias=eps_sb[:]),
            waits=[t_ag, t_eps] + tk4(("sd", rb)))
        t_rs = b.op("dve", lambda e, rb=rb: e.reciprocal(rs_sb[rb][:], sd_sb[rb][:]),
                    waits=[t_sd] + tk4(("rs", rb)))
        L4[("sd", rb)] = t_rs
        t_h = b.op("dve", lambda e, rb=rb: e.tensor_scalar(
            o1r_sb[rb][:], o1r_sb[rb][:], mv_sb[rb][:, 0:1], rs_sb[rb][:],
            op0=ALU.subtract, op1=ALU.mult),
            waits=[t_rs])
        L4[("rs", rb)] = t_h
        t_cb = b.op("act", lambda e, rb=rb: e.copy(hbf_t[:], o1r_sb[rb][:]),
                    waits=[t_h] + tk4(("hbft",)))
        t_s1 = b.dma("sp", lambda e, qc=qc, rb=rb: e.dma_start(
            h_s[qc * 128:(qc + 1) * 128, :], o1r_sb[rb][:]), waits=[t_h])
        L4[("o1r", rb)] = [t_cb, t_s1]
        t_tr_last = None
        for g in range(NF):
            pt = (qc * NF + g) % 3
            t_tr = b.op("pe", lambda e, g=g, pt=pt: e.transpose(
                ps_h4[pt][:], hbf_t[:, g * 128:(g + 1) * 128], ident_sb[:]),
                waits=[t_cb, t_ident] + tk4(("h4ps", pt)))
            t_tr_last = t_tr
            col = g * S + qc * 128
            if g % 2 == 0:
                t_ev = b.op("act", lambda e, col=col, pt=pt: e.copy(
                    hfm_sb[:, col:col + 128], ps_h4[pt][:]), waits=[t_tr])
            else:
                t_ev = b.op("dve", lambda e, col=col, pt=pt: e.tensor_copy(
                    hfm_sb[:, col:col + 128], ps_h4[pt][:]), waits=[t_tr])
            L4[("h4ps", pt)] = t_ev
        L4[("hbft",)] = t_tr_last

    free("ps_h4_2", "ps_h4_1", "ps_h4_0")
    free("rs_sb1", "rs_sb0", "sd_sb1", "sd_sb0", "mv_sb1", "mv_sb0",
         "bn_sb1", "bn_sb0", "hbft0", "xr1", "xr0", "o1r1", "o1r0")
    b.bar()

    grawq_sb = [alloc(f"grawq{i}", [128, D], BF16) for i in range(3)]
    sc5_sb = [[alloc(f"sc5_{m}_{i}", [128, NF], F32) for i in range(2)]
              for m in range(2)]
    zc5_sb = [[alloc(f"zc5_{m}_{i}", [128, NF], F32) for i in range(2)]
              for m in range(2)]
    zs5_sb = [[alloc(f"zs5_{m}_{i}", [128, NF], F32) for i in range(2)]
              for m in range(2)]
    wn5_sb = alloc("wn5_sb", [128, NF * 128], BF16)
    wslab5_sb = [alloc(f"wslab5_{m}", [128, NF * 128], BF16) for m in range(2)]
    gtile5_sb = alloc("gtile5_sb", [128, S], BF16)
    hgt5_sb = [alloc(f"hgt5_{i}", [128, S], BF16) for i in range(2)]
    ps_g5 = [alloc(f"ps_g5_{i}", [128, 512], F32, psum=True) for i in range(2)]
    ps_u5 = [alloc(f"ps_u5_{i}", [128, 512], F32, psum=True) for i in range(2)]
    ps_t5 = [alloc(f"ps_t5_{i}", [128, 128], BF16, psum=True) for i in range(4)]

    L5 = {}

    def tk5(key):
        v = L5.get(key)
        return [] if v is None else (v if isinstance(v, list) else [v])

    MATS5 = [("g", gq, gs, gz), ("u", uq, us, uz)]
    raw5_ctr = [0]
    tr5_ctr = [0]

    def emit_gu_scales(ib, m):
        ip = ib % 2
        _, qw, sw, zw = MATS5[m]
        t_s = b.dma("sp", lambda e, sw=sw, ib=ib, m=m, ip=ip: e.dma_start(
            sc5_sb[m][ip][:], sw[ib * 128:(ib + 1) * 128, :]),
            waits=tk5(("scuse5", m, ip)))
        t_z = b.dma("sp", lambda e, zw=zw, ib=ib, m=m, ip=ip: e.dma_start(
            zc5_sb[m][ip][:], zw[ib * 128:(ib + 1) * 128, :]),
            waits=tk5(("scuse5", m, ip)))
        t_zs = b.op("dve", lambda e, m=m, ip=ip: e.tensor_mul(
            zs5_sb[m][ip][:], zc5_sb[m][ip][:], sc5_sb[m][ip][:]),
            waits=[t_s, t_z] + tk5(("scuse5", m, ip)))
        L5[("scld5", m, ip)] = [t_s, t_zs]

    def emit_gu_raw(ib, m):
        slot = raw5_ctr[0] % 3
        raw5_ctr[0] += 1
        _, qw, _, _ = MATS5[m]
        toks = []
        for q4 in range(4):
            toks.append(b.dma("pool", lambda e, qw=qw, ib=ib, slot=slot, q4=q4: e.dma_start(
                grawq_sb[slot][:, q4 * 1024:(q4 + 1) * 1024],
                qw[ib * 128:(ib + 1) * 128, q4 * 1024:(q4 + 1) * 1024]),
                waits=tk5(("rawfree5", slot))))
        return slot, toks

    def production5(ib, m, rawslot, rawtoks):
        ip = ib % 2
        sc_last = None
        for g in range(NF):
            pslot = tr5_ctr[0] % 4
            tr5_ctr[0] += 1
            t_sc = b.op("dve", lambda e, m=m, ip=ip, g=g, rawslot=rawslot: e.tensor_scalar(
                wn5_sb[:, g * 128:(g + 1) * 128],
                grawq_sb[rawslot][:, g * 128:(g + 1) * 128],
                sc5_sb[m][ip][:, g:g + 1], zs5_sb[m][ip][:, g:g + 1],
                op0=ALU.mult, op1=ALU.subtract),
                waits=[rawtoks[g // 8]] + tk5(("scld5", m, ip)) + tk5(("wn5", g)))
            sc_last = t_sc
            L5[("scuse5", m, ip)] = t_sc
            t_tr = b.op("pe", lambda e, g=g, pslot=pslot: e.transpose(
                ps_t5[pslot][:],
                wn5_sb[:, g * 128:(g + 1) * 128], ident_sb[:]),
                waits=[t_sc, t_ident] + tk5(("t5ps", pslot)))
            L5[("wn5", g)] = t_tr
            if g % 2 == 0:
                t_ev = b.op("act", lambda e, m=m, g=g, pslot=pslot: e.copy(
                    wslab5_sb[m][:, g * 128:(g + 1) * 128], ps_t5[pslot][:]),
                    waits=[t_tr] + tk5(("mmlast5", m)))
            else:
                t_ev = b.op("dve", lambda e, m=m, g=g, pslot=pslot: e.tensor_copy(
                    wslab5_sb[m][:, g * 128:(g + 1) * 128], ps_t5[pslot][:]),
                    waits=[t_tr] + tk5(("mmlast5", m)))
            L5[("t5ps", pslot)] = t_ev
            L5[("slabrdy5", m, g)] = t_ev
            yield
        L5[("rawfree5", rawslot)] = sc_last

    def emit_gu_matmuls(ib, m, prod):
        psb = ps_g5 if m == 0 else ps_u5
        hb5 = (ib % 2)
        step = 0
        for tc in range(NTC):
            pp = tc % 2
            t_mm = None
            for g in range(NF):
                t_mm = b.op("pe", lambda e, m=m, g=g, tc=tc, pp=pp, psb=psb: e.matmul(
                    psb[pp][:],
                    wslab5_sb[m][:, g * 128:(g + 1) * 128],
                    hfm_sb[:, g * S + tc * 512: g * S + tc * 512 + 512],
                    start=(g == 0), stop=(g == NF - 1)),
                    waits=tk5(("slabrdy5", m, g)) +
                          (tk5(("psfree5", m, pp)) if g == 0 else []),
                    inc=(g == NF - 1))
                step += 1
                if prod is not None and step % 4 == 0:
                    next(prod, None)
            L5[("mmlast5", m)] = t_mm
            if m == 0:
                t_si = b.op("act", lambda e, tc=tc, pp=pp: e.activation(
                    gtile5_sb[:, tc * 512:(tc + 1) * 512], ps_g5[pp][:], AF.Silu),
                    waits=[t_mm] + tk5(("gtfree", tc)))
                L5[("psfree5", 0, pp)] = t_si
                L5[("gtw", tc)] = t_si
            else:
                t_mul = b.op("dve", lambda e, tc=tc, pp=pp, hb5=hb5: e.tensor_mul(
                    hgt5_sb[hb5][:, tc * 512:(tc + 1) * 512],
                    ps_u5[pp][:], gtile5_sb[:, tc * 512:(tc + 1) * 512]),
                    waits=[t_mm] + tk5(("gtw", tc)) + tk5(("hgt5", hb5)))
                L5[("psfree5", 1, pp)] = t_mul
                L5[("gtfree", tc)] = t_mul
                L5[("mulw", tc)] = t_mul
        if m == 1:
            t_hg = b.dma("sp", lambda e, ib=ib, hb5=hb5: e.dma_start(
                hg_s[ib * 128:(ib + 1) * 128, :], hgt5_sb[hb5][:]),
                waits=[L5[("mulw", tc2)] for tc2 in range(NTC)])
            L5[("hgt5", hb5)] = t_hg

    order5 = [(ib, m) for ib in range(NIB) for m in range(2)]
    emit_gu_scales(0, 0)
    s0, t0 = emit_gu_raw(0, 0)
    for _ in production5(0, 0, s0, t0):
        pass
    pend_raw5 = {}
    if len(order5) > 1:
        emit_gu_scales(*order5[1])
        pend_raw5[1] = emit_gu_raw(*order5[1])
    for idx, (ib, m) in enumerate(order5):
        if idx + 2 < len(order5):
            emit_gu_scales(*order5[idx + 2])
            pend_raw5[idx + 2] = emit_gu_raw(*order5[idx + 2])
        if idx + 1 < len(order5):
            s2, t2 = pend_raw5.pop(idx + 1)
            nxt = production5(order5[idx + 1][0], order5[idx + 1][1], s2, t2)
        else:
            nxt = None
        emit_gu_matmuls(ib, m, nxt)
        if nxt is not None:
            for _ in nxt:
                pass

    free("ps_t5_3", "ps_t5_2", "ps_t5_1", "ps_t5_0", "ps_u5_1", "ps_u5_0",
         "ps_g5_1", "ps_g5_0")
    free("hgt5_1", "hgt5_0", "gtile5_sb", "wslab5_1", "wslab5_0", "wn5_sb")
    free(*[f"zs5_{m}_{i}" for m in (1, 0) for i in (1, 0)])
    free(*[f"zc5_{m}_{i}" for m in (1, 0) for i in (1, 0)])
    free(*[f"sc5_{m}_{i}" for m in (1, 0) for i in (1, 0)])
    free("grawq2", "grawq1", "grawq0")
    free("hfm_sb")
    b.bar()

    IC_CH = [22, 22, 21, 21]
    IC_OFF = [0, 22, 44, 65]
    MAXC = 22
    NHG = 8

    ident32_sb = alloc("ident32_sb", [128, 128], F32)
    rawd_sb = [[alloc(f"rawd{p}_{j}", [128, MAXC * 128], BF16) for j in range(4)]
               for p in range(2)]
    dsc6_sb = [[alloc(f"dsc6_{p}_{j}", [128, NIB], F32) for j in range(4)]
               for p in range(2)]
    dzc6_sb = [[alloc(f"dzc6_{p}_{j}", [128, NIB], F32) for j in range(4)]
               for p in range(2)]
    dzs6_sb = [[alloc(f"dzs6_{p}_{j}", [128, NIB], F32) for j in range(4)]
               for p in range(2)]
    wnat6_sb = alloc("wnat6_sb", [128, 8 * 128], BF16)
    wdts_sb = [alloc(f"wdts{p}", [128, MAXC * 512], BF16) for p in range(2)]
    NHGT = 8
    hgt6_sb = [alloc(f"hgt6_{p}", [128, 512], BF16) for p in range(NHGT)]
    o2acc_sb = [alloc(f"o2acc_{j}", [128, NTC * 512], F32) for j in range(4)]
    o2a_sb = [alloc(f"o2a_{p}", [128, 2048], F32) for p in range(2)]
    ps_acc = [alloc(f"ps_acc{j}", [128, 512], F32, psum=True) for j in range(4)]
    ps_wtr = [alloc(f"ps_wtr{i}", [128, 128], BF16, psum=True) for i in range(4)]

    t_id32 = b.dma("sp", lambda e: e.dma_start(ident32_sb[:], ident32_c[:]))

    L6 = {}

    def tk6(key):
        v = L6.get(key)
        return [] if v is None else (v if isinstance(v, list) else [v])

    wtr_ctr = [0]
    hgl_ctr = [0]

    def emit_hg_scales(hg):
        hp = hg % 2
        for j in range(4):
            r0 = (hg * 4 + j) * 128
            t_s = b.dma("act", lambda e, r0=r0, hp=hp, j=j: e.dma_start(
                dsc6_sb[hp][j][:], ds[r0:r0 + 128, :]),
                waits=tk6(("scuse", hp, j)))
            t_z = b.dma("act", lambda e, r0=r0, hp=hp, j=j: e.dma_start(
                dzc6_sb[hp][j][:], dz[r0:r0 + 128, :]),
                waits=tk6(("scuse", hp, j)))
            t_zs = b.op("dve", lambda e, hp=hp, j=j: e.tensor_mul(
                dzs6_sb[hp][j][:], dzc6_sb[hp][j][:], dsc6_sb[hp][j][:]),
                waits=[t_s, t_z] + tk6(("scuse", hp, j)))
            L6[("scld", hp, j)] = [t_s, t_zs]

    def emit_raw6(hg, c):
        p = c % 2
        cw = IC_CH[c]
        i0 = IC_OFF[c]
        raw_toks = []
        for j in range(4):
            r0 = (hg * 4 + j) * 128

            def fn(e, r0=r0, p=p, j=j, i0=i0, cw=cw):
                return e.dma_start(
                    rawd_sb[p][j][:, 0:cw * 128],
                    dq[r0:r0 + 128, i0 * 128:(i0 + cw) * 128])
            raw_toks.append(b.dma("pool", fn, waits=tk6(("rawfree", p, j))))
        return raw_toks

    def production_steps(hg, c, raw_toks):
        hp = hg % 2
        p = c % 2
        cw = IC_CH[c]
        i0 = IC_OFF[c]
        sc_last = {j: None for j in range(4)}
        pend_ev = []

        def flush_evacs():
            while pend_ev:
                j2, il2, pslot, t_tr = pend_ev.pop(0)
                col = il2 * 512 + j2 * 128
                if (il2 * 4 + j2) % 2 == 0:
                    t_ev = b.op("act", lambda e, p=p, col=col, pslot=pslot: e.copy(
                        wdts_sb[p][:, col:col + 128], ps_wtr[pslot][:]),
                        waits=[t_tr] + tk6(("mmlast", p)))
                else:
                    t_ev = b.op("dve", lambda e, p=p, col=col, pslot=pslot: e.tensor_copy(
                        wdts_sb[p][:, col:col + 128], ps_wtr[pslot][:]),
                        waits=[t_tr] + tk6(("mmlast", p)))
                L6[("wtrps", pslot)] = t_ev
                L6[("slabrdy", p, il2, j2)] = t_ev

        for il in range(cw):
            i = i0 + il
            flush_evacs()
            for j in range(4):
                wslot = wtr_ctr[0] % 8
                pslot = wtr_ctr[0] % 4
                wtr_ctr[0] += 1
                t_sc = b.op("dve", lambda e, p=p, j=j, il=il, i=i, wslot=wslot, hp=hp: e.tensor_scalar(
                    wnat6_sb[:, wslot * 128:(wslot + 1) * 128],
                    rawd_sb[p][j][:, il * 128:(il + 1) * 128],
                    dsc6_sb[hp][j][:, i:i + 1], dzs6_sb[hp][j][:, i:i + 1],
                    op0=ALU.mult, op1=ALU.subtract),
                    waits=[raw_toks[j]] + tk6(("scld", hp, j)) + tk6(("wnslot", wslot)))
                sc_last[j] = t_sc
                L6[("scuse", hp, j)] = t_sc
                t_tr = b.op("pe", lambda e, wslot=wslot, pslot=pslot: e.transpose(
                    ps_wtr[pslot][:],
                    wnat6_sb[:, wslot * 128:(wslot + 1) * 128], ident_sb[:]),
                    waits=[t_sc, t_ident] + tk6(("wtrps", pslot)))
                L6[("wnslot", wslot)] = t_tr
                pend_ev.append((j, il, pslot, t_tr))
            yield
        flush_evacs()
        for j in range(4):
            L6[("rawfree", p, j)] = sc_last[j]

    def emit_chunk_matmuls(hg, c, prod):
        p = c % 2
        cw = IC_CH[c]
        i0 = IC_OFF[c]
        seq = [(tc, il) for tc in range(NTC) for il in range(cw)]
        AHEAD = 6
        hl_toks = {}

        def issue_load(k):
            tc, il = seq[k]
            i = i0 + il
            hb = hgl_ctr[0] % NHGT
            hgl_ctr[0] += 1
            hl_toks[k] = (hb, b.dma("sp", lambda e, i=i, tc=tc, hb=hb: e.dma_start(
                hgt6_sb[hb][:],
                hg_s[i * 128:(i + 1) * 128, tc * 512:(tc + 1) * 512]),
                waits=tk6(("hgfree", hb))))

        for k in range(min(AHEAD, len(seq))):
            issue_load(k)
        mm_last = None
        mm_toks = {}
        for k, (tc, il) in enumerate(seq):
            if k + AHEAD < len(seq):
                issue_load(k + AHEAD)
            hb, t_hl = hl_toks.pop(k)
            for j in range(4):
                do_inc = (j == 3) or (il == cw - 1)
                t_mm = b.op("pe", lambda e, p=p, il=il, j=j, hb=hb, cw=cw: e.matmul(
                    ps_acc[j][:],
                    wdts_sb[p][:, il * 512 + j * 128: il * 512 + j * 128 + 128],
                    hgt6_sb[hb][:],
                    start=(il == 0), stop=(il == cw - 1)),
                    waits=[t_hl] + tk6(("slabrdy", p, il, j)) +
                          (tk6(("psfree", j)) if il == 0 else []),
                    inc=do_inc)
                if il == cw - 1:
                    mm_toks[j] = t_mm
                if j == 3:
                    L6[("hgfree", hb)] = t_mm
                    mm_last = t_mm
            if prod is not None:
                next(prod, None)
            if il == cw - 1:
                for j in range(4):
                    if c == 0:
                        t_fl = b.op("dve", lambda e, j=j, tc=tc: e.tensor_copy(
                            o2acc_sb[j][:, tc * 512:(tc + 1) * 512], ps_acc[j][:]),
                            waits=[mm_toks[j]] + tk6(("o2accfree", j, tc)))
                    else:
                        t_fl = b.op("dve", lambda e, j=j, tc=tc: e.tensor_add(
                            o2acc_sb[j][:, tc * 512:(tc + 1) * 512],
                            o2acc_sb[j][:, tc * 512:(tc + 1) * 512], ps_acc[j][:]),
                            waits=[mm_toks[j]])
                    L6[("psfree", j)] = t_fl
                    if c == 3:
                        L6[("accdone", j, tc)] = t_fl
        L6[("mmlast", p)] = mm_last

    def emit_hg_output(hg):
        otr_last = {}
        for tc in range(NTC):
            op_ = (hg * NTC + tc) % 2
            ev_toks = []
            for j in range(4):
                for ts_ in range(4):
                    po = (j + ts_) % 4
                    src_off = tc * 512 + ts_ * 128
                    t_tr = b.op("pe", lambda e, j=j, po=po, src_off=src_off: e.transpose(
                        ps_acc[po][:, 0:128],
                        o2acc_sb[j][:, src_off:src_off + 128], ident32_sb[:]),
                        waits=tk6(("accdone", j, tc)) + [t_id32] +
                              tk6(("otrps", po)) + tk6(("psfree", po)))
                    dcol = ts_ * 512 + j * 128
                    if (j * 4 + ts_) % 2 == 0:
                        t_ev = b.op("act", lambda e, op_=op_, dcol=dcol, po=po: e.copy(
                            o2a_sb[op_][:, dcol:dcol + 128],
                            ps_acc[po][:, 0:128]),
                            waits=[t_tr] + tk6(("o2afree", op_)))
                    else:
                        t_ev = b.op("dve", lambda e, op_=op_, dcol=dcol, po=po: e.tensor_copy(
                            o2a_sb[op_][:, dcol:dcol + 128],
                            ps_acc[po][:, 0:128]),
                            waits=[t_tr] + tk6(("o2afree", op_)))
                    L6[("otrps", po)] = t_ev
                    otr_last[po] = t_ev
                    ev_toks.append(t_ev)
                    if ts_ == 3:
                        L6[("o2accfree", j, tc)] = t_tr
            dst = bass.AP(o2_s.tensor, (tc * 512) * D + hg * 512,
                          [[D, 128], [128 * D, 4], [1, 512]])
            src = o2a_sb[op_][:].rearrange("p (s h) -> p s h", h=512)
            t_st = b.dma("act", lambda e, dst=dst, src=src: e.dma_start(dst, src),
                         waits=ev_toks)
            L6[("o2afree", op_)] = t_st
        for po in range(4):
            L6[("psfree", po)] = otr_last[po]

    chunk_seq = [(hg, c) for hg in range(NHG) for c in range(4)]
    emit_hg_scales(0)
    for _ in production_steps(0, 0, emit_raw6(0, 0)):
        pass
    pend_raw6 = {}
    if len(chunk_seq) > 1:
        pend_raw6[1] = emit_raw6(*chunk_seq[1])
    for ci, (hg, c) in enumerate(chunk_seq):
        if ci + 2 < len(chunk_seq):
            hg2, c2 = chunk_seq[ci + 2]
            if c2 == 0:
                emit_hg_scales(hg2)
            pend_raw6[ci + 2] = emit_raw6(hg2, c2)
        if ci + 1 < len(chunk_seq):
            hg1, c1 = chunk_seq[ci + 1]
            nxt = production_steps(hg1, c1, pend_raw6.pop(ci + 1))
        else:
            nxt = None
        emit_chunk_matmuls(hg, c, nxt)
        if nxt is not None:
            for _ in nxt:
                pass
        if c == 3:
            emit_hg_output(hg)

    free("ps_wtr3", "ps_wtr2", "ps_wtr1", "ps_wtr0",
         "ps_acc3", "ps_acc2", "ps_acc1", "ps_acc0")
    free("o2a_1", "o2a_0", "o2acc_3", "o2acc_2", "o2acc_1", "o2acc_0",
         *[f"hgt6_{i}" for i in range(NHGT - 1, -1, -1)],
         "wdts1", "wdts0", "wnat6_sb")
    free(*[f"dzs6_{p}_{j}" for p in (1, 0) for j in (3, 2, 1, 0)][::1])
    free(*[f"dzc6_{p}_{j}" for p in (1, 0) for j in (3, 2, 1, 0)])
    free(*[f"dsc6_{p}_{j}" for p in (1, 0) for j in (3, 2, 1, 0)])
    free(*[f"rawd{p}_{j}" for p in (1, 0) for j in (3, 2, 1, 0)])
    free("ident32_sb")
    b.bar()

    o2r_sb = [alloc(f"o2r{i}", [128, D], F32) for i in range(2)]
    hr_sb = [alloc(f"hr{i}", [128, D], F32) for i in range(2)]
    bn7_sb = [alloc(f"bn7_sb{i}", [128, 48], F32) for i in range(2)]
    mv7_sb = [alloc(f"mv7_sb{i}", [128, 2], F32) for i in range(2)]
    sd7_sb = [alloc(f"sd7_sb{i}", [128, 1], F32) for i in range(2)]
    rs7_sb = [alloc(f"rs7_sb{i}", [128, 1], F32) for i in range(2)]

    L7 = {}

    def tk7(key):
        v = L7.get(key)
        return [] if v is None else (v if isinstance(v, list) else [v])

    for qc in range(NT128):
        rb = qc % 2
        t_lo = b.dma("sp", lambda e, qc=qc, rb=rb: e.dma_start(
            o2r_sb[rb][:], o2_s[qc * 128:(qc + 1) * 128, :]),
            waits=tk7(("o2r", rb)))
        t_lh = b.dma("pool", lambda e, qc=qc, rb=rb: e.dma_start(
            hr_sb[rb][:], h_s[qc * 128:(qc + 1) * 128, :]),
            waits=tk7(("hr", rb)))
        ad_toks = []
        t_bn = None
        for j in range(8):
            t_ad = b.op("dve", lambda e, rb=rb, j=j: e.tensor_add(
                o2r_sb[rb][:, j * 512:(j + 1) * 512],
                o2r_sb[rb][:, j * 512:(j + 1) * 512],
                hr_sb[rb][:, j * 512:(j + 1) * 512]),
                waits=[t_lo, t_lh])
            ad_toks.append(t_ad)
            t_bn = b.op("dve", lambda e, rb=rb, j=j: e.bn_stats(
                bn7_sb[rb][:, j * 6:(j + 1) * 6],
                o2r_sb[rb][:, j * 512:(j + 1) * 512]),
                waits=[t_ad] + (tk7(("bn7", rb)) if j == 0 else []))
        L7[("hr", rb)] = ad_toks
        t_ag = b.op("dve", lambda e, rb=rb: e.bn_aggr(mv7_sb[rb][:], bn7_sb[rb][:]),
                    waits=[t_bn])
        L7[("bn7", rb)] = t_ag
        t_sd = b.op("act", lambda e, rb=rb: e.activation(
            sd7_sb[rb][:], mv7_sb[rb][:, 1:2], AF.Sqrt, bias=eps_sb[:]),
            waits=[t_ag, t_eps] + tk7(("sd7", rb)))
        t_rs = b.op("dve", lambda e, rb=rb: e.reciprocal(rs7_sb[rb][:], sd7_sb[rb][:]),
                    waits=[t_sd] + tk7(("rs7", rb)))
        L7[("sd7", rb)] = t_rs
        t_no = b.op("dve", lambda e, rb=rb: e.tensor_scalar(
            o2r_sb[rb][:], o2r_sb[rb][:], mv7_sb[rb][:, 0:1], rs7_sb[rb][:],
            op0=ALU.subtract, op1=ALU.mult), waits=[t_rs])
        L7[("rs7", rb)] = t_no
        t_st = b.dma("sp", lambda e, qc=qc, rb=rb: e.dma_start(
            out[qc * 128:(qc + 1) * 128, :], o2r_sb[rb][:]), waits=[t_no])
        L7[("o2r", rb)] = t_st

    if DEBUG_OUTS:
        b.wait("pool", [(s, b.cnt[s]) for s in SEMS if b.cnt[s] > 0])
        dbg = {
            "dbg_qs": (qs_s, [D, S], BF16),
            "dbg_ks": (ks_s, [HKV * DH, S], BF16),
            "dbg_as": (as_s, [S, D], BF16),
            "dbg_o1": (o1_s, [S, D], F32),
            "dbg_h": (h_s, [S, D], F32),
            "dbg_hg": (hg_s, [I_FF, S], BF16),
            "dbg_hbf": (hbf_s, [S, D], BF16),
            "dbg_wg": (wgdq_s, [I_FF, D], BF16),
            "dbg_wu": (wudq_s, [I_FF, D], BF16),
            "dbg_o2": (o2_s, [S, D], F32),
        }
        for name, (srcap, shp, dt) in dbg.items():
            dout = nc.dram_tensor(name, shp, dt, kind="ExternalOutput").ap()
            b.dma("pool", lambda e, dout=dout, srcap=srcap: e.dma_start(
                dout[:], srcap[:]))

    b.wait("sp", [(s, b.cnt[s]) for s in SEMS if b.cnt[s] > 0])

    import contextlib
    with contextlib.ExitStack() as es:
        for s in SEMS:
            b.sem[s] = es.enter_context(nc.semaphore(s))
        block = es.enter_context(nc.Block())
        b.replay(block)
    for n, cm in reversed(ctx_stack):
        cm.__exit__(None, None, None)
    ctx_stack.clear()
    return nc


def make_inputs(inputs, S):
    x = np.ascontiguousarray(np.asarray(inputs["L0_attn_x"], dtype=np.float32))
    qperm = _q_perm(HQ)
    kperm = _q_perm(HKV)
    base = {
        "wq": np.ascontiguousarray(np.asarray(inputs["L0_attn_wq"], dtype=np.float32)[:, qperm]),
        "wk": np.ascontiguousarray(np.asarray(inputs["L0_attn_wk"], dtype=np.float32)[:, kperm]),
        "wv": np.ascontiguousarray(np.asarray(inputs["L0_attn_wv"], dtype=np.float32)),
        "wo": np.ascontiguousarray(np.asarray(inputs["L0_attn_wo"], dtype=np.float32)),
        "gq": np.ascontiguousarray(np.asarray(inputs["L0_ffn_gate_qweight"], dtype=np.int32)),
        "gs": np.ascontiguousarray(np.asarray(inputs["L0_ffn_gate_scales"], dtype=np.float32)),
        "gz": np.ascontiguousarray(np.asarray(inputs["L0_ffn_gate_zeros"], dtype=np.float32)),
        "uq": np.ascontiguousarray(np.asarray(inputs["L0_ffn_up_qweight"], dtype=np.int32)),
        "us": np.ascontiguousarray(np.asarray(inputs["L0_ffn_up_scales"], dtype=np.float32)),
        "uz": np.ascontiguousarray(np.asarray(inputs["L0_ffn_up_zeros"], dtype=np.float32)),
        "dq": np.ascontiguousarray(np.asarray(inputs["L0_ffn_down_qweight"], dtype=np.int32)),
        "ds": np.ascontiguousarray(np.asarray(inputs["L0_ffn_down_scales"], dtype=np.float32)),
        "dz": np.ascontiguousarray(np.asarray(inputs["L0_ffn_down_zeros"], dtype=np.float32)),
    }
    base.update(_host_consts(S))
    return x, base


def kernel(**inputs):
    x, base = make_inputs(inputs, 2048)
    B, S, _ = x.shape
    assert B == 8 and S == 2048
    nc = build_program(S)
    in_maps = []
    for c in range(8):
        m = dict(base)
        m["xin"] = np.ascontiguousarray(x[c])
        in_maps.append(m)
    res = run_bass_kernel_spmd(nc, in_maps, core_ids=list(range(8)))
    return np.stack([r["out"] for r in res.results], axis=0)


# version 15
# speedup vs baseline: 1.1752x; 1.0346x over previous
import math
import numpy as np
import ml_dtypes
import concourse.bass as bass
import concourse.mybir as mybir
from concourse.bass_utils import run_bass_kernel_spmd

F32 = mybir.dt.float32
BF16 = mybir.dt.bfloat16
I32 = mybir.dt.int32
AF = mybir.ActivationFunctionType
ALU = mybir.AluOpType
AX = mybir.AxisListType

D = 4096
NF = 32
HQ = 32
HKV = 8
DH = 128
I_FF = 11008
NIB = 86
RSQ = 1.0 / math.sqrt(128.0)
EPS = 1e-5
DEBUG_OUTS = False

ENGINES = ("pe", "dve", "act", "pool", "sp")
NDMASEM = 8
DMA_QUEUES = ("qsp", "qact", "qpool")
SEMS = ("pe", "dve", "act", "pool") + tuple(
    f"{q}{r}" for q in DMA_QUEUES for r in range(NDMASEM))
DMA_SEM = {"sp": "qsp", "act": "qact", "pool": "qpool"}


def _flat(toks):
    out = []
    for t in toks:
        if t is None:
            continue
        if isinstance(t, list):
            out.extend(x for x in t if x is not None)
        else:
            out.append(t)
    return out


class _B:

    def __init__(self, nc):
        self.nc = nc
        self.ops = {e: [] for e in ENGINES}
        self.cnt = {s: 0 for s in SEMS}
        self.cache = {e: {s: 0 for s in SEMS} for e in ENGINES}
        self.pend_bar = {e: None for e in ENGINES}
        self.sem = {}
        self.n_inst = 0
        self.dma_rot = {q: 0 for q in DMA_QUEUES}

    def _waits(self, eng, waits):
        out = []
        bar = self.pend_bar[eng]
        if bar is not None:
            self.pend_bar[eng] = None
            for s, v in bar:
                if s == eng:
                    continue
                out.append((s, v))
        out.extend(_flat(list(waits)))
        ded = []
        for s, v in out:
            if v > self.cache[eng][s]:
                self.cache[eng][s] = v
                ded.append((s, v))
        return ded

    def op(self, eng, fn, waits=(), inc=True):
        ded = self._waits(eng, waits)
        tok = None
        incs = []
        if inc:
            self.cnt[eng] += 1
            tok = (eng, self.cnt[eng])
            incs = [(eng, 1)]
        self.ops[eng].append((ded, fn, incs))
        self.n_inst += 1 + len(ded)
        return tok

    def dma(self, eng, fn, waits=()):
        q = DMA_SEM[eng]
        r = self.dma_rot[q] % NDMASEM
        self.dma_rot[q] += 1
        s = f"{q}{r}"
        ded = self._waits(eng, list(waits) + ([(s, self.cnt[s])] if self.cnt[s] else []))
        self.cnt[s] += 16
        tok = (s, self.cnt[s])
        self.ops[eng].append((ded, fn, [(s, 16)]))
        self.n_inst += 1 + len(ded)
        return tok

    def wait(self, eng, waits):
        ded = self._waits(eng, waits)
        if ded:
            self.ops[eng].append((ded, None, []))
            self.n_inst += len(ded)

    def bar(self):
        snap = [(s, self.cnt[s]) for s in SEMS if self.cnt[s] > 0]
        for e in ENGINES:
            self.pend_bar[e] = snap

    def replay(self, block):
        def mk(eng):
            def run(e):
                for ded, fn, incs in self.ops[eng]:
                    for s, v in ded:
                        e.wait_ge(self.sem[s], v)
                    if fn is not None:
                        inst = fn(e)
                        for s, a in incs:
                            inst.then_inc(self.sem[s], a)
            return run

        block.tensor(mk("pe"))
        block.vector(mk("dve"))
        block.scalar(mk("act"))
        block.gpsimd(mk("pool"))
        block.sync(mk("sp"))


def _q_perm(nheads):
    p = []
    for h in range(nheads):
        base = h * DH
        p.extend(base + j for j in range(0, DH, 2))
        p.extend(base + j for j in range(1, DH, 2))
    return np.array(p, dtype=np.int64)


def _host_consts(S):
    inv = 1.0 / (10000.0 ** (np.arange(0, DH, 2, dtype=np.float64) / DH))
    t = np.arange(S, dtype=np.float64)
    ang = np.outer(inv, t)
    cos = np.cos(ang).astype(np.float32)
    sin = np.sin(ang).astype(np.float32)
    cp = np.concatenate([cos, cos], axis=0)
    sp = np.concatenate([-sin, sin], axis=0)
    swap = np.zeros((128, 128), dtype=np.float32)
    for i in range(128):
        swap[i, (i + 64) % 128] = 1.0
    masks = np.zeros((4, 128, 512), dtype=np.float32)
    ql = np.arange(512)
    kl = np.arange(128)
    for off in range(4):
        masks[off] = (ql[None, :] >= off * 128 + kl[:, None]).astype(np.float32)
    bf = ml_dtypes.bfloat16
    return {
        "cp_c": cp.astype(bf), "sp_c": sp.astype(bf),
        "swap_c": swap.astype(bf),
        "mask_c": masks.astype(bf),
        "onec_c": np.ones((128, 1), dtype=np.float32).astype(bf),
        "oner_c": np.ones((1, 128), dtype=np.float32).astype(bf),
        "ident_c": np.eye(128, dtype=np.float32).astype(bf),
        "ident32_c": np.eye(128, dtype=np.float32),
    }


def build_program(S):
    assert S in (512, 1024, 2048)
    NTC = S // 512
    NQB = NTC
    NT128 = S // 128
    NQ = HQ * NQB
    import os
    NHALF = 2 if S >= 1024 else 1
    if os.environ.get("KNHALF"):
        NHALF = int(os.environ["KNHALF"])
        assert S // NHALF >= 512
    HALF_BAR = bool(os.environ.get("KHALFBAR"))
    TH = S // NHALF
    NTC_H = TH // 512

    nc = bass.Bass("TRN2", target_bir_lowering=False)

    xin = nc.dram_tensor("xin", [S, D], F32, kind="ExternalInput").ap()
    wq = nc.dram_tensor("wq", [D, D], F32, kind="ExternalInput").ap()
    wk = nc.dram_tensor("wk", [D, HKV * DH], F32, kind="ExternalInput").ap()
    wv = nc.dram_tensor("wv", [D, HKV * DH], F32, kind="ExternalInput").ap()
    wo = nc.dram_tensor("wo", [D, D], F32, kind="ExternalInput").ap()
    gq = nc.dram_tensor("gq", [I_FF, D], I32, kind="ExternalInput").ap()
    gs = nc.dram_tensor("gs", [I_FF, NF], F32, kind="ExternalInput").ap()
    gz = nc.dram_tensor("gz", [I_FF, NF], F32, kind="ExternalInput").ap()
    uq = nc.dram_tensor("uq", [I_FF, D], I32, kind="ExternalInput").ap()
    us = nc.dram_tensor("us", [I_FF, NF], F32, kind="ExternalInput").ap()
    uz = nc.dram_tensor("uz", [I_FF, NF], F32, kind="ExternalInput").ap()
    dq = nc.dram_tensor("dq", [D, I_FF], I32, kind="ExternalInput").ap()
    ds = nc.dram_tensor("ds", [D, NIB], F32, kind="ExternalInput").ap()
    dz = nc.dram_tensor("dz", [D, NIB], F32, kind="ExternalInput").ap()
    cp_c = nc.dram_tensor("cp_c", [128, S], BF16, kind="ExternalInput").ap()
    sp_c = nc.dram_tensor("sp_c", [128, S], BF16, kind="ExternalInput").ap()
    swap_c = nc.dram_tensor("swap_c", [128, 128], BF16, kind="ExternalInput").ap()
    mask_c = nc.dram_tensor("mask_c", [4, 128, 512], BF16, kind="ExternalInput").ap()
    onec_c = nc.dram_tensor("onec_c", [128, 1], BF16, kind="ExternalInput").ap()
    oner_c = nc.dram_tensor("oner_c", [1, 128], BF16, kind="ExternalInput").ap()
    ident_c = nc.dram_tensor("ident_c", [128, 128], BF16, kind="ExternalInput").ap()
    ident32_c = nc.dram_tensor("ident32_c", [128, 128], F32, kind="ExternalInput").ap()
    out = nc.dram_tensor("out", [S, D], F32, kind="ExternalOutput").ap()

    xbf_s = nc.dram_tensor("xbf_s", [S, D], BF16).ap()
    qs_s = nc.dram_tensor("qs_s", [D, S], BF16).ap()
    ks_s = nc.dram_tensor("ks_s", [HKV * DH, S], BF16).ap()
    as_s = nc.dram_tensor("as_s", [S, D], BF16).ap()
    o1_s = nc.dram_tensor("o1_s", [S, D], F32).ap()
    h_s = nc.dram_tensor("h_s", [S, D], F32).ap()
    hbf_s = nc.dram_tensor("hbf_s", [S, D], BF16).ap()
    wgdq_s = nc.dram_tensor("wgdq_s", [I_FF, D], BF16).ap()
    wudq_s = nc.dram_tensor("wudq_s", [I_FF, D], BF16).ap()
    hg_s = nc.dram_tensor("hg_s", [I_FF, S], BF16).ap()
    o2_s = nc.dram_tensor("o2_s", [S, D], F32).ap()

    b = _B(nc)
    ctx_stack = []

    def alloc(name, shape, dtype, psum=False):
        cm = nc.psum_tensor(name, shape, dtype) if psum else nc.sbuf_tensor(name, shape, dtype)
        h = cm.__enter__()
        ctx_stack.append((name, cm))
        return h

    def free(*names):
        names = set(names)
        while names:
            n, cm = ctx_stack[-1]
            assert n in names, f"non-LIFO free: top={n}, want={names}"
            cm.__exit__(None, None, None)
            ctx_stack.pop()
            names.discard(n)

    swap_sb = alloc("swap_sb", [128, 128], BF16)
    onec_sb = alloc("onec_sb", [128, 1], BF16)
    oner_sb = alloc("oner_sb", [1, 128], BF16)
    ident_sb = alloc("ident_sb", [128, 128], BF16)
    negc_sb = alloc("negc_sb", [128, NQ], F32)
    eps_sb = alloc("eps_sb", [128, 1], F32)
    v_sb = alloc("v_sb", [128, NT128 * HKV * 130], BF16)

    t_swap = b.dma("sp", lambda e: e.dma_start(swap_sb[:], swap_c[:]))
    t_onec = b.dma("sp", lambda e: e.dma_start(onec_sb[:], onec_c[:]))
    t_oner = b.dma("sp", lambda e: e.dma_start(oner_sb[:], oner_c[:]))
    t_ident = b.dma("sp", lambda e: e.dma_start(ident_sb[:], ident_c[:]))
    t_vones = b.op("dve", lambda e: e.memset(v_sb[:], 1.0))
    t_eps = b.op("dve", lambda e: e.memset(eps_sb[:], EPS))

    x_fm = alloc("x_fm", [128, NF * S], BF16)
    xrow_sb = [alloc("xrow0", [128, D], BF16)]
    ps_tr1 = [alloc(f"ps_tr1_{i}", [128, 128], BF16, psum=True) for i in range(3)]
    t_h2h = []
    rch = S // 4
    for j in range(4):
        t_h2h.append(b.dma("pool", lambda e, j=j: e.dma_start(
            xbf_s[j * rch:(j + 1) * rch, :], xin[j * rch:(j + 1) * rch, :])))
    t_xfm = {}
    tr01_ctr = [0]
    L0 = {}

    def tk0(key):
        v = L0.get(key)
        return [] if v is None else (v if isinstance(v, list) else [v])

    for qc in range(NT128):
        rb = 0
        t_xr = b.dma("sp", lambda e, qc=qc, rb=rb: e.dma_start(
            xrow_sb[rb][:], xbf_s[qc * 128:(qc + 1) * 128, :]),
            waits=[t_h2h[qc // 4]] + tk0(("xrowfree", rb)))
        last_tr = None
        for g in range(NF):
            pt = tr01_ctr[0] % 3
            tr01_ctr[0] += 1
            t_tr = b.op("pe", lambda e, rb=rb, g=g, pt=pt: e.transpose(
                ps_tr1[pt][:], xrow_sb[rb][:, g * 128:(g + 1) * 128], ident_sb[:]),
                waits=[t_xr, t_ident] + tk0(("tr1", pt)))
            last_tr = t_tr
            col = g * S + qc * 128
            if g % 2 == 0:
                t_ev = b.op("act", lambda e, col=col, pt=pt: e.copy(
                    x_fm[:, col:col + 128], ps_tr1[pt][:]), waits=[t_tr])
            else:
                t_ev = b.op("dve", lambda e, col=col, pt=pt: e.tensor_copy(
                    x_fm[:, col:col + 128], ps_tr1[pt][:]), waits=[t_tr])
            L0[("tr1", pt)] = t_ev
            t_xfm[(qc, g)] = t_ev
        L0[("xrowfree", rb)] = last_tr

    cp_sb = alloc("cp_sb", [128, S], BF16)
    sp_sb = alloc("sp_sb", [128, S], BF16)
    t_cp = b.dma("sp", lambda e: e.dma_start(cp_sb[:], cp_c[:]))
    t_sp = b.dma("sp", lambda e: e.dma_start(sp_sb[:], sp_c[:]))

    cb_row = alloc("cb_row", [1, NQ], F32)
    kn_row = alloc("kn_row", [1, HKV * NTC], F32)
    slab_sb = [alloc(f"slab{i}", [128, NF * 128], BF16) for i in range(2)]
    qraw_sb = [alloc(f"qraw{i}", [128, 512], BF16) for i in range(2)]
    m1_sb = [alloc(f"m1_{i}", [128, 512], BF16) for i in range(2)]
    m2_sb = [alloc(f"m2_{i}", [128, 512], BF16) for i in range(2)]
    qfm_sb = [alloc(f"qfm{i}", [128, 512], BF16) for i in range(2)]
    qsq_sb = [alloc("qsq0", [128, 512], BF16)]
    vfm_sb = [alloc(f"vfm{i}", [128, 512], BF16) for i in range(2)]
    kmax2 = alloc("kmax2", [1, HKV], F32)
    cbs_sb = alloc("cbs_sb", [1, NQ], F32)
    negc_bf = alloc("negc_bf", [1, NQ], BF16)
    ps_p = [alloc(f"ps_p{i}", [128, 512], F32, psum=True) for i in range(2)]
    ps_sw = [alloc(f"ps_sw{i}", [128, 512], F32, psum=True) for i in range(2)]
    ps_n = alloc("ps_n", [1, 512], F32, psum=True)

    last = {}

    def tk(key):
        v = last.get(key)
        return [] if v is None else (v if isinstance(v, list) else [v])

    n_ob = HQ + HKV + HKV

    def slab_src(ob):
        if ob < HQ:
            return wq, ob, D
        if ob < HQ + HKV:
            return wk, ob - HQ, HKV * DH
        return wv, ob - HQ - HKV, HKV * DH

    def emit_slab_load(ob, buf):
        w, col, rw = slab_src(ob)
        toks = []
        for j in range(4):
            def fn(e, w=w, col=col, j=j, buf=buf, rw=rw):
                src = bass.AP(w.tensor, j * 8 * 128 * rw + col * 128,
                              [[rw, 128], [128 * rw, 8], [1, 128]])
                dst = slab_sb[buf][:, j * 1024:(j + 1) * 1024].rearrange(
                    "p (c d) -> p c d", d=128)
                return e.dma_start(dst, src)
            toks.append(b.dma("pool", fn, waits=tk(("slabfree", buf))))
        return toks

    slab_tok = [None, None]
    slab_tok[0] = emit_slab_load(0, 0)
    npp = 0

    pendA = []
    pendB = []

    def make_rope_tail(ob, tc, rb, is_q, g_kv, t_ev):
        st = {}

        def cloA():
            psw = (ob * NTC + tc) % 2
            t_sw = b.op("pe", lambda e, rb=rb, psw=psw: e.matmul(
                ps_sw[psw][:], swap_sb[:], qraw_sb[rb][:], start=True, stop=True),
                waits=[t_ev, t_swap] + tk(("ps_sw", psw)))
            t_m1 = b.op("pool", lambda e, rb=rb, tc=tc: e.tensor_mul(
                m1_sb[rb][:], qraw_sb[rb][:], cp_sb[:, tc * 512:tc * 512 + 512]),
                waits=[t_ev, t_cp] + tk(("m1", rb)))
            last[("qraw", rb)] = [t_sw, t_m1]
            t_m2 = b.op("dve", lambda e, rb=rb, psw=psw, tc=tc: e.tensor_mul(
                m2_sb[rb][:], ps_sw[psw][:], sp_sb[:, tc * 512:tc * 512 + 512]),
                waits=[t_sw, t_sp] + tk(("m2", rb)))
            last[("ps_sw", psw)] = t_m2
            t_fm = b.op("dve", lambda e, rb=rb: e.tensor_add(
                qfm_sb[rb][:], m1_sb[rb][:], m2_sb[rb][:]),
                waits=[t_m1, t_m2] + tk(("qfm", rb)))
            last[("m1", rb)] = t_fm
            last[("m2", rb)] = t_fm
            if is_q:
                t_st = b.dma("sp", lambda e, ob=ob, tc=tc, rb=rb: e.dma_start(
                    qs_s[ob * 128:(ob + 1) * 128, tc * 512:tc * 512 + 512],
                    qfm_sb[rb][:]), waits=[t_fm])
            else:
                t_st = b.dma("sp", lambda e, g_kv=g_kv, tc=tc, rb=rb: e.dma_start(
                    ks_s[g_kv * 128:(g_kv + 1) * 128, tc * 512:tc * 512 + 512],
                    qfm_sb[rb][:]), waits=[t_fm])
            t_sq = b.op("act", lambda e, rb=rb: e.square(qsq_sb[0][:], qfm_sb[rb][:]),
                        waits=[t_fm] + tk(("qsq",)))
            last[("qfm", rb)] = [t_st, t_sq]
            st["sq"] = t_sq

        def cloB():
            t_nm = b.op("pe", lambda e: e.matmul(
                ps_n[:], onec_sb[:], qsq_sb[0][:], start=True, stop=True),
                waits=[st["sq"], t_onec] + tk(("ps_n",)))
            last[("qsq",)] = t_nm
            if is_q:
                idx = ob * NQB + tc
                t_rd = b.op("dve", lambda e, idx=idx: e.reduce_max(
                    cb_row[0:1, idx:idx + 1], ps_n[0:1, :], axis=AX.X),
                    waits=[t_nm])
            else:
                idx = g_kv * NTC + tc
                t_rd = b.op("dve", lambda e, idx=idx: e.reduce_max(
                    kn_row[0:1, idx:idx + 1], ps_n[0:1, :], axis=AX.X),
                    waits=[t_nm])
            last[("ps_n",)] = t_rd
        return cloA, cloB

    def make_v_tail(tc, rb, g_kv, t_ev):
        def clo():
            t_ve = None
            for j in range(4):
                pt = tr01_ctr[0] % 3
                tr01_ctr[0] += 1
                t_tr = b.op("pe", lambda e, rb=rb, j=j, pt=pt: e.transpose(
                    ps_tr1[pt][:], vfm_sb[rb][:, j * 128:(j + 1) * 128], ident_sb[:]),
                    waits=[t_ev, t_ident] + tk0(("tr1", pt)))
                col = (tc * 4 + j) * HKV * 130 + g_kv * 130
                t_ve = b.op("dve", lambda e, pt=pt, col=col: e.tensor_copy(
                    v_sb[:, col:col + 128], ps_tr1[pt][:]),
                    waits=[t_tr, t_vones])
                L0[("tr1", pt)] = t_ve
            last[("vfm", rb)] = t_ve
        return clo

    for ob in range(n_ob):
        buf = ob % 2
        if ob + 1 < n_ob:
            slab_tok[(ob + 1) % 2] = emit_slab_load(ob + 1, (ob + 1) % 2)
        is_q = ob < HQ
        is_v = ob >= HQ + HKV
        g_kv = None if is_q else (ob - HQ if not is_v else ob - HQ - HKV)
        for tc in range(NTC):
            pp = npp % 2
            npp += 1
            mm_tok = None
            for fc in range(NF):
                mm_tok = b.op("pe", lambda e, fc=fc, tc=tc, pp=pp, buf=buf: e.matmul(
                    ps_p[pp][:],
                    slab_sb[buf][:, fc * 128:(fc + 1) * 128],
                    x_fm[:, fc * S + tc * 512: fc * S + tc * 512 + 512],
                    start=(fc == 0), stop=(fc == NF - 1)),
                    waits=slab_tok[buf] +
                          [t_xfm[(qq, fc)] for qq in range(4 * tc, 4 * tc + 4)] +
                          tk(("ps_p", pp)),
                    inc=(fc == NF - 1))
            if tc == NTC - 1:
                last[("slabfree", buf)] = mm_tok
            rb = (ob * NTC + tc) % 2
            if is_v:
                t_ev = b.op("act", lambda e, pp=pp, rb=rb: e.copy(
                    vfm_sb[rb][:], ps_p[pp][:]),
                    waits=[mm_tok] + tk(("vfm", rb)))
                last[("ps_p", pp)] = t_ev
                pendA.append(make_v_tail(tc, rb, g_kv, t_ev))
                nb_app = 0
            else:
                t_ev = b.op("act", lambda e, pp=pp, rb=rb: e.copy(
                    qraw_sb[rb][:], ps_p[pp][:]),
                    waits=[mm_tok] + tk(("qraw", rb)))
                last[("ps_p", pp)] = t_ev
                cloA, cloB = make_rope_tail(ob, tc, rb, is_q, g_kv, t_ev)
                pendA.append(cloA)
                pendB.append(cloB)
                nb_app = 1
            while len(pendB) > 1 + nb_app:
                pendB.pop(0)()
            while len(pendA) > 1:
                pendA.pop(0)()
    while len(pendB) > 1:
        pendB.pop(0)()
    while pendA:
        pendA.pop(0)()
    while pendB:
        pendB.pop(0)()

    kn_3d = kn_row[0:1, :].rearrange("p (g t) -> p g t", g=HKV)
    km_3d = kmax2[0:1, :].rearrange("p (g o) -> p g o", o=1)
    t1 = b.op("dve", lambda e: e.tensor_reduce(
        km_3d, kn_3d, axis=AX.X, op=ALU.max),
        waits=[last[("ps_n",)]])
    cb_3d = cb_row[0:1, :].rearrange("p (g q) -> p g q", g=HKV)
    cb_b, km_b = bass.broadcast_tensor_aps(cb_3d, km_3d)
    t2 = b.op("dve", lambda e: e.tensor_tensor(cb_b, cb_b, km_b, op=ALU.mult),
              waits=[t1])
    t3 = b.op("act", lambda e: e.activation(
        cbs_sb[0:1, :], cb_row[0:1, :], AF.Sqrt, scale=1.0 / 128.0), waits=[t2])
    t4 = b.op("dve", lambda e: e.tensor_scalar(
        negc_bf[0:1, :], cbs_sb[0:1, :], -1.0, None, op0=ALU.mult), waits=[t3])
    t5 = b.op("pe", lambda e: e.matmul(
        ps_sw[0][:, 0:NQ], oner_sb[:], negc_bf[0:1, :], start=True, stop=True),
        waits=[t4, t_oner] + tk(("ps_sw", 0)))
    t_negc = b.op("dve", lambda e: e.tensor_copy(negc_sb[:, 0:NQ], ps_sw[0][:, 0:NQ]),
                  waits=[t5])

    free("ps_n", "ps_sw1", "ps_sw0", "ps_p1", "ps_p0")
    free("negc_bf", "cbs_sb", "kmax2", "vfm1", "vfm0", "qsq0", "qfm1", "qfm0",
         "m2_1", "m2_0", "m1_1", "m1_0", "qraw1", "qraw0", "slab1", "slab0",
         "kn_row", "cb_row", "sp_sb", "cp_sb")
    free("ps_tr1_2", "ps_tr1_1", "ps_tr1_0")
    free("xrow0")
    free("x_fm")
    b.bar()

    mask_sb = alloc("mask_sb", [128, 4 * 512], BF16)
    kfm_sb = alloc("kfm_sb", [128, HKV * S], BF16)
    qh_sb = [alloc(f"qh{i}", [128, S], BF16) for i in range(2)]
    pkm_sb = [alloc(f"pkm{i}", [128, NT128 * 512], BF16) for i in range(2)]
    aacc_sb = alloc("aacc_sb", [128, NT128 * 512], BF16)
    r_sb = alloc("r_sb", [128, 1], F32)
    ps_s = [alloc(f"ps_s{i}", [128, 512], F32, psum=True) for i in range(4)]
    ps_av = [alloc(f"ps_av{i}", [128, 132], F32, psum=True) for i in range(2)]

    _mask_src = bass.AP(mask_c.tensor, 0, [[512, 128], [128 * 512, 4], [1, 512]])
    _mask_dst = mask_sb[:].rearrange("p (o q) -> p o q", q=512)
    t_mask = b.dma("sp", lambda e: e.dma_start(_mask_dst, _mask_src))
    t_k = {}
    for g in range(HKV):
        t_k[g] = b.dma("sp", lambda e, g=g: e.dma_start(
            kfm_sb[:, g * S:(g + 1) * S], ks_s[g * 128:(g + 1) * 128, :]))

    L2 = {}

    def tk2(key):
        v = L2.get(key)
        return [] if v is None else (v if isinstance(v, list) else [v])

    def q_load(h, buf):
        return b.dma("sp", lambda e, h=h, buf=buf: e.dma_start(
            qh_sb[buf][:], qs_s[h * 128:(h + 1) * 128, :]),
            waits=tk2(("qhfree", buf)))

    L2[("qld", 0)] = q_load(0, 0)
    prod_pkm = {}
    avq = []
    av_ctr = [0]
    sc_ctr = [0]

    def pop_av(n):
        for _ in range(n):
            if avq:
                avq.pop(0)[1]()

    def drain_av_through(seq):
        while avq and avq[0][0] <= seq:
            avq.pop(0)[1]()

    def make_av_group(h, qb, pb, g, hh, kb_hi, q4, state):
        qsub = qb * 4 + q4

        def clo():
            ab = av_ctr[0] % 2
            av_ctr[0] += 1
            t_av = None
            for kb in range(qsub + 1):
                t_av = b.op("pe", lambda e, pb=pb, kb=kb, q4=q4, g=g, qsub=qsub, ab=ab: e.matmul(
                    ps_av[ab][:, 0:129],
                    pkm_sb[pb][:, kb * 512 + q4 * 128: kb * 512 + q4 * 128 + 128],
                    v_sb[:, kb * HKV * 130 + g * 130: kb * HKV * 130 + g * 130 + 129],
                    start=(kb == 0), stop=(kb == qsub)),
                    waits=[prod_pkm[(pb, kb)]] + tk2(("ps_av", ab)),
                    inc=(kb == qsub))
            state["last"] = t_av
            t_r = b.op("dve", lambda e, ab=ab: e.reciprocal(
                r_sb[:], ps_av[ab][:, 128:129]), waits=[t_av] + tk2(("r",)))
            t_an = b.op("dve", lambda e, ab=ab, qsub=qsub, hh=hh: e.tensor_scalar(
                aacc_sb[:, qsub * 512 + hh * 128: qsub * 512 + hh * 128 + 128],
                ps_av[ab][:, 0:128], r_sb[:], None, op0=ALU.mult),
                waits=[t_r] + tk2(("aaccfree", qsub)))
            L2[("r",)] = t_an
            L2[("ps_av", ab)] = t_an
            L2[("aaccw", qsub, hh)] = t_an
            if q4 == 3:
                for kb in range(kb_hi):
                    L2[("pkmfree", pb, kb)] = state["last"]
        return clo

    def make_store(g):
        def clo():
            for qsub in range(NT128):
                w = [L2[("aaccw", qsub, j)] for j in range(4)]
                t_sta = b.dma("sp", lambda e, qsub=qsub, g=g: e.dma_start(
                    as_s[qsub * 128:(qsub + 1) * 128, g * 512:(g + 1) * 512],
                    aacc_sb[:, qsub * 512:(qsub + 1) * 512]), waits=w)
                L2[("aaccfree", qsub)] = t_sta
        return clo

    for h in range(HQ):
        g = h // 4
        hh = h % 4
        hb = h % 2
        if h + 1 < HQ:
            L2[("qld", (h + 1) % 2)] = q_load(h + 1, (h + 1) % 2)
        last_sc_mm = None
        for qb in range(NQB):
            seq = h * NQB + qb
            pb = seq % 2
            kb_hi = 4 * qb + 4
            drain_av_through(seq - 2)
            for kb in range(kb_hi):
                sb_ = sc_ctr[0] % 4
                sc_ctr[0] += 1
                t_mm = b.op("pe", lambda e, g=g, kb=kb, qb=qb, hb=hb, sb_=sb_: e.matmul(
                    ps_s[sb_][:],
                    kfm_sb[:, g * S + kb * 128: g * S + kb * 128 + 128],
                    qh_sb[hb][:, qb * 512: qb * 512 + 512],
                    start=True, stop=True),
                    waits=[t_k[g]] + tk2(("qld", hb)) + tk2(("ps_s", sb_)))
                last_sc_mm = t_mm
                t_ex = b.op("act", lambda e, sb_=sb_, pb=pb, kb=kb, h=h, qb=qb: e.activation(
                    pkm_sb[pb][:, kb * 512: kb * 512 + 512], ps_s[sb_][:],
                    AF.Exp, bias=negc_sb[:, h * NQB + qb: h * NQB + qb + 1],
                    scale=RSQ),
                    waits=[t_mm, t_negc] + tk2(("pkmfree", pb, kb)))
                L2[("ps_s", sb_)] = t_ex
                if kb >= 4 * qb:
                    off = kb - 4 * qb
                    t_mk = b.op("dve", lambda e, pb=pb, kb=kb, off=off: e.tensor_mul(
                        pkm_sb[pb][:, kb * 512: kb * 512 + 512],
                        pkm_sb[pb][:, kb * 512: kb * 512 + 512],
                        mask_sb[:, off * 512:(off + 1) * 512]),
                        waits=[t_ex, t_mask])
                    prod_pkm[(pb, kb)] = t_mk
                else:
                    prod_pkm[(pb, kb)] = t_ex
                if kb % 2 == 1:
                    pop_av(1)
            state = {"last": None}
            for q4 in range(4):
                avq.append((seq, make_av_group(h, qb, pb, g, hh, kb_hi, q4, state)))
            if hh == 3 and qb == NQB - 1:
                avq.append((seq, make_store(g)))
        L2[("qhfree", hb)] = last_sc_mm
    pop_av(len(avq))

    free("ps_av1", "ps_av0", "ps_s3", "ps_s2", "ps_s1", "ps_s0")
    free("r_sb", "aacc_sb", "pkm1", "pkm0", "qh1", "qh0", "kfm_sb", "mask_sb")
    free("v_sb")
    b.bar()

    afm_sb = alloc("afm_sb", [128, NF * S], BF16)
    arow_sb = [alloc("arow0", [128, D], BF16)]
    wo_sb = [alloc(f"wo{i}", [128, NF * 512], BF16) for i in range(2)]
    o1t_sb = [alloc(f"o1t{i}", [128, 512], F32) for i in range(2)]
    ps_o = [alloc(f"ps_o{i}", [128, 512], F32, psum=True) for i in range(2)]
    ps_tr3 = [alloc(f"ps_tr3_{i}", [128, 128], BF16, psum=True) for i in range(3)]

    L3 = {}

    def tk3(key):
        v = L3.get(key)
        return [] if v is None else (v if isinstance(v, list) else [v])

    t_afm = {}
    tr3_ctr = [0]
    for qc in range(NT128):
        rb = 0
        t_ar = b.dma("sp", lambda e, qc=qc, rb=rb: e.dma_start(
            arow_sb[rb][:], as_s[qc * 128:(qc + 1) * 128, :]),
            waits=tk3(("arowfree", rb)))
        last_tr = None
        for cc in range(NF):
            pt = tr3_ctr[0] % 3
            tr3_ctr[0] += 1
            t_tr = b.op("pe", lambda e, rb=rb, cc=cc, pt=pt: e.transpose(
                ps_tr3[pt][:], arow_sb[rb][:, cc * 128:(cc + 1) * 128], ident_sb[:]),
                waits=[t_ar, t_ident] + tk3(("tr3", pt)))
            last_tr = t_tr
            col = cc * S + qc * 128
            if cc % 2 == 0:
                t_ev = b.op("act", lambda e, col=col, pt=pt: e.copy(
                    afm_sb[:, col:col + 128], ps_tr3[pt][:]), waits=[t_tr])
            else:
                t_ev = b.op("dve", lambda e, col=col, pt=pt: e.tensor_copy(
                    afm_sb[:, col:col + 128], ps_tr3[pt][:]), waits=[t_tr])
            L3[("tr3", pt)] = t_ev
            t_afm[(qc, cc)] = t_ev
        L3[("arowfree", rb)] = last_tr

    def wo_load(fc, buf):
        toks = []
        for j in range(4):
            def fn(e, fc=fc, j=j, buf=buf):
                src = bass.AP(wo.tensor, j * 8 * 128 * D + fc * 512,
                              [[D, 128], [128 * D, 8], [1, 512]])
                dst = wo_sb[buf][:, j * 8 * 512:(j + 1) * 8 * 512].rearrange(
                    "p (c f) -> p c f", f=512)
                return e.dma_start(dst, src)
            toks.append(b.dma("pool", fn, waits=tk3(("wofree", buf))))
        return toks

    wo_tok = [None, None]
    wo_tok[0] = wo_load(0, 0)
    for fc in range(8):
        buf = fc % 2
        if fc + 1 < 8:
            wo_tok[(fc + 1) % 2] = wo_load(fc + 1, (fc + 1) % 2)
        for qc in range(NT128):
            po = qc % 2
            t_mm = None
            for cc in range(NF):
                t_mm = b.op("pe", lambda e, cc=cc, qc=qc, po=po, buf=buf: e.matmul(
                    ps_o[po][:],
                    afm_sb[:, cc * S + qc * 128: cc * S + qc * 128 + 128],
                    wo_sb[buf][:, cc * 512:(cc + 1) * 512],
                    start=(cc == 0), stop=(cc == NF - 1)),
                    waits=wo_tok[buf] + [t_afm[(qc, cc)]] + tk3(("ps_o", po)),
                    inc=(cc == NF - 1))
            if qc == NT128 - 1:
                L3[("wofree", buf)] = t_mm
            t_ev = b.op("act", lambda e, po=po: e.copy(
                o1t_sb[po][:], ps_o[po][:]),
                waits=[t_mm] + tk3(("o1t", po)))
            L3[("ps_o", po)] = t_ev
            t_st = b.dma("sp", lambda e, qc=qc, fc=fc, po=po: e.dma_start(
                o1_s[qc * 128:(qc + 1) * 128, fc * 512:(fc + 1) * 512],
                o1t_sb[po][:]), waits=[t_ev])
            L3[("o1t", po)] = t_st

    free("ps_tr3_2", "ps_tr3_1", "ps_tr3_0", "ps_o1", "ps_o0")
    free("o1t1", "o1t0", "wo1", "wo0", "arow0", "afm_sb")
    b.bar()

    hfm_sb = alloc("hfm_sb", [128, NF * S], BF16)
    o1r_sb = [alloc(f"o1r{i}", [128, D], F32) for i in range(2)]
    xr_sb = [alloc(f"xr{i}", [128, D], F32) for i in range(2)]
    hbf_t = alloc("hbft0", [128, D], BF16)
    bn_sb = [alloc(f"bn_sb{i}", [128, 48], F32) for i in range(2)]
    mv_sb = [alloc(f"mv_sb{i}", [128, 2], F32) for i in range(2)]
    sd_sb = [alloc(f"sd_sb{i}", [128, 1], F32) for i in range(2)]
    rs_sb = [alloc(f"rs_sb{i}", [128, 1], F32) for i in range(2)]
    nb4_sb = [alloc(f"nb4_sb{i}", [128, 1], F32) for i in range(2)]
    ps_h4 = [alloc(f"ps_h4_{i}", [128, 128], BF16, psum=True) for i in range(3)]

    L4 = {}

    def tk4(key):
        v = L4.get(key)
        return [] if v is None else (v if isinstance(v, list) else [v])

    for qc in range(NT128):
        rb = qc % 2
        t_lo = b.dma("sp", lambda e, qc=qc, rb=rb: e.dma_start(
            o1r_sb[rb][:], o1_s[qc * 128:(qc + 1) * 128, :]),
            waits=tk4(("o1r", rb)))
        t_lx = b.dma("pool", lambda e, qc=qc, rb=rb: e.dma_start(
            xr_sb[rb][:], xin[qc * 128:(qc + 1) * 128, :]),
            waits=tk4(("xr", rb)) + tk4(("xr2", rb)))
        ad_toks = []
        t_bn = None
        for j in range(8):
            t_ad = b.op("dve", lambda e, rb=rb, j=j: e.tensor_add(
                o1r_sb[rb][:, j * 512:(j + 1) * 512],
                o1r_sb[rb][:, j * 512:(j + 1) * 512],
                xr_sb[rb][:, j * 512:(j + 1) * 512]),
                waits=[t_lo, t_lx])
            ad_toks.append(t_ad)
            t_bn = b.op("dve", lambda e, rb=rb, j=j: e.bn_stats(
                bn_sb[rb][:, j * 6:(j + 1) * 6],
                o1r_sb[rb][:, j * 512:(j + 1) * 512]),
                waits=[t_ad] + (tk4(("bn", rb)) if j == 0 else []))
        L4[("xr", rb)] = ad_toks
        t_ag = b.op("dve", lambda e, rb=rb: e.bn_aggr(mv_sb[rb][:], bn_sb[rb][:]),
                    waits=[t_bn])
        L4[("bn", rb)] = t_ag
        t_sd = b.op("act", lambda e, rb=rb: e.activation(
            sd_sb[rb][:], mv_sb[rb][:, 1:2], AF.Sqrt, bias=eps_sb[:]),
            waits=[t_ag, t_eps] + tk4(("sd", rb)))
        t_rs = b.op("dve", lambda e, rb=rb: e.reciprocal(rs_sb[rb][:], sd_sb[rb][:]),
                    waits=[t_sd] + tk4(("rs", rb)))
        L4[("sd", rb)] = t_rs
        t_nb = b.op("dve", lambda e, rb=rb: e.scalar_tensor_tensor(
            nb4_sb[rb][:], mv_sb[rb][:, 0:1], -1.0, rs_sb[rb][:],
            op0=ALU.mult, op1=ALU.mult),
            waits=[t_rs] + tk4(("nb", rb)))
        L4[("rs", rb)] = t_nb
        t_h = b.op("act", lambda e, rb=rb: e.activation(
            xr_sb[rb][:], o1r_sb[rb][:], AF.Identity,
            bias=nb4_sb[rb][:], scale=rs_sb[rb][:]),
            waits=[t_nb] + tk4(("xr", rb)))
        L4[("nb", rb)] = t_h
        t_cb = b.op("act", lambda e, rb=rb: e.copy(hbf_t[:], xr_sb[rb][:]),
                    waits=[t_h] + tk4(("hbft",)))
        t_s1 = b.dma("sp", lambda e, qc=qc, rb=rb: e.dma_start(
            h_s[qc * 128:(qc + 1) * 128, :], xr_sb[rb][:]), waits=[t_h])
        L4[("o1r", rb)] = t_h
        L4[("xr2", rb)] = [t_cb, t_s1]
        t_tr_last = None
        for g in range(NF):
            pt = (qc * NF + g) % 3
            t_tr = b.op("pe", lambda e, g=g, pt=pt: e.transpose(
                ps_h4[pt][:], hbf_t[:, g * 128:(g + 1) * 128], ident_sb[:]),
                waits=[t_cb, t_ident] + tk4(("h4ps", pt)))
            t_tr_last = t_tr
            col = g * S + qc * 128
            if g % 2 == 0:
                t_ev = b.op("act", lambda e, col=col, pt=pt: e.copy(
                    hfm_sb[:, col:col + 128], ps_h4[pt][:]), waits=[t_tr])
            else:
                t_ev = b.op("dve", lambda e, col=col, pt=pt: e.tensor_copy(
                    hfm_sb[:, col:col + 128], ps_h4[pt][:]), waits=[t_tr])
            L4[("h4ps", pt)] = t_ev
        L4[("hbft",)] = t_tr_last

    free("ps_h4_2", "ps_h4_1", "ps_h4_0")
    free("nb4_sb1", "nb4_sb0", "rs_sb1", "rs_sb0", "sd_sb1", "sd_sb0",
         "mv_sb1", "mv_sb0", "bn_sb1", "bn_sb0", "hbft0",
         "xr1", "xr0", "o1r1", "o1r0")
    b.bar()

    grawq_sb = [alloc(f"grawq{i}", [128, D], BF16) for i in range(3)]
    sc5_sb = [[alloc(f"sc5_{m}_{i}", [128, NF], F32) for i in range(2)]
              for m in range(2)]
    zc5_sb = [[alloc(f"zc5_{m}_{i}", [128, NF], F32) for i in range(2)]
              for m in range(2)]
    zs5_sb = [[alloc(f"zs5_{m}_{i}", [128, NF], F32) for i in range(2)]
              for m in range(2)]
    wn5_sb = alloc("wn5_sb", [128, NF * 128], BF16)
    wslab5_sb = [alloc(f"wslab5_{m}", [128, NF * 128], BF16) for m in range(2)]
    gtile5_sb = alloc("gtile5_sb", [128, S], BF16)
    hgt5_sb = [alloc(f"hgt5_{i}", [128, S], BF16) for i in range(2)]
    ps_g5 = [alloc(f"ps_g5_{i}", [128, 512], F32, psum=True) for i in range(2)]
    ps_u5 = [alloc(f"ps_u5_{i}", [128, 512], F32, psum=True) for i in range(2)]
    ps_t5 = [alloc(f"ps_t5_{i}", [128, 128], BF16, psum=True) for i in range(4)]

    L5 = {}

    def tk5(key):
        v = L5.get(key)
        return [] if v is None else (v if isinstance(v, list) else [v])

    MATS5 = [("g", gq, gs, gz), ("u", uq, us, uz)]
    raw5_ctr = [0]
    tr5_ctr = [0]

    def emit_gu_scales(ib, m):
        ip = ib % 2
        _, qw, sw, zw = MATS5[m]
        t_s = b.dma("sp", lambda e, sw=sw, ib=ib, m=m, ip=ip: e.dma_start(
            sc5_sb[m][ip][:], sw[ib * 128:(ib + 1) * 128, :]),
            waits=tk5(("scuse5", m, ip)))
        t_z = b.dma("sp", lambda e, zw=zw, ib=ib, m=m, ip=ip: e.dma_start(
            zc5_sb[m][ip][:], zw[ib * 128:(ib + 1) * 128, :]),
            waits=tk5(("scuse5", m, ip)))
        t_zs = b.op("dve", lambda e, m=m, ip=ip: e.tensor_mul(
            zs5_sb[m][ip][:], zc5_sb[m][ip][:], sc5_sb[m][ip][:]),
            waits=[t_s, t_z] + tk5(("scuse5", m, ip)))
        L5[("scld5", m, ip)] = [t_s, t_zs]

    def emit_gu_raw(ib, m):
        slot = raw5_ctr[0] % 3
        raw5_ctr[0] += 1
        _, qw, _, _ = MATS5[m]
        toks = []
        for q4 in range(4):
            toks.append(b.dma("pool", lambda e, qw=qw, ib=ib, slot=slot, q4=q4: e.dma_start(
                grawq_sb[slot][:, q4 * 1024:(q4 + 1) * 1024],
                qw[ib * 128:(ib + 1) * 128, q4 * 1024:(q4 + 1) * 1024]),
                waits=tk5(("rawfree5", slot))))
        return slot, toks

    def production5(ib, m, rawslot, rawtoks):
        ip = ib % 2
        sc_last = None
        for g in range(NF):
            pslot = tr5_ctr[0] % 4
            tr5_ctr[0] += 1
            t_sc = b.op("dve", lambda e, m=m, ip=ip, g=g, rawslot=rawslot: e.tensor_scalar(
                wn5_sb[:, g * 128:(g + 1) * 128],
                grawq_sb[rawslot][:, g * 128:(g + 1) * 128],
                sc5_sb[m][ip][:, g:g + 1], zs5_sb[m][ip][:, g:g + 1],
                op0=ALU.mult, op1=ALU.subtract),
                waits=[rawtoks[g // 8]] + tk5(("scld5", m, ip)) + tk5(("wn5", g)))
            sc_last = t_sc
            L5[("scuse5", m, ip)] = t_sc
            t_tr = b.op("pe", lambda e, g=g, pslot=pslot: e.transpose(
                ps_t5[pslot][:],
                wn5_sb[:, g * 128:(g + 1) * 128], ident_sb[:]),
                waits=[t_sc, t_ident] + tk5(("t5ps", pslot)))
            L5[("wn5", g)] = t_tr
            if g % 2 == 0:
                t_ev = b.op("act", lambda e, m=m, g=g, pslot=pslot: e.copy(
                    wslab5_sb[m][:, g * 128:(g + 1) * 128], ps_t5[pslot][:]),
                    waits=[t_tr] + tk5(("mmlast5", m)))
            else:
                t_ev = b.op("dve", lambda e, m=m, g=g, pslot=pslot: e.tensor_copy(
                    wslab5_sb[m][:, g * 128:(g + 1) * 128], ps_t5[pslot][:]),
                    waits=[t_tr] + tk5(("mmlast5", m)))
            L5[("t5ps", pslot)] = t_ev
            L5[("slabrdy5", m, g)] = t_ev
            yield
        L5[("rawfree5", rawslot)] = sc_last

    def emit_gu_matmuls(ib, m, prod):
        psb = ps_g5 if m == 0 else ps_u5
        hb5 = (ib % 2)
        step = 0
        for tc in range(NTC):
            pp = tc % 2
            t_mm = None
            for g in range(NF):
                t_mm = b.op("pe", lambda e, m=m, g=g, tc=tc, pp=pp, psb=psb: e.matmul(
                    psb[pp][:],
                    wslab5_sb[m][:, g * 128:(g + 1) * 128],
                    hfm_sb[:, g * S + tc * 512: g * S + tc * 512 + 512],
                    start=(g == 0), stop=(g == NF - 1)),
                    waits=tk5(("slabrdy5", m, g)) +
                          (tk5(("psfree5", m, pp)) if g == 0 else []),
                    inc=(g == NF - 1))
                step += 1
                if prod is not None and step % 4 == 0:
                    next(prod, None)
            L5[("mmlast5", m)] = t_mm
            if m == 0:
                t_si = b.op("act", lambda e, tc=tc, pp=pp: e.activation(
                    gtile5_sb[:, tc * 512:(tc + 1) * 512], ps_g5[pp][:], AF.Silu),
                    waits=[t_mm] + tk5(("gtfree", tc)))
                L5[("psfree5", 0, pp)] = t_si
                L5[("gtw", tc)] = t_si
            else:
                t_mul = b.op("dve", lambda e, tc=tc, pp=pp, hb5=hb5: e.tensor_mul(
                    hgt5_sb[hb5][:, tc * 512:(tc + 1) * 512],
                    ps_u5[pp][:], gtile5_sb[:, tc * 512:(tc + 1) * 512]),
                    waits=[t_mm] + tk5(("gtw", tc)) + tk5(("hgt5", hb5)))
                L5[("psfree5", 1, pp)] = t_mul
                L5[("gtfree", tc)] = t_mul
                L5[("mulw", tc)] = t_mul
        if m == 1:
            t_hg = b.dma("sp", lambda e, ib=ib, hb5=hb5: e.dma_start(
                hg_s[ib * 128:(ib + 1) * 128, :], hgt5_sb[hb5][:]),
                waits=[L5[("mulw", tc2)] for tc2 in range(NTC)])
            L5[("hgt5", hb5)] = t_hg

    order5 = [(ib, m) for ib in range(NIB) for m in range(2)]
    emit_gu_scales(0, 0)
    s0, t0 = emit_gu_raw(0, 0)
    for _ in production5(0, 0, s0, t0):
        pass
    pend_raw5 = {}
    if len(order5) > 1:
        emit_gu_scales(*order5[1])
        pend_raw5[1] = emit_gu_raw(*order5[1])
    for idx, (ib, m) in enumerate(order5):
        if idx + 2 < len(order5):
            emit_gu_scales(*order5[idx + 2])
            pend_raw5[idx + 2] = emit_gu_raw(*order5[idx + 2])
        if idx + 1 < len(order5):
            s2, t2 = pend_raw5.pop(idx + 1)
            nxt = production5(order5[idx + 1][0], order5[idx + 1][1], s2, t2)
        else:
            nxt = None
        emit_gu_matmuls(ib, m, nxt)
        if nxt is not None:
            for _ in nxt:
                pass

    free("ps_t5_3", "ps_t5_2", "ps_t5_1", "ps_t5_0", "ps_u5_1", "ps_u5_0",
         "ps_g5_1", "ps_g5_0")
    free("hgt5_1", "hgt5_0", "gtile5_sb", "wslab5_1", "wslab5_0", "wn5_sb")
    free(*[f"zs5_{m}_{i}" for m in (1, 0) for i in (1, 0)])
    free(*[f"zc5_{m}_{i}" for m in (1, 0) for i in (1, 0)])
    free(*[f"sc5_{m}_{i}" for m in (1, 0) for i in (1, 0)])
    free("grawq2", "grawq1", "grawq0")
    free("hfm_sb")
    b.bar()

    IC_CH = [22, 22, 21, 21]
    IC_OFF = [0, 22, 44, 65]
    MAXC = 22
    NHG = 8

    ident32_sb = alloc("ident32_sb", [128, 128], F32)
    rawd_sb = [[alloc(f"rawd{p}_{j}", [128, MAXC * 128], BF16) for j in range(4)]
               for p in range(2)]
    dsc6_sb = [[alloc(f"dsc6_{p}_{j}", [128, NIB], F32) for j in range(4)]
               for p in range(2)]
    dzc6_sb = [[alloc(f"dzc6_{p}_{j}", [128, NIB], F32) for j in range(4)]
               for p in range(2)]
    dzs6_sb = [[alloc(f"dzs6_{p}_{j}", [128, NIB], F32) for j in range(4)]
               for p in range(2)]
    wnat6_sb = alloc("wnat6_sb", [128, 8 * 128], BF16)
    wdts_sb = [alloc(f"wdts{p}", [128, MAXC * 512], BF16) for p in range(2)]
    NHGT = 8
    hgt6_sb = [alloc(f"hgt6_{p}", [128, 512], BF16) for p in range(NHGT)]
    o2acc_sb = [alloc(f"o2acc_{j}", [128, NTC * 512], F32) for j in range(4)]
    o2a_sb = [alloc(f"o2a_{p}", [128, 2048], F32) for p in range(2)]
    ps_acc = [alloc(f"ps_acc{j}", [128, 512], F32, psum=True) for j in range(4)]
    ps_wtr = [alloc(f"ps_wtr{i}", [128, 128], BF16, psum=True) for i in range(4)]

    t_id32 = b.dma("sp", lambda e: e.dma_start(ident32_sb[:], ident32_c[:]))

    L6 = {}

    def tk6(key):
        v = L6.get(key)
        return [] if v is None else (v if isinstance(v, list) else [v])

    wtr_ctr = [0]
    hgl_ctr = [0]

    def emit_hg_scales(hg):
        hp = hg % 2
        for j in range(4):
            r0 = (hg * 4 + j) * 128
            t_s = b.dma("act", lambda e, r0=r0, hp=hp, j=j: e.dma_start(
                dsc6_sb[hp][j][:], ds[r0:r0 + 128, :]),
                waits=tk6(("scuse", hp, j)))
            t_z = b.dma("act", lambda e, r0=r0, hp=hp, j=j: e.dma_start(
                dzc6_sb[hp][j][:], dz[r0:r0 + 128, :]),
                waits=tk6(("scuse", hp, j)))
            t_zs = b.op("dve", lambda e, hp=hp, j=j: e.tensor_mul(
                dzs6_sb[hp][j][:], dzc6_sb[hp][j][:], dsc6_sb[hp][j][:]),
                waits=[t_s, t_z] + tk6(("scuse", hp, j)))
            L6[("scld", hp, j)] = [t_s, t_zs]

    def emit_raw6(hg, c):
        p = c % 2
        cw = IC_CH[c]
        i0 = IC_OFF[c]
        raw_toks = []
        for j in range(4):
            r0 = (hg * 4 + j) * 128

            def fn(e, r0=r0, p=p, j=j, i0=i0, cw=cw):
                return e.dma_start(
                    rawd_sb[p][j][:, 0:cw * 128],
                    dq[r0:r0 + 128, i0 * 128:(i0 + cw) * 128])
            raw_toks.append(b.dma("pool", fn, waits=tk6(("rawfree", p, j))))
        return raw_toks

    def production_steps(hg, c, raw_toks):
        hp = hg % 2
        p = c % 2
        cw = IC_CH[c]
        i0 = IC_OFF[c]
        sc_last = {j: None for j in range(4)}
        pend_ev = []

        def flush_evacs():
            while pend_ev:
                j2, il2, pslot, t_tr = pend_ev.pop(0)
                col = il2 * 512 + j2 * 128
                if (il2 * 4 + j2) % 2 == 0:
                    t_ev = b.op("act", lambda e, p=p, col=col, pslot=pslot: e.copy(
                        wdts_sb[p][:, col:col + 128], ps_wtr[pslot][:]),
                        waits=[t_tr] + tk6(("mmlast", p)))
                else:
                    t_ev = b.op("dve", lambda e, p=p, col=col, pslot=pslot: e.tensor_copy(
                        wdts_sb[p][:, col:col + 128], ps_wtr[pslot][:]),
                        waits=[t_tr] + tk6(("mmlast", p)))
                L6[("wtrps", pslot)] = t_ev
                L6[("slabrdy", p, il2, j2)] = t_ev

        for il in range(cw):
            i = i0 + il
            flush_evacs()
            for j in range(4):
                wslot = wtr_ctr[0] % 8
                pslot = wtr_ctr[0] % 4
                wtr_ctr[0] += 1
                t_sc = b.op("dve", lambda e, p=p, j=j, il=il, i=i, wslot=wslot, hp=hp: e.tensor_scalar(
                    wnat6_sb[:, wslot * 128:(wslot + 1) * 128],
                    rawd_sb[p][j][:, il * 128:(il + 1) * 128],
                    dsc6_sb[hp][j][:, i:i + 1], dzs6_sb[hp][j][:, i:i + 1],
                    op0=ALU.mult, op1=ALU.subtract),
                    waits=[raw_toks[j]] + tk6(("scld", hp, j)) + tk6(("wnslot", wslot)))
                sc_last[j] = t_sc
                L6[("scuse", hp, j)] = t_sc
                t_tr = b.op("pe", lambda e, wslot=wslot, pslot=pslot: e.transpose(
                    ps_wtr[pslot][:],
                    wnat6_sb[:, wslot * 128:(wslot + 1) * 128], ident_sb[:]),
                    waits=[t_sc, t_ident] + tk6(("wtrps", pslot)))
                L6[("wnslot", wslot)] = t_tr
                pend_ev.append((j, il, pslot, t_tr))
            yield
        flush_evacs()
        for j in range(4):
            L6[("rawfree", p, j)] = sc_last[j]

    def emit_chunk_matmuls(hg, c, prod):
        p = c % 2
        cw = IC_CH[c]
        i0 = IC_OFF[c]
        seq = [(tc, il) for tc in range(NTC) for il in range(cw)]
        AHEAD = 6
        hl_toks = {}

        def issue_load(k):
            tc, il = seq[k]
            i = i0 + il
            hb = hgl_ctr[0] % NHGT
            hgl_ctr[0] += 1
            hl_toks[k] = (hb, b.dma("sp", lambda e, i=i, tc=tc, hb=hb: e.dma_start(
                hgt6_sb[hb][:],
                hg_s[i * 128:(i + 1) * 128, tc * 512:(tc + 1) * 512]),
                waits=tk6(("hgfree", hb))))

        for k in range(min(AHEAD, len(seq))):
            issue_load(k)
        mm_last = None
        mm_toks = {}
        for k, (tc, il) in enumerate(seq):
            if k + AHEAD < len(seq):
                issue_load(k + AHEAD)
            hb, t_hl = hl_toks.pop(k)
            for j in range(4):
                do_inc = (j == 3) or (il == cw - 1)
                t_mm = b.op("pe", lambda e, p=p, il=il, j=j, hb=hb, cw=cw: e.matmul(
                    ps_acc[j][:],
                    wdts_sb[p][:, il * 512 + j * 128: il * 512 + j * 128 + 128],
                    hgt6_sb[hb][:],
                    start=(il == 0), stop=(il == cw - 1)),
                    waits=[t_hl] + tk6(("slabrdy", p, il, j)) +
                          (tk6(("psfree", j)) if il == 0 else []),
                    inc=do_inc)
                if il == cw - 1:
                    mm_toks[j] = t_mm
                if j == 3:
                    L6[("hgfree", hb)] = t_mm
                    mm_last = t_mm
            if prod is not None:
                next(prod, None)
            if il == cw - 1:
                for j in range(4):
                    if c == 0:
                        t_fl = b.op("dve", lambda e, j=j, tc=tc: e.tensor_copy(
                            o2acc_sb[j][:, tc * 512:(tc + 1) * 512], ps_acc[j][:]),
                            waits=[mm_toks[j]] + tk6(("o2accfree", j, tc)))
                    else:
                        t_fl = b.op("dve", lambda e, j=j, tc=tc: e.tensor_add(
                            o2acc_sb[j][:, tc * 512:(tc + 1) * 512],
                            o2acc_sb[j][:, tc * 512:(tc + 1) * 512], ps_acc[j][:]),
                            waits=[mm_toks[j]])
                    L6[("psfree", j)] = t_fl
                    if c == 3:
                        L6[("accdone", j, tc)] = t_fl
        L6[("mmlast", p)] = mm_last

    def emit_hg_output(hg):
        otr_last = {}
        for tc in range(NTC):
            op_ = (hg * NTC + tc) % 2
            ev_toks = []
            for j in range(4):
                for ts_ in range(4):
                    po = (j + ts_) % 4
                    src_off = tc * 512 + ts_ * 128
                    t_tr = b.op("pe", lambda e, j=j, po=po, src_off=src_off: e.transpose(
                        ps_acc[po][:, 0:128],
                        o2acc_sb[j][:, src_off:src_off + 128], ident32_sb[:]),
                        waits=tk6(("accdone", j, tc)) + [t_id32] +
                              tk6(("otrps", po)) + tk6(("psfree", po)))
                    dcol = ts_ * 512 + j * 128
                    if (j * 4 + ts_) % 2 == 0:
                        t_ev = b.op("act", lambda e, op_=op_, dcol=dcol, po=po: e.copy(
                            o2a_sb[op_][:, dcol:dcol + 128],
                            ps_acc[po][:, 0:128]),
                            waits=[t_tr] + tk6(("o2afree", op_)))
                    else:
                        t_ev = b.op("dve", lambda e, op_=op_, dcol=dcol, po=po: e.tensor_copy(
                            o2a_sb[op_][:, dcol:dcol + 128],
                            ps_acc[po][:, 0:128]),
                            waits=[t_tr] + tk6(("o2afree", op_)))
                    L6[("otrps", po)] = t_ev
                    otr_last[po] = t_ev
                    ev_toks.append(t_ev)
                    if ts_ == 3:
                        L6[("o2accfree", j, tc)] = t_tr
            dst = bass.AP(o2_s.tensor, (tc * 512) * D + hg * 512,
                          [[D, 128], [128 * D, 4], [1, 512]])
            src = o2a_sb[op_][:].rearrange("p (s h) -> p s h", h=512)
            t_st = b.dma("act", lambda e, dst=dst, src=src: e.dma_start(dst, src),
                         waits=ev_toks)
            L6[("o2afree", op_)] = t_st
        for po in range(4):
            L6[("psfree", po)] = otr_last[po]

    chunk_seq = [(hg, c) for hg in range(NHG) for c in range(4)]
    emit_hg_scales(0)
    for _ in production_steps(0, 0, emit_raw6(0, 0)):
        pass
    pend_raw6 = {}
    if len(chunk_seq) > 1:
        pend_raw6[1] = emit_raw6(*chunk_seq[1])
    for ci, (hg, c) in enumerate(chunk_seq):
        if ci + 2 < len(chunk_seq):
            hg2, c2 = chunk_seq[ci + 2]
            if c2 == 0:
                emit_hg_scales(hg2)
            pend_raw6[ci + 2] = emit_raw6(hg2, c2)
        if ci + 1 < len(chunk_seq):
            hg1, c1 = chunk_seq[ci + 1]
            nxt = production_steps(hg1, c1, pend_raw6.pop(ci + 1))
        else:
            nxt = None
        emit_chunk_matmuls(hg, c, nxt)
        if nxt is not None:
            for _ in nxt:
                pass
        if c == 3:
            emit_hg_output(hg)

    free("ps_wtr3", "ps_wtr2", "ps_wtr1", "ps_wtr0",
         "ps_acc3", "ps_acc2", "ps_acc1", "ps_acc0")
    free("o2a_1", "o2a_0", "o2acc_3", "o2acc_2", "o2acc_1", "o2acc_0",
         *[f"hgt6_{i}" for i in range(NHGT - 1, -1, -1)],
         "wdts1", "wdts0", "wnat6_sb")
    free(*[f"dzs6_{p}_{j}" for p in (1, 0) for j in (3, 2, 1, 0)][::1])
    free(*[f"dzc6_{p}_{j}" for p in (1, 0) for j in (3, 2, 1, 0)])
    free(*[f"dsc6_{p}_{j}" for p in (1, 0) for j in (3, 2, 1, 0)])
    free(*[f"rawd{p}_{j}" for p in (1, 0) for j in (3, 2, 1, 0)])
    free("ident32_sb")
    b.bar()

    o2r_sb = [alloc(f"o2r{i}", [128, D], F32) for i in range(2)]
    hr_sb = [alloc(f"hr{i}", [128, D], F32) for i in range(2)]
    bn7_sb = [alloc(f"bn7_sb{i}", [128, 48], F32) for i in range(2)]
    mv7_sb = [alloc(f"mv7_sb{i}", [128, 2], F32) for i in range(2)]
    sd7_sb = [alloc(f"sd7_sb{i}", [128, 1], F32) for i in range(2)]
    rs7_sb = [alloc(f"rs7_sb{i}", [128, 1], F32) for i in range(2)]
    nb7_sb = [alloc(f"nb7_sb{i}", [128, 1], F32) for i in range(2)]

    L7 = {}

    def tk7(key):
        v = L7.get(key)
        return [] if v is None else (v if isinstance(v, list) else [v])

    for qc in range(NT128):
        rb = qc % 2
        t_lo = b.dma("sp", lambda e, qc=qc, rb=rb: e.dma_start(
            o2r_sb[rb][:], o2_s[qc * 128:(qc + 1) * 128, :]),
            waits=tk7(("o2r", rb)))
        t_lh = b.dma("pool", lambda e, qc=qc, rb=rb: e.dma_start(
            hr_sb[rb][:], h_s[qc * 128:(qc + 1) * 128, :]),
            waits=tk7(("hr", rb)) + tk7(("hr2", rb)))
        ad_toks = []
        t_bn = None
        for j in range(8):
            t_ad = b.op("dve", lambda e, rb=rb, j=j: e.tensor_add(
                o2r_sb[rb][:, j * 512:(j + 1) * 512],
                o2r_sb[rb][:, j * 512:(j + 1) * 512],
                hr_sb[rb][:, j * 512:(j + 1) * 512]),
                waits=[t_lo, t_lh])
            ad_toks.append(t_ad)
            t_bn = b.op("dve", lambda e, rb=rb, j=j: e.bn_stats(
                bn7_sb[rb][:, j * 6:(j + 1) * 6],
                o2r_sb[rb][:, j * 512:(j + 1) * 512]),
                waits=[t_ad] + (tk7(("bn7", rb)) if j == 0 else []))
        L7[("hr", rb)] = ad_toks
        t_ag = b.op("dve", lambda e, rb=rb: e.bn_aggr(mv7_sb[rb][:], bn7_sb[rb][:]),
                    waits=[t_bn])
        L7[("bn7", rb)] = t_ag
        t_sd = b.op("act", lambda e, rb=rb: e.activation(
            sd7_sb[rb][:], mv7_sb[rb][:, 1:2], AF.Sqrt, bias=eps_sb[:]),
            waits=[t_ag, t_eps] + tk7(("sd7", rb)))
        t_rs = b.op("dve", lambda e, rb=rb: e.reciprocal(rs7_sb[rb][:], sd7_sb[rb][:]),
                    waits=[t_sd] + tk7(("rs7", rb)))
        L7[("sd7", rb)] = t_rs
        t_nb = b.op("dve", lambda e, rb=rb: e.scalar_tensor_tensor(
            nb7_sb[rb][:], mv7_sb[rb][:, 0:1], -1.0, rs7_sb[rb][:],
            op0=ALU.mult, op1=ALU.mult),
            waits=[t_rs] + tk7(("nb7", rb)))
        L7[("rs7", rb)] = t_nb
        t_no = b.op("act", lambda e, rb=rb: e.activation(
            hr_sb[rb][:], o2r_sb[rb][:], AF.Identity,
            bias=nb7_sb[rb][:], scale=rs7_sb[rb][:]),
            waits=[t_nb] + tk7(("hr", rb)))
        L7[("nb7", rb)] = t_no
        t_st = b.dma("sp", lambda e, qc=qc, rb=rb: e.dma_start(
            out[qc * 128:(qc + 1) * 128, :], hr_sb[rb][:]), waits=[t_no])
        L7[("o2r", rb)] = t_no
        L7[("hr2", rb)] = t_st

    if DEBUG_OUTS:
        b.wait("pool", [(s, b.cnt[s]) for s in SEMS if b.cnt[s] > 0])
        dbg = {
            "dbg_qs": (qs_s, [D, S], BF16),
            "dbg_ks": (ks_s, [HKV * DH, S], BF16),
            "dbg_as": (as_s, [S, D], BF16),
            "dbg_o1": (o1_s, [S, D], F32),
            "dbg_h": (h_s, [S, D], F32),
            "dbg_hg": (hg_s, [I_FF, S], BF16),
            "dbg_hbf": (hbf_s, [S, D], BF16),
            "dbg_wg": (wgdq_s, [I_FF, D], BF16),
            "dbg_wu": (wudq_s, [I_FF, D], BF16),
            "dbg_o2": (o2_s, [S, D], F32),
        }
        for name, (srcap, shp, dt) in dbg.items():
            dout = nc.dram_tensor(name, shp, dt, kind="ExternalOutput").ap()
            b.dma("pool", lambda e, dout=dout, srcap=srcap: e.dma_start(
                dout[:], srcap[:]))

    b.wait("sp", [(s, b.cnt[s]) for s in SEMS if b.cnt[s] > 0])

    import contextlib
    with contextlib.ExitStack() as es:
        for s in SEMS:
            b.sem[s] = es.enter_context(nc.semaphore(s))
        block = es.enter_context(nc.Block())
        b.replay(block)
    for n, cm in reversed(ctx_stack):
        cm.__exit__(None, None, None)
    ctx_stack.clear()
    return nc


def make_inputs(inputs, S):
    x = np.ascontiguousarray(np.asarray(inputs["L0_attn_x"], dtype=np.float32))
    qperm = _q_perm(HQ)
    kperm = _q_perm(HKV)
    base = {
        "wq": np.ascontiguousarray(np.asarray(inputs["L0_attn_wq"], dtype=np.float32)[:, qperm]),
        "wk": np.ascontiguousarray(np.asarray(inputs["L0_attn_wk"], dtype=np.float32)[:, kperm]),
        "wv": np.ascontiguousarray(np.asarray(inputs["L0_attn_wv"], dtype=np.float32)),
        "wo": np.ascontiguousarray(np.asarray(inputs["L0_attn_wo"], dtype=np.float32)),
        "gq": np.ascontiguousarray(np.asarray(inputs["L0_ffn_gate_qweight"], dtype=np.int32)),
        "gs": np.ascontiguousarray(np.asarray(inputs["L0_ffn_gate_scales"], dtype=np.float32)),
        "gz": np.ascontiguousarray(np.asarray(inputs["L0_ffn_gate_zeros"], dtype=np.float32)),
        "uq": np.ascontiguousarray(np.asarray(inputs["L0_ffn_up_qweight"], dtype=np.int32)),
        "us": np.ascontiguousarray(np.asarray(inputs["L0_ffn_up_scales"], dtype=np.float32)),
        "uz": np.ascontiguousarray(np.asarray(inputs["L0_ffn_up_zeros"], dtype=np.float32)),
        "dq": np.ascontiguousarray(np.asarray(inputs["L0_ffn_down_qweight"], dtype=np.int32)),
        "ds": np.ascontiguousarray(np.asarray(inputs["L0_ffn_down_scales"], dtype=np.float32)),
        "dz": np.ascontiguousarray(np.asarray(inputs["L0_ffn_down_zeros"], dtype=np.float32)),
    }
    base.update(_host_consts(S))
    return x, base


def kernel(**inputs):
    x, base = make_inputs(inputs, 2048)
    B, S, _ = x.shape
    assert B == 8 and S == 2048
    nc = build_program(S)
    in_maps = []
    for c in range(8):
        m = dict(base)
        m["xin"] = np.ascontiguousarray(x[c])
        in_maps.append(m)
    res = run_bass_kernel_spmd(nc, in_maps, core_ids=list(range(8)))
    return np.stack([r["out"] for r in res.results], axis=0)


# version 18 (final)
# speedup vs baseline: 1.2389x; 1.0542x over previous
import math
import numpy as np
import ml_dtypes
import concourse.bass as bass
import concourse.mybir as mybir
from concourse.bass_utils import run_bass_kernel_spmd

F32 = mybir.dt.float32
BF16 = mybir.dt.bfloat16
I32 = mybir.dt.int32
AF = mybir.ActivationFunctionType
ALU = mybir.AluOpType
AX = mybir.AxisListType

D = 4096
NF = 32
HQ = 32
HKV = 8
DH = 128
I_FF = 11008
NIB = 86
RSQ = 1.0 / math.sqrt(128.0)
EPS = 1e-5
DEBUG_OUTS = False

ENGINES = ("pe", "dve", "act", "pool", "sp")
NDMASEM = 8
DMA_QUEUES = ("qsp", "qact", "qpool")
SEMS = ("pe", "dve", "act", "pool") + tuple(
    f"{q}{r}" for q in DMA_QUEUES for r in range(NDMASEM))
DMA_SEM = {"sp": "qsp", "act": "qact", "pool": "qpool"}


def _flat(toks):
    out = []
    for t in toks:
        if t is None:
            continue
        if isinstance(t, list):
            out.extend(x for x in t if x is not None)
        else:
            out.append(t)
    return out


class _B:

    def __init__(self, nc):
        self.nc = nc
        self.ops = {e: [] for e in ENGINES}
        self.cnt = {s: 0 for s in SEMS}
        self.cache = {e: {s: 0 for s in SEMS} for e in ENGINES}
        self.pend_bar = {e: None for e in ENGINES}
        self.sem = {}
        self.n_inst = 0
        self.dma_rot = {q: 0 for q in DMA_QUEUES}

    def _waits(self, eng, waits):
        out = []
        bar = self.pend_bar[eng]
        if bar is not None:
            self.pend_bar[eng] = None
            for s, v in bar:
                if s == eng:
                    continue
                out.append((s, v))
        out.extend(_flat(list(waits)))
        ded = []
        for s, v in out:
            if v > self.cache[eng][s]:
                self.cache[eng][s] = v
                ded.append((s, v))
        return ded

    def op(self, eng, fn, waits=(), inc=True):
        ded = self._waits(eng, waits)
        tok = None
        incs = []
        if inc:
            self.cnt[eng] += 1
            tok = (eng, self.cnt[eng])
            incs = [(eng, 1)]
        self.ops[eng].append((ded, fn, incs))
        self.n_inst += 1 + len(ded)
        return tok

    def dma(self, eng, fn, waits=()):
        q = DMA_SEM[eng]
        r = self.dma_rot[q] % NDMASEM
        self.dma_rot[q] += 1
        s = f"{q}{r}"
        ded = self._waits(eng, list(waits) + ([(s, self.cnt[s])] if self.cnt[s] else []))
        self.cnt[s] += 16
        tok = (s, self.cnt[s])
        self.ops[eng].append((ded, fn, [(s, 16)]))
        self.n_inst += 1 + len(ded)
        return tok

    def wait(self, eng, waits):
        ded = self._waits(eng, waits)
        if ded:
            self.ops[eng].append((ded, None, []))
            self.n_inst += len(ded)

    def bar(self):
        snap = [(s, self.cnt[s]) for s in SEMS if self.cnt[s] > 0]
        for e in ENGINES:
            self.pend_bar[e] = snap

    def replay(self, block):
        def mk(eng):
            def run(e):
                for ded, fn, incs in self.ops[eng]:
                    for s, v in ded:
                        e.wait_ge(self.sem[s], v)
                    if fn is not None:
                        inst = fn(e)
                        for s, a in incs:
                            inst.then_inc(self.sem[s], a)
            return run

        block.tensor(mk("pe"))
        block.vector(mk("dve"))
        block.scalar(mk("act"))
        block.gpsimd(mk("pool"))
        block.sync(mk("sp"))


def _q_perm(nheads):
    p = []
    for h in range(nheads):
        base = h * DH
        p.extend(base + j for j in range(0, DH, 2))
        p.extend(base + j for j in range(1, DH, 2))
    return np.array(p, dtype=np.int64)


def _host_consts(S):
    inv = 1.0 / (10000.0 ** (np.arange(0, DH, 2, dtype=np.float64) / DH))
    t = np.arange(S, dtype=np.float64)
    ang = np.outer(inv, t)
    cos = np.cos(ang).astype(np.float32)
    sin = np.sin(ang).astype(np.float32)
    cp = np.concatenate([cos, cos], axis=0)
    sp = np.concatenate([-sin, sin], axis=0)
    swap = np.zeros((128, 128), dtype=np.float32)
    for i in range(128):
        swap[i, (i + 64) % 128] = 1.0
    masks = np.zeros((4, 128, 512), dtype=np.float32)
    ql = np.arange(512)
    kl = np.arange(128)
    for off in range(4):
        masks[off] = (ql[None, :] >= off * 128 + kl[:, None]).astype(np.float32)
    bf = ml_dtypes.bfloat16
    return {
        "cp_c": cp.astype(bf), "sp_c": sp.astype(bf),
        "swap_c": swap.astype(bf),
        "mask_c": masks.astype(bf),
        "onec_c": np.ones((128, 1), dtype=np.float32).astype(bf),
        "oner_c": np.ones((1, 128), dtype=np.float32).astype(bf),
        "ident_c": np.eye(128, dtype=np.float32).astype(bf),
        "ident32_c": np.eye(128, dtype=np.float32),
    }


def build_program(S):
    assert S in (512, 1024, 2048)
    NTC = S // 512
    NQB = NTC
    NT128 = S // 128
    NQ = HQ * NQB
    import os
    NHALF = 2 if S >= 1024 else 1
    if os.environ.get("KNHALF"):
        NHALF = int(os.environ["KNHALF"])
        assert S // NHALF >= 512
    HALF_BAR = bool(os.environ.get("KHALFBAR"))
    TH = S // NHALF
    NTC_H = TH // 512

    nc = bass.Bass("TRN2", target_bir_lowering=False)

    xin = nc.dram_tensor("xin", [S, D], F32, kind="ExternalInput").ap()
    wq = nc.dram_tensor("wq", [D, D], F32, kind="ExternalInput").ap()
    wk = nc.dram_tensor("wk", [D, HKV * DH], F32, kind="ExternalInput").ap()
    wv = nc.dram_tensor("wv", [D, HKV * DH], F32, kind="ExternalInput").ap()
    wo = nc.dram_tensor("wo", [D, D], F32, kind="ExternalInput").ap()
    gq = nc.dram_tensor("gq", [I_FF, D], I32, kind="ExternalInput").ap()
    gs = nc.dram_tensor("gs", [I_FF, NF], F32, kind="ExternalInput").ap()
    gz = nc.dram_tensor("gz", [I_FF, NF], F32, kind="ExternalInput").ap()
    uq = nc.dram_tensor("uq", [I_FF, D], I32, kind="ExternalInput").ap()
    us = nc.dram_tensor("us", [I_FF, NF], F32, kind="ExternalInput").ap()
    uz = nc.dram_tensor("uz", [I_FF, NF], F32, kind="ExternalInput").ap()
    dq = nc.dram_tensor("dq", [D, I_FF], I32, kind="ExternalInput").ap()
    ds = nc.dram_tensor("ds", [D, NIB], F32, kind="ExternalInput").ap()
    dz = nc.dram_tensor("dz", [D, NIB], F32, kind="ExternalInput").ap()
    cp_c = nc.dram_tensor("cp_c", [128, S], BF16, kind="ExternalInput").ap()
    sp_c = nc.dram_tensor("sp_c", [128, S], BF16, kind="ExternalInput").ap()
    swap_c = nc.dram_tensor("swap_c", [128, 128], BF16, kind="ExternalInput").ap()
    mask_c = nc.dram_tensor("mask_c", [4, 128, 512], BF16, kind="ExternalInput").ap()
    onec_c = nc.dram_tensor("onec_c", [128, 1], BF16, kind="ExternalInput").ap()
    oner_c = nc.dram_tensor("oner_c", [1, 128], BF16, kind="ExternalInput").ap()
    ident_c = nc.dram_tensor("ident_c", [128, 128], BF16, kind="ExternalInput").ap()
    ident32_c = nc.dram_tensor("ident32_c", [128, 128], F32, kind="ExternalInput").ap()
    out = nc.dram_tensor("out", [S, D], F32, kind="ExternalOutput").ap()

    xbf_s = nc.dram_tensor("xbf_s", [S, D], BF16).ap()
    qs_s = nc.dram_tensor("qs_s", [D, S], BF16).ap()
    ks_s = nc.dram_tensor("ks_s", [HKV * DH, S], BF16).ap()
    as_s = nc.dram_tensor("as_s", [S, D], BF16).ap()
    o1_s = nc.dram_tensor("o1_s", [S, D], F32).ap()
    h_s = nc.dram_tensor("h_s", [S, D], F32).ap()
    hbf_s = nc.dram_tensor("hbf_s", [S, D], BF16).ap()
    wgdq_s = nc.dram_tensor("wgdq_s", [I_FF, D], BF16).ap()
    wudq_s = nc.dram_tensor("wudq_s", [I_FF, D], BF16).ap()
    hg_s = nc.dram_tensor("hg_s", [I_FF, S], BF16).ap()
    o2_s = nc.dram_tensor("o2_s", [S, D], F32).ap()

    b = _B(nc)
    ctx_stack = []

    def alloc(name, shape, dtype, psum=False):
        cm = nc.psum_tensor(name, shape, dtype) if psum else nc.sbuf_tensor(name, shape, dtype)
        h = cm.__enter__()
        ctx_stack.append((name, cm))
        return h

    def free(*names):
        names = set(names)
        while names:
            n, cm = ctx_stack[-1]
            assert n in names, f"non-LIFO free: top={n}, want={names}"
            cm.__exit__(None, None, None)
            ctx_stack.pop()
            names.discard(n)

    swap_sb = alloc("swap_sb", [128, 128], BF16)
    onec_sb = alloc("onec_sb", [128, 1], BF16)
    oner_sb = alloc("oner_sb", [1, 128], BF16)
    ident_sb = alloc("ident_sb", [128, 128], BF16)
    negc_sb = alloc("negc_sb", [128, NQ], F32)
    eps_sb = alloc("eps_sb", [128, 1], F32)
    v_sb = alloc("v_sb", [128, NT128 * HKV * 130], BF16)

    t_swap = b.dma("sp", lambda e: e.dma_start(swap_sb[:], swap_c[:]))
    t_onec = b.dma("sp", lambda e: e.dma_start(onec_sb[:], onec_c[:]))
    t_oner = b.dma("sp", lambda e: e.dma_start(oner_sb[:], oner_c[:]))
    t_ident = b.dma("sp", lambda e: e.dma_start(ident_sb[:], ident_c[:]))
    t_vones = b.op("dve", lambda e: e.memset(v_sb[:], 1.0))
    t_eps = b.op("dve", lambda e: e.memset(eps_sb[:], EPS))

    x_fm = alloc("x_fm", [128, NF * S], BF16)
    xrow_sb = [alloc("xrow0", [128, D], BF16)]
    ps_tr1 = [alloc(f"ps_tr1_{i}", [128, 128], BF16, psum=True) for i in range(3)]
    t_h2h = []
    rch = S // 4
    for j in range(4):
        t_h2h.append(b.dma("pool", lambda e, j=j: e.dma_start(
            xbf_s[j * rch:(j + 1) * rch, :], xin[j * rch:(j + 1) * rch, :])))
    t_xfm = {}
    tr01_ctr = [0]
    L0 = {}

    def tk0(key):
        v = L0.get(key)
        return [] if v is None else (v if isinstance(v, list) else [v])

    for qc in range(NT128):
        rb = 0
        t_xr = b.dma("sp", lambda e, qc=qc, rb=rb: e.dma_start(
            xrow_sb[rb][:], xbf_s[qc * 128:(qc + 1) * 128, :]),
            waits=[t_h2h[qc // 4]] + tk0(("xrowfree", rb)))
        last_tr = None
        for g in range(NF):
            pt = tr01_ctr[0] % 3
            tr01_ctr[0] += 1
            t_tr = b.op("pe", lambda e, rb=rb, g=g, pt=pt: e.transpose(
                ps_tr1[pt][:], xrow_sb[rb][:, g * 128:(g + 1) * 128], ident_sb[:]),
                waits=[t_xr, t_ident] + tk0(("tr1", pt)))
            last_tr = t_tr
            col = g * S + qc * 128
            if g % 2 == 0:
                t_ev = b.op("act", lambda e, col=col, pt=pt: e.copy(
                    x_fm[:, col:col + 128], ps_tr1[pt][:]), waits=[t_tr])
            else:
                t_ev = b.op("dve", lambda e, col=col, pt=pt: e.tensor_copy(
                    x_fm[:, col:col + 128], ps_tr1[pt][:]), waits=[t_tr])
            L0[("tr1", pt)] = t_ev
            t_xfm[(qc, g)] = t_ev
        L0[("xrowfree", rb)] = last_tr

    cp_sb = alloc("cp_sb", [128, S], BF16)
    sp_sb = alloc("sp_sb", [128, S], BF16)
    t_cp = b.dma("sp", lambda e: e.dma_start(cp_sb[:], cp_c[:]))
    t_sp = b.dma("sp", lambda e: e.dma_start(sp_sb[:], sp_c[:]))

    cb_row = alloc("cb_row", [1, NQ], F32)
    kn_row = alloc("kn_row", [1, HKV * NTC], F32)
    slab_sb = [alloc(f"slab{i}", [128, NF * 128], BF16) for i in range(2)]
    qraw_sb = [alloc(f"qraw{i}", [128, 512], BF16) for i in range(2)]
    m1_sb = [alloc(f"m1_{i}", [128, 512], BF16) for i in range(2)]
    m2_sb = [alloc(f"m2_{i}", [128, 512], BF16) for i in range(2)]
    qfm_sb = [alloc(f"qfm{i}", [128, 512], BF16) for i in range(2)]
    qsq_sb = [alloc("qsq0", [128, 512], BF16)]
    vfm_sb = [alloc(f"vfm{i}", [128, 512], BF16) for i in range(2)]
    kmax2 = alloc("kmax2", [1, HKV], F32)
    cbs_sb = alloc("cbs_sb", [1, NQ], F32)
    negc_bf = alloc("negc_bf", [1, NQ], BF16)
    ps_p = [alloc(f"ps_p{i}", [128, 512], F32, psum=True) for i in range(2)]
    ps_sw = [alloc(f"ps_sw{i}", [128, 512], F32, psum=True) for i in range(2)]
    ps_n = alloc("ps_n", [1, 512], F32, psum=True)

    last = {}

    def tk(key):
        v = last.get(key)
        return [] if v is None else (v if isinstance(v, list) else [v])

    n_ob = HQ + HKV + HKV

    def slab_src(ob):
        if ob < HQ:
            return wq, ob, D
        if ob < HQ + HKV:
            return wk, ob - HQ, HKV * DH
        return wv, ob - HQ - HKV, HKV * DH

    def emit_slab_load(ob, buf):
        w, col, rw = slab_src(ob)
        toks = []
        for j in range(4):
            def fn(e, w=w, col=col, j=j, buf=buf, rw=rw):
                src = bass.AP(w.tensor, j * 8 * 128 * rw + col * 128,
                              [[rw, 128], [128 * rw, 8], [1, 128]])
                dst = slab_sb[buf][:, j * 1024:(j + 1) * 1024].rearrange(
                    "p (c d) -> p c d", d=128)
                return e.dma_start(dst, src)
            toks.append(b.dma("pool", fn, waits=tk(("slabfree", buf))))
        return toks

    slab_tok = [None, None]
    slab_tok[0] = emit_slab_load(0, 0)
    npp = 0

    pendA = []
    pendB = []

    def make_rope_tail(ob, tc, rb, is_q, g_kv, t_ev):
        st = {}

        def cloA():
            psw = (ob * NTC + tc) % 2
            t_sw = b.op("pe", lambda e, rb=rb, psw=psw: e.matmul(
                ps_sw[psw][:], swap_sb[:], qraw_sb[rb][:], start=True, stop=True),
                waits=[t_ev, t_swap] + tk(("ps_sw", psw)))
            t_m1 = b.op("pool", lambda e, rb=rb, tc=tc: e.tensor_mul(
                m1_sb[rb][:], qraw_sb[rb][:], cp_sb[:, tc * 512:tc * 512 + 512]),
                waits=[t_ev, t_cp] + tk(("m1", rb)))
            last[("qraw", rb)] = [t_sw, t_m1]
            t_m2 = b.op("dve", lambda e, rb=rb, psw=psw, tc=tc: e.tensor_mul(
                m2_sb[rb][:], ps_sw[psw][:], sp_sb[:, tc * 512:tc * 512 + 512]),
                waits=[t_sw, t_sp] + tk(("m2", rb)))
            last[("ps_sw", psw)] = t_m2
            t_fm = b.op("dve", lambda e, rb=rb: e.tensor_add(
                qfm_sb[rb][:], m1_sb[rb][:], m2_sb[rb][:]),
                waits=[t_m1, t_m2] + tk(("qfm", rb)))
            last[("m1", rb)] = t_fm
            last[("m2", rb)] = t_fm
            if is_q:
                t_st = b.dma("sp", lambda e, ob=ob, tc=tc, rb=rb: e.dma_start(
                    qs_s[ob * 128:(ob + 1) * 128, tc * 512:tc * 512 + 512],
                    qfm_sb[rb][:]), waits=[t_fm])
            else:
                t_st = b.dma("sp", lambda e, g_kv=g_kv, tc=tc, rb=rb: e.dma_start(
                    ks_s[g_kv * 128:(g_kv + 1) * 128, tc * 512:tc * 512 + 512],
                    qfm_sb[rb][:]), waits=[t_fm])
            t_sq = b.op("act", lambda e, rb=rb: e.square(qsq_sb[0][:], qfm_sb[rb][:]),
                        waits=[t_fm] + tk(("qsq",)))
            last[("qfm", rb)] = [t_st, t_sq]
            st["sq"] = t_sq

        def cloB():
            t_nm = b.op("pe", lambda e: e.matmul(
                ps_n[:], onec_sb[:], qsq_sb[0][:], start=True, stop=True),
                waits=[st["sq"], t_onec] + tk(("ps_n",)))
            last[("qsq",)] = t_nm
            if is_q:
                idx = ob * NQB + tc
                t_rd = b.op("dve", lambda e, idx=idx: e.reduce_max(
                    cb_row[0:1, idx:idx + 1], ps_n[0:1, :], axis=AX.X),
                    waits=[t_nm])
            else:
                idx = g_kv * NTC + tc
                t_rd = b.op("dve", lambda e, idx=idx: e.reduce_max(
                    kn_row[0:1, idx:idx + 1], ps_n[0:1, :], axis=AX.X),
                    waits=[t_nm])
            last[("ps_n",)] = t_rd
        return cloA, cloB

    def make_v_tail(tc, rb, g_kv, t_ev):
        def clo():
            t_ve = None
            for j in range(4):
                pt = tr01_ctr[0] % 3
                tr01_ctr[0] += 1
                t_tr = b.op("pe", lambda e, rb=rb, j=j, pt=pt: e.transpose(
                    ps_tr1[pt][:], vfm_sb[rb][:, j * 128:(j + 1) * 128], ident_sb[:]),
                    waits=[t_ev, t_ident] + tk0(("tr1", pt)))
                col = (tc * 4 + j) * HKV * 130 + g_kv * 130
                t_ve = b.op("dve", lambda e, pt=pt, col=col: e.tensor_copy(
                    v_sb[:, col:col + 128], ps_tr1[pt][:]),
                    waits=[t_tr, t_vones])
                L0[("tr1", pt)] = t_ve
            last[("vfm", rb)] = t_ve
        return clo

    for ob in range(n_ob):
        buf = ob % 2
        if ob + 1 < n_ob:
            slab_tok[(ob + 1) % 2] = emit_slab_load(ob + 1, (ob + 1) % 2)
        is_q = ob < HQ
        is_v = ob >= HQ + HKV
        g_kv = None if is_q else (ob - HQ if not is_v else ob - HQ - HKV)
        for tc in range(NTC):
            pp = npp % 2
            npp += 1
            mm_tok = None
            for fc in range(NF):
                mm_tok = b.op("pe", lambda e, fc=fc, tc=tc, pp=pp, buf=buf: e.matmul(
                    ps_p[pp][:],
                    slab_sb[buf][:, fc * 128:(fc + 1) * 128],
                    x_fm[:, fc * S + tc * 512: fc * S + tc * 512 + 512],
                    start=(fc == 0), stop=(fc == NF - 1)),
                    waits=slab_tok[buf] +
                          [t_xfm[(qq, fc)] for qq in range(4 * tc, 4 * tc + 4)] +
                          tk(("ps_p", pp)),
                    inc=(fc == NF - 1))
            if tc == NTC - 1:
                last[("slabfree", buf)] = mm_tok
            rb = (ob * NTC + tc) % 2
            if is_v:
                t_ev = b.op("act", lambda e, pp=pp, rb=rb: e.copy(
                    vfm_sb[rb][:], ps_p[pp][:]),
                    waits=[mm_tok] + tk(("vfm", rb)))
                last[("ps_p", pp)] = t_ev
                pendA.append(make_v_tail(tc, rb, g_kv, t_ev))
                nb_app = 0
            else:
                t_ev = b.op("act", lambda e, pp=pp, rb=rb: e.copy(
                    qraw_sb[rb][:], ps_p[pp][:]),
                    waits=[mm_tok] + tk(("qraw", rb)))
                last[("ps_p", pp)] = t_ev
                cloA, cloB = make_rope_tail(ob, tc, rb, is_q, g_kv, t_ev)
                pendA.append(cloA)
                pendB.append(cloB)
                nb_app = 1
            while len(pendB) > 1 + nb_app:
                pendB.pop(0)()
            while len(pendA) > 1:
                pendA.pop(0)()
    while len(pendB) > 1:
        pendB.pop(0)()
    while pendA:
        pendA.pop(0)()
    while pendB:
        pendB.pop(0)()

    kn_3d = kn_row[0:1, :].rearrange("p (g t) -> p g t", g=HKV)
    km_3d = kmax2[0:1, :].rearrange("p (g o) -> p g o", o=1)
    t1 = b.op("dve", lambda e: e.tensor_reduce(
        km_3d, kn_3d, axis=AX.X, op=ALU.max),
        waits=[last[("ps_n",)]])
    cb_3d = cb_row[0:1, :].rearrange("p (g q) -> p g q", g=HKV)
    cb_b, km_b = bass.broadcast_tensor_aps(cb_3d, km_3d)
    t2 = b.op("dve", lambda e: e.tensor_tensor(cb_b, cb_b, km_b, op=ALU.mult),
              waits=[t1])
    t3 = b.op("act", lambda e: e.activation(
        cbs_sb[0:1, :], cb_row[0:1, :], AF.Sqrt, scale=1.0 / 128.0), waits=[t2])
    t4 = b.op("dve", lambda e: e.tensor_scalar(
        negc_bf[0:1, :], cbs_sb[0:1, :], -1.0, None, op0=ALU.mult), waits=[t3])
    t5 = b.op("pe", lambda e: e.matmul(
        ps_sw[0][:, 0:NQ], oner_sb[:], negc_bf[0:1, :], start=True, stop=True),
        waits=[t4, t_oner] + tk(("ps_sw", 0)))
    t_negc = b.op("dve", lambda e: e.tensor_copy(negc_sb[:, 0:NQ], ps_sw[0][:, 0:NQ]),
                  waits=[t5])

    free("ps_n", "ps_sw1", "ps_sw0", "ps_p1", "ps_p0")
    free("negc_bf", "cbs_sb", "kmax2", "vfm1", "vfm0", "qsq0", "qfm1", "qfm0",
         "m2_1", "m2_0", "m1_1", "m1_0", "qraw1", "qraw0", "slab1", "slab0",
         "kn_row", "cb_row", "sp_sb", "cp_sb")
    free("ps_tr1_2", "ps_tr1_1", "ps_tr1_0")
    free("xrow0")
    free("x_fm")
    b.bar()

    mask_sb = alloc("mask_sb", [128, 4 * 512], BF16)
    kfm_sb = alloc("kfm_sb", [128, HKV * S], BF16)
    qh_sb = [alloc(f"qh{i}", [128, S], BF16) for i in range(4)]
    pkm_sb = [alloc(f"pkm{i}", [128, NT128 * 512], BF16) for i in range(2)]
    aacc_sb = [alloc(f"aacc_sb{i}", [128, NT128 * 512], BF16) for i in range(2)]
    r_sb = [alloc(f"r_sb{i}", [128, 1], F32) for i in range(4)]
    ps_s = [alloc(f"ps_s{i}", [128, 512], F32, psum=True) for i in range(4)]
    ps_av = [alloc(f"ps_av{i}", [128, 132], F32, psum=True) for i in range(4)]

    _mask_src = bass.AP(mask_c.tensor, 0, [[512, 128], [128 * 512, 4], [1, 512]])
    _mask_dst = mask_sb[:].rearrange("p (o q) -> p o q", q=512)
    t_mask = b.dma("sp", lambda e: e.dma_start(_mask_dst, _mask_src))
    t_k = {}
    for g in range(HKV):
        t_k[g] = b.dma("sp", lambda e, g=g: e.dma_start(
            kfm_sb[:, g * S:(g + 1) * S], ks_s[g * 128:(g + 1) * 128, :]))

    L2 = {}

    def tk2(key):
        v = L2.get(key)
        return [] if v is None else (v if isinstance(v, list) else [v])

    def q_load(h, buf):
        return b.dma("sp", lambda e, h=h, buf=buf: e.dma_start(
            qh_sb[buf][:], qs_s[h * 128:(h + 1) * 128, :]),
            waits=tk2(("qhfree", buf)))

    for h0 in range(3):
        L2[("qld", h0)] = q_load(h0, h0)
    prod_pkm = {}
    avq = []
    av_ctr = [0]
    sc_ctr = [0]

    def pop_av(n):
        for _ in range(n):
            if avq:
                avq.pop(0)[1]()

    def drain_av_through(seq):
        while avq and avq[0][0] <= seq:
            avq.pop(0)[1]()

    def make_av_group(h, qb, pb, g, hh, kb_hi, q4, state):
        qsub = qb * 4 + q4
        ga = g % 2

        def clo():
            ab = av_ctr[0] % 4
            av_ctr[0] += 1
            t_av = None
            for kb in range(qsub + 1):
                t_av = b.op("pe", lambda e, pb=pb, kb=kb, q4=q4, g=g, qsub=qsub, ab=ab: e.matmul(
                    ps_av[ab][:, 0:129],
                    pkm_sb[pb][:, kb * 512 + q4 * 128: kb * 512 + q4 * 128 + 128],
                    v_sb[:, kb * HKV * 130 + g * 130: kb * HKV * 130 + g * 130 + 129],
                    start=(kb == 0), stop=(kb == qsub)),
                    waits=[prod_pkm[(pb, kb)]] + tk2(("ps_av", ab)),
                    inc=(kb == qsub))
            state["last"] = t_av
            t_r = b.op("dve", lambda e, ab=ab: e.reciprocal(
                r_sb[ab][:], ps_av[ab][:, 128:129]), waits=[t_av] + tk2(("r", ab)))
            t_an = b.op("dve", lambda e, ab=ab, qsub=qsub, hh=hh, ga=ga: e.tensor_scalar(
                aacc_sb[ga][:, qsub * 512 + hh * 128: qsub * 512 + hh * 128 + 128],
                ps_av[ab][:, 0:128], r_sb[ab][:], None, op0=ALU.mult),
                waits=[t_r] + tk2(("aaccfree", ga, qsub)))
            L2[("r", ab)] = t_an
            L2[("ps_av", ab)] = t_an
            L2[("aaccw", qsub, hh)] = t_an
            if q4 == 3:
                for kb in range(kb_hi):
                    L2[("pkmfree", pb, kb)] = state["last"]
        return clo

    def make_store(g):
        ga = g % 2

        def clo():
            for qsub in range(NT128):
                w = [L2[("aaccw", qsub, j)] for j in range(4)]
                t_sta = b.dma("sp", lambda e, qsub=qsub, g=g, ga=ga: e.dma_start(
                    as_s[qsub * 128:(qsub + 1) * 128, g * 512:(g + 1) * 512],
                    aacc_sb[ga][:, qsub * 512:(qsub + 1) * 512]), waits=w)
                L2[("aaccfree", ga, qsub)] = t_sta
        return clo

    for h in range(HQ):
        g = h // 4
        hh = h % 4
        hb = h % 4
        if h + 3 < HQ:
            L2[("qld", (h + 3) % 4)] = q_load(h + 3, (h + 3) % 4)
        last_sc_mm = None
        for qb in range(NQB):
            seq = h * NQB + qb
            pb = seq % 2
            kb_hi = 4 * qb + 4
            drain_av_through(seq - 2)
            for kb in range(kb_hi):
                sb_ = sc_ctr[0] % 4
                sc_ctr[0] += 1
                t_mm = b.op("pe", lambda e, g=g, kb=kb, qb=qb, hb=hb, sb_=sb_: e.matmul(
                    ps_s[sb_][:],
                    kfm_sb[:, g * S + kb * 128: g * S + kb * 128 + 128],
                    qh_sb[hb][:, qb * 512: qb * 512 + 512],
                    start=True, stop=True),
                    waits=[t_k[g]] + tk2(("qld", hb)) + tk2(("ps_s", sb_)))
                last_sc_mm = t_mm
                t_ex = b.op("act", lambda e, sb_=sb_, pb=pb, kb=kb, h=h, qb=qb: e.activation(
                    pkm_sb[pb][:, kb * 512: kb * 512 + 512], ps_s[sb_][:],
                    AF.Exp, bias=negc_sb[:, h * NQB + qb: h * NQB + qb + 1],
                    scale=RSQ),
                    waits=[t_mm, t_negc] + tk2(("pkmfree", pb, kb)))
                L2[("ps_s", sb_)] = t_ex
                if kb >= 4 * qb:
                    off = kb - 4 * qb
                    t_mk = b.op("dve", lambda e, pb=pb, kb=kb, off=off: e.tensor_mul(
                        pkm_sb[pb][:, kb * 512: kb * 512 + 512],
                        pkm_sb[pb][:, kb * 512: kb * 512 + 512],
                        mask_sb[:, off * 512:(off + 1) * 512]),
                        waits=[t_ex, t_mask])
                    prod_pkm[(pb, kb)] = t_mk
                else:
                    prod_pkm[(pb, kb)] = t_ex
                if kb % 2 == 1:
                    pop_av(1)
            state = {"last": None}
            for q4 in range(4):
                avq.append((seq, make_av_group(h, qb, pb, g, hh, kb_hi, q4, state)))
            if hh == 3 and qb == NQB - 1:
                avq.append((seq, make_store(g)))
        L2[("qhfree", hb)] = last_sc_mm
    pop_av(len(avq))

    free("ps_av3", "ps_av2", "ps_av1", "ps_av0", "ps_s3", "ps_s2", "ps_s1", "ps_s0")
    free("r_sb3", "r_sb2", "r_sb1", "r_sb0", "aacc_sb1", "aacc_sb0",
         "pkm1", "pkm0", "qh3", "qh2", "qh1", "qh0", "kfm_sb", "mask_sb")
    free("v_sb")
    b.bar()

    afm_sb = alloc("afm_sb", [128, NF * S], BF16)
    arow_sb = [alloc("arow0", [128, D], BF16)]
    wo_sb = [alloc(f"wo{i}", [128, NF * 512], BF16) for i in range(2)]
    o1t_sb = [alloc(f"o1t{i}", [128, 512], F32) for i in range(2)]
    ps_o = [alloc(f"ps_o{i}", [128, 512], F32, psum=True) for i in range(2)]
    ps_tr3 = [alloc(f"ps_tr3_{i}", [128, 128], BF16, psum=True) for i in range(3)]

    L3 = {}

    def tk3(key):
        v = L3.get(key)
        return [] if v is None else (v if isinstance(v, list) else [v])

    t_afm = {}
    tr3_ctr = [0]
    for qc in range(NT128):
        rb = 0
        t_ar = b.dma("act", lambda e, qc=qc, rb=rb: e.dma_start(
            arow_sb[rb][:], as_s[qc * 128:(qc + 1) * 128, :]),
            waits=tk3(("arowfree", rb)))
        last_tr = None
        for cc in range(NF):
            pt = tr3_ctr[0] % 3
            tr3_ctr[0] += 1
            t_tr = b.op("pe", lambda e, rb=rb, cc=cc, pt=pt: e.transpose(
                ps_tr3[pt][:], arow_sb[rb][:, cc * 128:(cc + 1) * 128], ident_sb[:]),
                waits=[t_ar, t_ident] + tk3(("tr3", pt)))
            last_tr = t_tr
            col = cc * S + qc * 128
            if cc % 2 == 0:
                t_ev = b.op("act", lambda e, col=col, pt=pt: e.copy(
                    afm_sb[:, col:col + 128], ps_tr3[pt][:]), waits=[t_tr])
            else:
                t_ev = b.op("dve", lambda e, col=col, pt=pt: e.tensor_copy(
                    afm_sb[:, col:col + 128], ps_tr3[pt][:]), waits=[t_tr])
            L3[("tr3", pt)] = t_ev
            t_afm[(qc, cc)] = t_ev
        L3[("arowfree", rb)] = last_tr

    def wo_load(fc, buf):
        toks = []
        for j in range(4):
            def fn(e, fc=fc, j=j, buf=buf):
                src = bass.AP(wo.tensor, j * 8 * 128 * D + fc * 512,
                              [[D, 128], [128 * D, 8], [1, 512]])
                dst = wo_sb[buf][:, j * 8 * 512:(j + 1) * 8 * 512].rearrange(
                    "p (c f) -> p c f", f=512)
                return e.dma_start(dst, src)
            toks.append(b.dma("pool", fn, waits=tk3(("wofree", buf))))
        return toks

    wo_tok = [None, None]
    wo_tok[0] = wo_load(0, 0)
    for fc in range(8):
        buf = fc % 2
        if fc + 1 < 8:
            wo_tok[(fc + 1) % 2] = wo_load(fc + 1, (fc + 1) % 2)
        for qc in range(NT128):
            po = qc % 2
            t_mm = None
            for cc in range(NF):
                t_mm = b.op("pe", lambda e, cc=cc, qc=qc, po=po, buf=buf: e.matmul(
                    ps_o[po][:],
                    afm_sb[:, cc * S + qc * 128: cc * S + qc * 128 + 128],
                    wo_sb[buf][:, cc * 512:(cc + 1) * 512],
                    start=(cc == 0), stop=(cc == NF - 1)),
                    waits=wo_tok[buf] + [t_afm[(qc, cc)]] + tk3(("ps_o", po)),
                    inc=(cc == NF - 1))
            if qc == NT128 - 1:
                L3[("wofree", buf)] = t_mm
            t_ev = b.op("act", lambda e, po=po: e.copy(
                o1t_sb[po][:], ps_o[po][:]),
                waits=[t_mm] + tk3(("o1t", po)))
            L3[("ps_o", po)] = t_ev
            t_st = b.dma("sp", lambda e, qc=qc, fc=fc, po=po: e.dma_start(
                o1_s[qc * 128:(qc + 1) * 128, fc * 512:(fc + 1) * 512],
                o1t_sb[po][:]), waits=[t_ev])
            L3[("o1t", po)] = t_st

    free("ps_tr3_2", "ps_tr3_1", "ps_tr3_0", "ps_o1", "ps_o0")
    free("o1t1", "o1t0", "wo1", "wo0", "arow0", "afm_sb")
    b.bar()

    hfm_sb = alloc("hfm_sb", [128, NF * S], BF16)
    o1r_sb = [alloc(f"o1r{i}", [128, D], F32) for i in range(2)]
    xr_sb = [alloc(f"xr{i}", [128, D], F32) for i in range(2)]
    hbf_t = alloc("hbft0", [128, D], BF16)
    bn_sb = [alloc(f"bn_sb{i}", [128, 48], F32) for i in range(2)]
    mv_sb = [alloc(f"mv_sb{i}", [128, 2], F32) for i in range(2)]
    sd_sb = [alloc(f"sd_sb{i}", [128, 1], F32) for i in range(2)]
    rs_sb = [alloc(f"rs_sb{i}", [128, 1], F32) for i in range(2)]
    nb4_sb = [alloc(f"nb4_sb{i}", [128, 1], F32) for i in range(2)]
    ps_h4 = [alloc(f"ps_h4_{i}", [128, 128], BF16, psum=True) for i in range(3)]

    L4 = {}

    def tk4(key):
        v = L4.get(key)
        return [] if v is None else (v if isinstance(v, list) else [v])

    for qc in range(NT128):
        rb = qc % 2
        t_lo = b.dma("sp", lambda e, qc=qc, rb=rb: e.dma_start(
            o1r_sb[rb][:], o1_s[qc * 128:(qc + 1) * 128, :]),
            waits=tk4(("o1r", rb)))
        t_lx = b.dma("pool", lambda e, qc=qc, rb=rb: e.dma_start(
            xr_sb[rb][:], xin[qc * 128:(qc + 1) * 128, :]),
            waits=tk4(("xr", rb)) + tk4(("xr2", rb)))
        ad_toks = []
        t_bn = None
        for j in range(8):
            t_ad = b.op("dve", lambda e, rb=rb, j=j: e.tensor_add(
                o1r_sb[rb][:, j * 512:(j + 1) * 512],
                o1r_sb[rb][:, j * 512:(j + 1) * 512],
                xr_sb[rb][:, j * 512:(j + 1) * 512]),
                waits=[t_lo, t_lx])
            ad_toks.append(t_ad)
            t_bn = b.op("dve", lambda e, rb=rb, j=j: e.bn_stats(
                bn_sb[rb][:, j * 6:(j + 1) * 6],
                o1r_sb[rb][:, j * 512:(j + 1) * 512]),
                waits=[t_ad] + (tk4(("bn", rb)) if j == 0 else []))
        L4[("xr", rb)] = ad_toks
        t_ag = b.op("dve", lambda e, rb=rb: e.bn_aggr(mv_sb[rb][:], bn_sb[rb][:]),
                    waits=[t_bn])
        L4[("bn", rb)] = t_ag
        t_sd = b.op("act", lambda e, rb=rb: e.activation(
            sd_sb[rb][:], mv_sb[rb][:, 1:2], AF.Sqrt, bias=eps_sb[:]),
            waits=[t_ag, t_eps] + tk4(("sd", rb)))
        t_rs = b.op("dve", lambda e, rb=rb: e.reciprocal(rs_sb[rb][:], sd_sb[rb][:]),
                    waits=[t_sd] + tk4(("rs", rb)))
        L4[("sd", rb)] = t_rs
        t_nb = b.op("dve", lambda e, rb=rb: e.scalar_tensor_tensor(
            nb4_sb[rb][:], mv_sb[rb][:, 0:1], -1.0, rs_sb[rb][:],
            op0=ALU.mult, op1=ALU.mult),
            waits=[t_rs] + tk4(("nb", rb)))
        L4[("rs", rb)] = t_nb
        t_h = b.op("act", lambda e, rb=rb: e.activation(
            xr_sb[rb][:], o1r_sb[rb][:], AF.Identity,
            bias=nb4_sb[rb][:], scale=rs_sb[rb][:]),
            waits=[t_nb] + tk4(("xr", rb)))
        L4[("nb", rb)] = t_h
        t_cb = b.op("act", lambda e, rb=rb: e.copy(hbf_t[:], xr_sb[rb][:]),
                    waits=[t_h] + tk4(("hbft",)))
        t_s1 = b.dma("sp", lambda e, qc=qc, rb=rb: e.dma_start(
            h_s[qc * 128:(qc + 1) * 128, :], xr_sb[rb][:]), waits=[t_h])
        L4[("o1r", rb)] = t_h
        L4[("xr2", rb)] = [t_cb, t_s1]
        t_tr_last = None
        for g in range(NF):
            pt = (qc * NF + g) % 3
            t_tr = b.op("pe", lambda e, g=g, pt=pt: e.transpose(
                ps_h4[pt][:], hbf_t[:, g * 128:(g + 1) * 128], ident_sb[:]),
                waits=[t_cb, t_ident] + tk4(("h4ps", pt)))
            t_tr_last = t_tr
            col = g * S + qc * 128
            if g % 2 == 0:
                t_ev = b.op("act", lambda e, col=col, pt=pt: e.copy(
                    hfm_sb[:, col:col + 128], ps_h4[pt][:]), waits=[t_tr])
            else:
                t_ev = b.op("dve", lambda e, col=col, pt=pt: e.tensor_copy(
                    hfm_sb[:, col:col + 128], ps_h4[pt][:]), waits=[t_tr])
            L4[("h4ps", pt)] = t_ev
        L4[("hbft",)] = t_tr_last

    free("ps_h4_2", "ps_h4_1", "ps_h4_0")
    free("nb4_sb1", "nb4_sb0", "rs_sb1", "rs_sb0", "sd_sb1", "sd_sb0",
         "mv_sb1", "mv_sb0", "bn_sb1", "bn_sb0", "hbft0",
         "xr1", "xr0", "o1r1", "o1r0")
    b.bar()

    grawq_sb = [alloc(f"grawq{i}", [128, D], BF16) for i in range(3)]
    sc5_sb = [[alloc(f"sc5_{m}_{i}", [128, NF], F32) for i in range(2)]
              for m in range(2)]
    zc5_sb = [[alloc(f"zc5_{m}_{i}", [128, NF], F32) for i in range(2)]
              for m in range(2)]
    zs5_sb = [[alloc(f"zs5_{m}_{i}", [128, NF], F32) for i in range(2)]
              for m in range(2)]
    wn5_sb = alloc("wn5_sb", [128, NF * 128], BF16)
    wslab5_sb = [alloc(f"wslab5_{m}", [128, NF * 128], BF16) for m in range(2)]
    gtile5_sb = alloc("gtile5_sb", [128, S], BF16)
    hgt5_sb = [alloc(f"hgt5_{i}", [128, S], BF16) for i in range(2)]
    ps_g5 = [alloc(f"ps_g5_{i}", [128, 512], F32, psum=True) for i in range(2)]
    ps_u5 = [alloc(f"ps_u5_{i}", [128, 512], F32, psum=True) for i in range(2)]
    ps_t5 = [alloc(f"ps_t5_{i}", [128, 128], BF16, psum=True) for i in range(4)]

    L5 = {}

    def tk5(key):
        v = L5.get(key)
        return [] if v is None else (v if isinstance(v, list) else [v])

    MATS5 = [("g", gq, gs, gz), ("u", uq, us, uz)]
    raw5_ctr = [0]
    tr5_ctr = [0]

    def emit_gu_scales(ib, m):
        ip = ib % 2
        _, qw, sw, zw = MATS5[m]
        t_s = b.dma("sp", lambda e, sw=sw, ib=ib, m=m, ip=ip: e.dma_start(
            sc5_sb[m][ip][:], sw[ib * 128:(ib + 1) * 128, :]),
            waits=tk5(("scuse5", m, ip)))
        t_z = b.dma("sp", lambda e, zw=zw, ib=ib, m=m, ip=ip: e.dma_start(
            zc5_sb[m][ip][:], zw[ib * 128:(ib + 1) * 128, :]),
            waits=tk5(("scuse5", m, ip)))
        t_zs = b.op("dve", lambda e, m=m, ip=ip: e.tensor_mul(
            zs5_sb[m][ip][:], zc5_sb[m][ip][:], sc5_sb[m][ip][:]),
            waits=[t_s, t_z] + tk5(("scuse5", m, ip)))
        L5[("scld5", m, ip)] = [t_s, t_zs]

    def emit_gu_raw(ib, m):
        slot = raw5_ctr[0] % 3
        raw5_ctr[0] += 1
        _, qw, _, _ = MATS5[m]
        toks = []
        for q4 in range(4):
            toks.append(b.dma("pool", lambda e, qw=qw, ib=ib, slot=slot, q4=q4: e.dma_start(
                grawq_sb[slot][:, q4 * 1024:(q4 + 1) * 1024],
                qw[ib * 128:(ib + 1) * 128, q4 * 1024:(q4 + 1) * 1024]),
                waits=tk5(("rawfree5", slot))))
        return slot, toks

    def production5(ib, m, rawslot, rawtoks):
        ip = ib % 2
        sc_last = None
        for g in range(NF):
            pslot = tr5_ctr[0] % 4
            tr5_ctr[0] += 1
            t_sc = b.op("dve", lambda e, m=m, ip=ip, g=g, rawslot=rawslot: e.tensor_scalar(
                wn5_sb[:, g * 128:(g + 1) * 128],
                grawq_sb[rawslot][:, g * 128:(g + 1) * 128],
                sc5_sb[m][ip][:, g:g + 1], zs5_sb[m][ip][:, g:g + 1],
                op0=ALU.mult, op1=ALU.subtract),
                waits=[rawtoks[g // 8]] + tk5(("scld5", m, ip)) + tk5(("wn5", g)))
            sc_last = t_sc
            L5[("scuse5", m, ip)] = t_sc
            t_tr = b.op("pe", lambda e, g=g, pslot=pslot: e.transpose(
                ps_t5[pslot][:],
                wn5_sb[:, g * 128:(g + 1) * 128], ident_sb[:]),
                waits=[t_sc, t_ident] + tk5(("t5ps", pslot)))
            L5[("wn5", g)] = t_tr
            if g % 2 == 0:
                t_ev = b.op("act", lambda e, m=m, g=g, pslot=pslot: e.copy(
                    wslab5_sb[m][:, g * 128:(g + 1) * 128], ps_t5[pslot][:]),
                    waits=[t_tr] + tk5(("mmlast5", m)))
            else:
                t_ev = b.op("dve", lambda e, m=m, g=g, pslot=pslot: e.tensor_copy(
                    wslab5_sb[m][:, g * 128:(g + 1) * 128], ps_t5[pslot][:]),
                    waits=[t_tr] + tk5(("mmlast5", m)))
            L5[("t5ps", pslot)] = t_ev
            L5[("slabrdy5", m, g)] = t_ev
            yield
        L5[("rawfree5", rawslot)] = sc_last

    def emit_gu_matmuls(ib, m, prod):
        psb = ps_g5 if m == 0 else ps_u5
        hb5 = (ib % 2)
        step = 0
        for tc in range(NTC):
            pp = tc % 2
            t_mm = None
            for g in range(NF):
                t_mm = b.op("pe", lambda e, m=m, g=g, tc=tc, pp=pp, psb=psb: e.matmul(
                    psb[pp][:],
                    wslab5_sb[m][:, g * 128:(g + 1) * 128],
                    hfm_sb[:, g * S + tc * 512: g * S + tc * 512 + 512],
                    start=(g == 0), stop=(g == NF - 1)),
                    waits=tk5(("slabrdy5", m, g)) +
                          (tk5(("psfree5", m, pp)) if g == 0 else []),
                    inc=(g == NF - 1))
                step += 1
                if prod is not None and step % 4 == 0:
                    next(prod, None)
            L5[("mmlast5", m)] = t_mm
            if m == 0:
                t_si = b.op("act", lambda e, tc=tc, pp=pp: e.activation(
                    gtile5_sb[:, tc * 512:(tc + 1) * 512], ps_g5[pp][:], AF.Silu),
                    waits=[t_mm] + tk5(("gtfree", tc)))
                L5[("psfree5", 0, pp)] = t_si
                L5[("gtw", tc)] = t_si
            else:
                t_mul = b.op("dve", lambda e, tc=tc, pp=pp, hb5=hb5: e.tensor_mul(
                    hgt5_sb[hb5][:, tc * 512:(tc + 1) * 512],
                    ps_u5[pp][:], gtile5_sb[:, tc * 512:(tc + 1) * 512]),
                    waits=[t_mm] + tk5(("gtw", tc)) + tk5(("hgt5", hb5)))
                L5[("psfree5", 1, pp)] = t_mul
                L5[("gtfree", tc)] = t_mul
                L5[("mulw", tc)] = t_mul
        if m == 1:
            t_hg = b.dma("sp", lambda e, ib=ib, hb5=hb5: e.dma_start(
                hg_s[ib * 128:(ib + 1) * 128, :], hgt5_sb[hb5][:]),
                waits=[L5[("mulw", tc2)] for tc2 in range(NTC)])
            L5[("hgt5", hb5)] = t_hg

    order5 = [(ib, m) for ib in range(NIB) for m in range(2)]
    emit_gu_scales(0, 0)
    s0, t0 = emit_gu_raw(0, 0)
    for _ in production5(0, 0, s0, t0):
        pass
    pend_raw5 = {}
    if len(order5) > 1:
        emit_gu_scales(*order5[1])
        pend_raw5[1] = emit_gu_raw(*order5[1])
    for idx, (ib, m) in enumerate(order5):
        if idx + 2 < len(order5):
            emit_gu_scales(*order5[idx + 2])
            pend_raw5[idx + 2] = emit_gu_raw(*order5[idx + 2])
        if idx + 1 < len(order5):
            s2, t2 = pend_raw5.pop(idx + 1)
            nxt = production5(order5[idx + 1][0], order5[idx + 1][1], s2, t2)
        else:
            nxt = None
        emit_gu_matmuls(ib, m, nxt)
        if nxt is not None:
            for _ in nxt:
                pass

    free("ps_t5_3", "ps_t5_2", "ps_t5_1", "ps_t5_0", "ps_u5_1", "ps_u5_0",
         "ps_g5_1", "ps_g5_0")
    free("hgt5_1", "hgt5_0", "gtile5_sb", "wslab5_1", "wslab5_0", "wn5_sb")
    free(*[f"zs5_{m}_{i}" for m in (1, 0) for i in (1, 0)])
    free(*[f"zc5_{m}_{i}" for m in (1, 0) for i in (1, 0)])
    free(*[f"sc5_{m}_{i}" for m in (1, 0) for i in (1, 0)])
    free("grawq2", "grawq1", "grawq0")
    free("hfm_sb")
    b.bar()

    IC_CH = [22, 22, 21, 21]
    IC_OFF = [0, 22, 44, 65]
    MAXC = 22
    NHG = 8

    ident32_sb = alloc("ident32_sb", [128, 128], F32)
    rawd_sb = [[alloc(f"rawd{p}_{j}", [128, MAXC * 128], BF16) for j in range(4)]
               for p in range(2)]
    dsc6_sb = [[alloc(f"dsc6_{p}_{j}", [128, NIB], F32) for j in range(4)]
               for p in range(2)]
    dzc6_sb = [[alloc(f"dzc6_{p}_{j}", [128, NIB], F32) for j in range(4)]
               for p in range(2)]
    dzs6_sb = [[alloc(f"dzs6_{p}_{j}", [128, NIB], F32) for j in range(4)]
               for p in range(2)]
    wnat6_sb = alloc("wnat6_sb", [128, 8 * 128], BF16)
    wdts_sb = [alloc(f"wdts{p}", [128, MAXC * 512], BF16) for p in range(2)]
    NHGT = 8
    hgt6_sb = [alloc(f"hgt6_{p}", [128, 512], BF16) for p in range(NHGT)]
    o2acc_sb = [alloc(f"o2acc_{j}", [128, NTC * 512], F32) for j in range(4)]
    o2a_sb = [alloc(f"o2a_{p}", [128, 2048], F32) for p in range(2)]
    ps_acc = [alloc(f"ps_acc{j}", [128, 512], F32, psum=True) for j in range(4)]
    ps_wtr = [alloc(f"ps_wtr{i}", [128, 128], BF16, psum=True) for i in range(4)]

    t_id32 = b.dma("sp", lambda e: e.dma_start(ident32_sb[:], ident32_c[:]))

    L6 = {}

    def tk6(key):
        v = L6.get(key)
        return [] if v is None else (v if isinstance(v, list) else [v])

    wtr_ctr = [0]
    hgl_ctr = [0]

    def emit_hg_scales(hg):
        hp = hg % 2
        for j in range(4):
            r0 = (hg * 4 + j) * 128
            t_s = b.dma("act", lambda e, r0=r0, hp=hp, j=j: e.dma_start(
                dsc6_sb[hp][j][:], ds[r0:r0 + 128, :]),
                waits=tk6(("scuse", hp, j)))
            t_z = b.dma("act", lambda e, r0=r0, hp=hp, j=j: e.dma_start(
                dzc6_sb[hp][j][:], dz[r0:r0 + 128, :]),
                waits=tk6(("scuse", hp, j)))
            t_zs = b.op("dve", lambda e, hp=hp, j=j: e.tensor_mul(
                dzs6_sb[hp][j][:], dzc6_sb[hp][j][:], dsc6_sb[hp][j][:]),
                waits=[t_s, t_z] + tk6(("scuse", hp, j)))
            L6[("scld", hp, j)] = [t_s, t_zs]

    def emit_raw6(hg, c):
        p = c % 2
        cw = IC_CH[c]
        i0 = IC_OFF[c]
        raw_toks = []
        for j in range(4):
            r0 = (hg * 4 + j) * 128

            def fn(e, r0=r0, p=p, j=j, i0=i0, cw=cw):
                return e.dma_start(
                    rawd_sb[p][j][:, 0:cw * 128],
                    dq[r0:r0 + 128, i0 * 128:(i0 + cw) * 128])
            raw_toks.append(b.dma("pool", fn, waits=tk6(("rawfree", p, j))))
        return raw_toks

    def production_steps(hg, c, raw_toks):
        hp = hg % 2
        p = c % 2
        cw = IC_CH[c]
        i0 = IC_OFF[c]
        sc_last = {j: None for j in range(4)}
        pend_ev = []

        def flush_evacs():
            while pend_ev:
                j2, il2, pslot, t_tr = pend_ev.pop(0)
                col = il2 * 512 + j2 * 128
                if (il2 * 4 + j2) % 2 == 0:
                    t_ev = b.op("act", lambda e, p=p, col=col, pslot=pslot: e.copy(
                        wdts_sb[p][:, col:col + 128], ps_wtr[pslot][:]),
                        waits=[t_tr] + tk6(("mmlast", p)))
                else:
                    t_ev = b.op("dve", lambda e, p=p, col=col, pslot=pslot: e.tensor_copy(
                        wdts_sb[p][:, col:col + 128], ps_wtr[pslot][:]),
                        waits=[t_tr] + tk6(("mmlast", p)))
                L6[("wtrps", pslot)] = t_ev
                L6[("slabrdy", p, il2, j2)] = t_ev

        for il in range(cw):
            i = i0 + il
            flush_evacs()
            for j in range(4):
                wslot = wtr_ctr[0] % 8
                pslot = wtr_ctr[0] % 4
                wtr_ctr[0] += 1
                t_sc = b.op("dve", lambda e, p=p, j=j, il=il, i=i, wslot=wslot, hp=hp: e.tensor_scalar(
                    wnat6_sb[:, wslot * 128:(wslot + 1) * 128],
                    rawd_sb[p][j][:, il * 128:(il + 1) * 128],
                    dsc6_sb[hp][j][:, i:i + 1], dzs6_sb[hp][j][:, i:i + 1],
                    op0=ALU.mult, op1=ALU.subtract),
                    waits=[raw_toks[j]] + tk6(("scld", hp, j)) + tk6(("wnslot", wslot)))
                sc_last[j] = t_sc
                L6[("scuse", hp, j)] = t_sc
                t_tr = b.op("pe", lambda e, wslot=wslot, pslot=pslot: e.transpose(
                    ps_wtr[pslot][:],
                    wnat6_sb[:, wslot * 128:(wslot + 1) * 128], ident_sb[:]),
                    waits=[t_sc, t_ident] + tk6(("wtrps", pslot)))
                L6[("wnslot", wslot)] = t_tr
                pend_ev.append((j, il, pslot, t_tr))
            yield
        flush_evacs()
        for j in range(4):
            L6[("rawfree", p, j)] = sc_last[j]

    def emit_chunk_matmuls(hg, c, prod):
        p = c % 2
        cw = IC_CH[c]
        i0 = IC_OFF[c]
        seq = [(tc, il) for tc in range(NTC) for il in range(cw)]
        AHEAD = 6
        hl_toks = {}

        def issue_load(k):
            tc, il = seq[k]
            i = i0 + il
            hb = hgl_ctr[0] % NHGT
            hgl_ctr[0] += 1
            hl_toks[k] = (hb, b.dma("sp", lambda e, i=i, tc=tc, hb=hb: e.dma_start(
                hgt6_sb[hb][:],
                hg_s[i * 128:(i + 1) * 128, tc * 512:(tc + 1) * 512]),
                waits=tk6(("hgfree", hb))))

        for k in range(min(AHEAD, len(seq))):
            issue_load(k)
        mm_last = None
        mm_toks = {}
        for k, (tc, il) in enumerate(seq):
            if k + AHEAD < len(seq):
                issue_load(k + AHEAD)
            hb, t_hl = hl_toks.pop(k)
            for j in range(4):
                do_inc = (j == 3) or (il == cw - 1)
                t_mm = b.op("pe", lambda e, p=p, il=il, j=j, hb=hb, cw=cw: e.matmul(
                    ps_acc[j][:],
                    wdts_sb[p][:, il * 512 + j * 128: il * 512 + j * 128 + 128],
                    hgt6_sb[hb][:],
                    start=(il == 0), stop=(il == cw - 1)),
                    waits=[t_hl] + tk6(("slabrdy", p, il, j)) +
                          (tk6(("psfree", j)) if il == 0 else []),
                    inc=do_inc)
                if il == cw - 1:
                    mm_toks[j] = t_mm
                if j == 3:
                    L6[("hgfree", hb)] = t_mm
                    mm_last = t_mm
            if prod is not None:
                next(prod, None)
            if il == cw - 1:
                for j in range(4):
                    if c == 0:
                        t_fl = b.op("dve", lambda e, j=j, tc=tc: e.tensor_copy(
                            o2acc_sb[j][:, tc * 512:(tc + 1) * 512], ps_acc[j][:]),
                            waits=[mm_toks[j]] + tk6(("o2accfree", j, tc)))
                    else:
                        t_fl = b.op("dve", lambda e, j=j, tc=tc: e.tensor_add(
                            o2acc_sb[j][:, tc * 512:(tc + 1) * 512],
                            o2acc_sb[j][:, tc * 512:(tc + 1) * 512], ps_acc[j][:]),
                            waits=[mm_toks[j]])
                    L6[("psfree", j)] = t_fl
                    if c == 3:
                        L6[("accdone", j, tc)] = t_fl
        L6[("mmlast", p)] = mm_last

    def emit_hg_output(hg):
        otr_last = {}
        for tc in range(NTC):
            op_ = (hg * NTC + tc) % 2
            ev_toks = []
            for j in range(4):
                for ts_ in range(4):
                    po = (j + ts_) % 4
                    src_off = tc * 512 + ts_ * 128
                    t_tr = b.op("pe", lambda e, j=j, po=po, src_off=src_off: e.transpose(
                        ps_acc[po][:, 0:128],
                        o2acc_sb[j][:, src_off:src_off + 128], ident32_sb[:]),
                        waits=tk6(("accdone", j, tc)) + [t_id32] +
                              tk6(("otrps", po)) + tk6(("psfree", po)))
                    dcol = ts_ * 512 + j * 128
                    if (j * 4 + ts_) % 2 == 0:
                        t_ev = b.op("act", lambda e, op_=op_, dcol=dcol, po=po: e.copy(
                            o2a_sb[op_][:, dcol:dcol + 128],
                            ps_acc[po][:, 0:128]),
                            waits=[t_tr] + tk6(("o2afree", op_)))
                    else:
                        t_ev = b.op("dve", lambda e, op_=op_, dcol=dcol, po=po: e.tensor_copy(
                            o2a_sb[op_][:, dcol:dcol + 128],
                            ps_acc[po][:, 0:128]),
                            waits=[t_tr] + tk6(("o2afree", op_)))
                    L6[("otrps", po)] = t_ev
                    otr_last[po] = t_ev
                    ev_toks.append(t_ev)
                    if ts_ == 3:
                        L6[("o2accfree", j, tc)] = t_tr
            dst = bass.AP(o2_s.tensor, (tc * 512) * D + hg * 512,
                          [[D, 128], [128 * D, 4], [1, 512]])
            src = o2a_sb[op_][:].rearrange("p (s h) -> p s h", h=512)
            t_st = b.dma("act", lambda e, dst=dst, src=src: e.dma_start(dst, src),
                         waits=ev_toks)
            L6[("o2afree", op_)] = t_st
        for po in range(4):
            L6[("psfree", po)] = otr_last[po]

    chunk_seq = [(hg, c) for hg in range(NHG) for c in range(4)]
    emit_hg_scales(0)
    for _ in production_steps(0, 0, emit_raw6(0, 0)):
        pass
    pend_raw6 = {}
    if len(chunk_seq) > 1:
        pend_raw6[1] = emit_raw6(*chunk_seq[1])
    for ci, (hg, c) in enumerate(chunk_seq):
        if ci + 2 < len(chunk_seq):
            hg2, c2 = chunk_seq[ci + 2]
            if c2 == 0:
                emit_hg_scales(hg2)
            pend_raw6[ci + 2] = emit_raw6(hg2, c2)
        if ci + 1 < len(chunk_seq):
            hg1, c1 = chunk_seq[ci + 1]
            nxt = production_steps(hg1, c1, pend_raw6.pop(ci + 1))
        else:
            nxt = None
        emit_chunk_matmuls(hg, c, nxt)
        if nxt is not None:
            for _ in nxt:
                pass
        if c == 3:
            emit_hg_output(hg)

    free("ps_wtr3", "ps_wtr2", "ps_wtr1", "ps_wtr0",
         "ps_acc3", "ps_acc2", "ps_acc1", "ps_acc0")
    free("o2a_1", "o2a_0", "o2acc_3", "o2acc_2", "o2acc_1", "o2acc_0",
         *[f"hgt6_{i}" for i in range(NHGT - 1, -1, -1)],
         "wdts1", "wdts0", "wnat6_sb")
    free(*[f"dzs6_{p}_{j}" for p in (1, 0) for j in (3, 2, 1, 0)][::1])
    free(*[f"dzc6_{p}_{j}" for p in (1, 0) for j in (3, 2, 1, 0)])
    free(*[f"dsc6_{p}_{j}" for p in (1, 0) for j in (3, 2, 1, 0)])
    free(*[f"rawd{p}_{j}" for p in (1, 0) for j in (3, 2, 1, 0)])
    free("ident32_sb")
    b.bar()

    o2r_sb = [alloc(f"o2r{i}", [128, D], F32) for i in range(2)]
    hr_sb = [alloc(f"hr{i}", [128, D], F32) for i in range(2)]
    bn7_sb = [alloc(f"bn7_sb{i}", [128, 48], F32) for i in range(2)]
    mv7_sb = [alloc(f"mv7_sb{i}", [128, 2], F32) for i in range(2)]
    sd7_sb = [alloc(f"sd7_sb{i}", [128, 1], F32) for i in range(2)]
    rs7_sb = [alloc(f"rs7_sb{i}", [128, 1], F32) for i in range(2)]
    nb7_sb = [alloc(f"nb7_sb{i}", [128, 1], F32) for i in range(2)]

    L7 = {}

    def tk7(key):
        v = L7.get(key)
        return [] if v is None else (v if isinstance(v, list) else [v])

    for qc in range(NT128):
        rb = qc % 2
        t_lo = b.dma("sp", lambda e, qc=qc, rb=rb: e.dma_start(
            o2r_sb[rb][:], o2_s[qc * 128:(qc + 1) * 128, :]),
            waits=tk7(("o2r", rb)))
        t_lh = b.dma("pool", lambda e, qc=qc, rb=rb: e.dma_start(
            hr_sb[rb][:], h_s[qc * 128:(qc + 1) * 128, :]),
            waits=tk7(("hr", rb)) + tk7(("hr2", rb)))
        ad_toks = []
        t_bn = None
        for j in range(8):
            t_ad = b.op("dve", lambda e, rb=rb, j=j: e.tensor_add(
                o2r_sb[rb][:, j * 512:(j + 1) * 512],
                o2r_sb[rb][:, j * 512:(j + 1) * 512],
                hr_sb[rb][:, j * 512:(j + 1) * 512]),
                waits=[t_lo, t_lh])
            ad_toks.append(t_ad)
            t_bn = b.op("dve", lambda e, rb=rb, j=j: e.bn_stats(
                bn7_sb[rb][:, j * 6:(j + 1) * 6],
                o2r_sb[rb][:, j * 512:(j + 1) * 512]),
                waits=[t_ad] + (tk7(("bn7", rb)) if j == 0 else []))
        L7[("hr", rb)] = ad_toks
        t_ag = b.op("dve", lambda e, rb=rb: e.bn_aggr(mv7_sb[rb][:], bn7_sb[rb][:]),
                    waits=[t_bn])
        L7[("bn7", rb)] = t_ag
        t_sd = b.op("act", lambda e, rb=rb: e.activation(
            sd7_sb[rb][:], mv7_sb[rb][:, 1:2], AF.Sqrt, bias=eps_sb[:]),
            waits=[t_ag, t_eps] + tk7(("sd7", rb)))
        t_rs = b.op("dve", lambda e, rb=rb: e.reciprocal(rs7_sb[rb][:], sd7_sb[rb][:]),
                    waits=[t_sd] + tk7(("rs7", rb)))
        L7[("sd7", rb)] = t_rs
        t_nb = b.op("dve", lambda e, rb=rb: e.scalar_tensor_tensor(
            nb7_sb[rb][:], mv7_sb[rb][:, 0:1], -1.0, rs7_sb[rb][:],
            op0=ALU.mult, op1=ALU.mult),
            waits=[t_rs] + tk7(("nb7", rb)))
        L7[("rs7", rb)] = t_nb
        t_no = b.op("act", lambda e, rb=rb: e.activation(
            hr_sb[rb][:], o2r_sb[rb][:], AF.Identity,
            bias=nb7_sb[rb][:], scale=rs7_sb[rb][:]),
            waits=[t_nb] + tk7(("hr", rb)))
        L7[("nb7", rb)] = t_no
        t_st = b.dma("sp", lambda e, qc=qc, rb=rb: e.dma_start(
            out[qc * 128:(qc + 1) * 128, :], hr_sb[rb][:]), waits=[t_no])
        L7[("o2r", rb)] = t_no
        L7[("hr2", rb)] = t_st

    if DEBUG_OUTS:
        b.wait("pool", [(s, b.cnt[s]) for s in SEMS if b.cnt[s] > 0])
        dbg = {
            "dbg_qs": (qs_s, [D, S], BF16),
            "dbg_ks": (ks_s, [HKV * DH, S], BF16),
            "dbg_as": (as_s, [S, D], BF16),
            "dbg_o1": (o1_s, [S, D], F32),
            "dbg_h": (h_s, [S, D], F32),
            "dbg_hg": (hg_s, [I_FF, S], BF16),
            "dbg_hbf": (hbf_s, [S, D], BF16),
            "dbg_wg": (wgdq_s, [I_FF, D], BF16),
            "dbg_wu": (wudq_s, [I_FF, D], BF16),
            "dbg_o2": (o2_s, [S, D], F32),
        }
        for name, (srcap, shp, dt) in dbg.items():
            dout = nc.dram_tensor(name, shp, dt, kind="ExternalOutput").ap()
            b.dma("pool", lambda e, dout=dout, srcap=srcap: e.dma_start(
                dout[:], srcap[:]))

    b.wait("sp", [(s, b.cnt[s]) for s in SEMS if b.cnt[s] > 0])

    import contextlib
    with contextlib.ExitStack() as es:
        for s in SEMS:
            b.sem[s] = es.enter_context(nc.semaphore(s))
        block = es.enter_context(nc.Block())
        b.replay(block)
    for n, cm in reversed(ctx_stack):
        cm.__exit__(None, None, None)
    ctx_stack.clear()
    return nc


def make_inputs(inputs, S):
    x = np.ascontiguousarray(np.asarray(inputs["L0_attn_x"], dtype=np.float32))
    qperm = _q_perm(HQ)
    kperm = _q_perm(HKV)
    base = {
        "wq": np.ascontiguousarray(np.asarray(inputs["L0_attn_wq"], dtype=np.float32)[:, qperm]),
        "wk": np.ascontiguousarray(np.asarray(inputs["L0_attn_wk"], dtype=np.float32)[:, kperm]),
        "wv": np.ascontiguousarray(np.asarray(inputs["L0_attn_wv"], dtype=np.float32)),
        "wo": np.ascontiguousarray(np.asarray(inputs["L0_attn_wo"], dtype=np.float32)),
        "gq": np.ascontiguousarray(np.asarray(inputs["L0_ffn_gate_qweight"], dtype=np.int32)),
        "gs": np.ascontiguousarray(np.asarray(inputs["L0_ffn_gate_scales"], dtype=np.float32)),
        "gz": np.ascontiguousarray(np.asarray(inputs["L0_ffn_gate_zeros"], dtype=np.float32)),
        "uq": np.ascontiguousarray(np.asarray(inputs["L0_ffn_up_qweight"], dtype=np.int32)),
        "us": np.ascontiguousarray(np.asarray(inputs["L0_ffn_up_scales"], dtype=np.float32)),
        "uz": np.ascontiguousarray(np.asarray(inputs["L0_ffn_up_zeros"], dtype=np.float32)),
        "dq": np.ascontiguousarray(np.asarray(inputs["L0_ffn_down_qweight"], dtype=np.int32)),
        "ds": np.ascontiguousarray(np.asarray(inputs["L0_ffn_down_scales"], dtype=np.float32)),
        "dz": np.ascontiguousarray(np.asarray(inputs["L0_ffn_down_zeros"], dtype=np.float32)),
    }
    base.update(_host_consts(S))
    return x, base


def kernel(**inputs):
    x, base = make_inputs(inputs, 2048)
    B, S, _ = x.shape
    assert B == 8 and S == 2048
    nc = build_program(S)
    in_maps = []
    for c in range(8):
        m = dict(base)
        m["xin"] = np.ascontiguousarray(x[c])
        in_maps.append(m)
    res = run_bass_kernel_spmd(nc, in_maps, core_ids=list(range(8)))
    return np.stack([r["out"] for r in res.results], axis=0)
